# version 21
# baseline (speedup 1.0000x reference)
import sys
import numpy as np
import ml_dtypes
import concourse.bass as bass
import concourse.mybir as mybir
from concourse.bass_utils import run_bass_kernel_spmd

F32 = mybir.dt.float32
BF16 = mybir.dt.bfloat16
AF = mybir.ActivationFunctionType
ALU = mybir.AluOpType
AX = mybir.AxisListType

NCORES = 8
BATCH = 2
SEQ = 16384
D = 1024
DC = 8
H = 16
DHD = 64
WIN = 128
FFA = 2048
FFB = 4096
FFQ = 4096
T_CORE = SEQ // 4
TT = 512
LN_EPS = 1e-5
M_SHIFT = -40.0

ENGINES = ("tensor", "vector", "scalar", "gpsimd", "sync")
SKIP_SAME_ENGINE = {"tensor"}


class Buf:
    __slots__ = ("name", "last_w", "readers", "is_psum")

    def __init__(self, name, is_psum=False):
        self.name = name
        self.last_w = None
        self.readers = []
        self.is_psum = is_psum


class Prog:

    def __init__(self, nc: bass.Bass):
        self.nc = nc
        self.sems = {}
        self.cnt = {}
        self.q = {e: [] for e in ENGINES}
        self.meta = {e: [] for e in ENGINES}
        self.seen = {e: {} for e in ENGINES}
        for e in ENGINES:
            self.sems[e] = nc.alloc_semaphore(f"sem_{e}")
            self.cnt[e] = 0
        self.nbuf = 0
        self.nslot = 0

    def buf(self, name=None):
        self.nbuf += 1
        return Buf(name or f"b{self.nbuf}")

    def slot(self, name=None):
        self.nslot += 1
        key = f"ds{self.nslot}_{name or ''}"
        self.sems[key] = self.nc.alloc_semaphore(key)
        self.cnt[key] = 0
        return key

    def emit(self, engine, fn, reads=(), writes=(), sem_key=None, dma=False,
             inc=None, extra_deps=()):
        deps = {}

        def add(tok):
            if tok is None:
                return
            s, v = tok
            if deps.get(s, -1) < v:
                deps[s] = v

        for b in reads:
            add(b.last_w)
            if b.is_psum:
                for t in b.readers:
                    add(t)
        for b in writes:
            add(b.last_w)
            for t in b.readers:
                add(t)
        for t in extra_deps:
            add(t)

        seen = self.seen[engine]
        waits = []
        for s, v in sorted(deps.items()):
            if s == engine and engine in SKIP_SAME_ENGINE:
                continue
            if seen.get(s, -1) >= v:
                continue
            waits.append((s, v))
            seen[s] = v

        if sem_key is None:
            assert not dma
            sem_key = engine
        if inc is None:
            inc = 16 if dma else 1
        if fn is None:
            inc = 0
        self.cnt[sem_key] += inc
        tok = (sem_key, self.cnt[sem_key]) if inc else None
        fr = sys._getframe(1)
        if fr.f_code.co_name in ("dma", "collective"):
            fr = fr.f_back
        self.meta[engine].append((fr.f_lineno, fr.f_code.co_name, len(waits)))
        self.q[engine].append((waits, fn, inc, sem_key))

        if tok is not None:
            for b in writes:
                b.last_w = tok
                b.readers = []
            for b in reads:
                b.readers.append(tok)
        return tok

    def dma(self, queue, slot, out_ap, in_ap, reads=(), writes=(), **kw):
        return self.emit(queue, lambda e: e.dma_start(out=out_ap, in_=in_ap, **kw),
                         reads=reads, writes=writes, sem_key=slot, dma=True)

    def collective(self, kind, op, groups, ins, outs, reads=(), writes=()):
        slot = self.slot("cc")
        return self.emit(
            "gpsimd",
            lambda e: e.collective_compute(kind, op, replica_groups=groups,
                                           ins=ins, outs=outs),
            reads=reads, writes=writes, sem_key=slot, inc=1)

    def barrier(self):
        toks = [(e, self.cnt[e]) for e in ENGINES if self.cnt[e] > 0]
        for e in ENGINES:
            self.emit(e, None, extra_deps=toks)

    def realize(self):
        finals = [(k, v) for k, v in self.cnt.items()
                  if k not in ENGINES and v > 0]
        for k, v in finals:
            if self.seen["sync"].get(k, -1) < v:
                self.q["sync"].append(([(k, v)], None, 0, "sync"))
                self.seen["sync"][k] = v
        nc = self.nc
        sems = self.sems
        with nc.Block() as block:
            def body(engine_name):
                def _(eng):
                    for waits, fn, inc, sem_key in self.q[engine_name]:
                        for s, v in waits:
                            eng.wait_ge(sems[s], v)
                        if fn is not None:
                            inst = fn(eng)
                            inst.then_inc(sems[sem_key], inc)
                return _
            block.tensor(body("tensor"))
            block.vector(body("vector"))
            block.scalar(body("scalar"))
            block.gpsimd(body("gpsimd"))
            block.sync(body("sync"))


def fap(handle, free_off, dims):
    ap = handle.ap()
    part = list(ap.ap[0])
    return bass.AP(ap.tensor, free_off, [part] + [list(d) for d in dims])



def _fm(w):
    d, f = w.shape
    return np.ascontiguousarray(w.reshape(d // 128, 128, f), dtype=np.float32)


def _col(v):
    n = v.shape[0]
    return np.ascontiguousarray(v.reshape(n // 128, 128).T, dtype=np.float32)


def host_inputs(inp, T=T_CORE):
    nslice = SEQ // T
    x = np.asarray(inp["L0_attn_a_x"], dtype=np.float32)

    wq_a = _fm(np.ascontiguousarray(np.transpose(inp["L0_attn_a_q_proj"], (1, 0, 2))).reshape(D, D))
    wk_a = _fm(np.ascontiguousarray(np.transpose(inp["L0_attn_a_k_proj"], (1, 0, 2))).reshape(D, D))
    wv_a = _fm(np.ascontiguousarray(np.transpose(inp["L0_attn_a_v_proj"], (1, 0, 2))).reshape(D, D))
    qb_a = _col(np.asarray(inp["L0_attn_a_q_bias"]).reshape(D))
    kb_a = _col(np.asarray(inp["L0_attn_a_k_bias"]).reshape(D))
    vb_a = _col(np.asarray(inp["L0_attn_a_v_bias"]).reshape(D))
    ln1g = _col(np.asarray(inp["L0_attn_a_ln1_g"]))
    ln1b = _col(np.asarray(inp["L0_attn_a_ln1_b"]))
    ln2g = _col(np.asarray(inp["L0_attn_a_ln2_g"]))
    ln2b = _col(np.asarray(inp["L0_attn_a_ln2_b"]))
    ff1w = _fm(np.asarray(inp["L0_attn_a_ff1_w"]))
    ff1b = _col(np.asarray(inp["L0_attn_a_ff1_b"]))
    ff2w = _fm(np.asarray(inp["L0_attn_a_ff2_w"]))
    ff2b = _col(np.asarray(inp["L0_attn_a_ff2_b"]))

    w1f = _fm(np.asarray(inp["L0_ffn_w1"]))
    b1f = _col(np.asarray(inp["L0_ffn_b1"]))
    w2f = _fm(np.asarray(inp["L0_ffn_w2"]))
    b2f = _col(np.asarray(inp["L0_ffn_b2"]))

    wq_b = _fm(np.asarray(inp["L0_attn_b_wq"]))
    wk_b = _fm(np.asarray(inp["L0_attn_b_wk"]))
    wv_b = _fm(np.asarray(inp["L0_attn_b_wv"]))
    wo_b = _fm(np.asarray(inp["L0_attn_b_wo"]))
    proj = np.asarray(inp["L0_attn_b_proj"], dtype=np.float32)
    projblk = np.zeros((128, 128), dtype=np.float32)
    projblk[0:64, 0:64] = proj
    projblk[64:128, 64:128] = proj
    g1 = _col(np.asarray(inp["L0_attn_b_g1"]))
    be1 = _col(np.asarray(inp["L0_attn_b_beta1"]))
    g2 = _col(np.asarray(inp["L0_attn_b_g2"]))
    be2 = _col(np.asarray(inp["L0_attn_b_beta2"]))
    w1b = _fm(np.asarray(inp["L0_attn_b_w1"]))
    b1b = _col(np.asarray(inp["L0_attn_b_b1"]))
    w2b = _fm(np.asarray(inp["L0_attn_b_w2"]))
    b2b = _col(np.asarray(inp["L0_attn_b_b2"]))

    i_idx = np.arange(WIN)[:, None]
    j_idx = np.arange(WIN)[None, :]
    mT_own = np.where(j_idx.T >= i_idx.T, 0.0, 0.0).astype(np.float32)
    mT_own = np.where(j_idx.reshape(-1, 1) <= i_idx.reshape(1, -1), 0.0, -1e9).astype(np.float32)
    mT_allow = np.zeros((WIN, WIN), dtype=np.float32)
    mT_deny = np.full((WIN, WIN), -1e9, dtype=np.float32)
    mb = (j_idx.reshape(-1, 1) <= i_idx.reshape(1, -1)).astype(np.float32)
    mTb4 = np.tile(mb, (1, 4)).reshape(128, 4, 128).transpose(0, 1, 2)
    mTb4 = np.ascontiguousarray(np.concatenate([mb] * 4, axis=1))

    ident = np.eye(128, dtype=np.float32)

    mT_own4 = np.ascontiguousarray(np.concatenate([mT_own] * 4, axis=1))
    shared = dict(
        wq_a=wq_a, wk_a=wk_a, wv_a=wv_a, qb_a=qb_a, kb_a=kb_a, vb_a=vb_a,
        ln1g=ln1g, ln1b=ln1b, ln2g=ln2g, ln2b=ln2b,
        ff1w=ff1w, ff1b=ff1b, ff2w=ff2w, ff2b=ff2b,
        w1f=w1f, b1f=b1f, w2f=w2f, b2f=b2f,
        wq_b=wq_b, wk_b=wk_b, wv_b=wv_b, wo_b=wo_b, projblk=projblk,
        g1=g1, be1=be1, g2=g2, be2=be2,
        w1b=w1b, b1b=b1b, w2b=w2b, b2b=b2b,
        mT_own4=mT_own4, mTb4=mTb4, ident=ident,
    )

    in_maps = []
    for c in range(NCORES):
        b = c // nslice
        s = c % nslice
        t0 = s * T
        xt = np.zeros((T + 128, D), dtype=np.float32)
        xt[128:] = x[b, t0:t0 + T]
        if s > 0:
            xt[:128] = x[b, t0 - 128:t0]
        nt = T // TT
        arr = xt[128:].T.reshape(DC, 128, nt, TT)
        x0t = np.ascontiguousarray(arr.transpose(2, 1, 0, 3).reshape(nt, 128, DC * TT))
        x0h = np.ascontiguousarray(xt[:128].T.reshape(DC, 128, 128).transpose(1, 0, 2).reshape(128, DC * 128))
        wpref = np.zeros((128, NCORES), dtype=np.float32)
        for r in range(NCORES):
            if r // nslice == b and r % nslice < s:
                wpref[:, r] = 1.0
        m = dict(shared)
        m["x0t"] = x0t
        m["x0h"] = x0h
        mp = mT_deny if s == 0 else mT_allow
        m["mT_prev04"] = np.ascontiguousarray(np.concatenate([mp] * 4, axis=1))
        m["mTbp4"] = np.ascontiguousarray((m["mT_prev04"] == 0.0).astype(np.float32))
        m["wpref"] = wpref
        in_maps.append(m)
    return in_maps


def host_output(results, T=T_CORE):
    nslice = SEQ // T
    out = np.empty((BATCH, SEQ, D), dtype=np.float32)
    for c in range(NCORES):
        b = c // nslice
        s = c % nslice
        fm = results[c]["xout"]
        nt = T // TT
        dmt = fm.reshape(nt, 128, DC, TT).transpose(2, 1, 0, 3).reshape(D, T)
        out[b, s * T:(s + 1) * T, :] = dmt.T
    return out



class KB:
    def __init__(self, T=T_CORE, n_sub=4):
        self.T = T
        self.NT = T // TT
        self.n_sub = n_sub
        nc = bass.Bass("TRN2", target_bir_lowering=False)
        self.nc = nc
        self.p = Prog(nc)
        self.ctxs = []
        self.dbg = False
        self._declare_dram()
        self.sdbg1, self.sdbg2 = self.p.slot("dbg1"), self.p.slot("dbg2")
        self.bdbg1, self.bdbg2 = self.p.buf(), self.p.buf()

    def _declare_dram(self):
        nc, T = self.nc, self.T
        NT = T // TT

        def din(name, shape):
            return nc.dram_tensor(name, list(shape), F32, kind="ExternalInput").ap()

        i = {}
        i["x0t"] = din("x0t", (NT, 128, DC * TT))
        i["x0h"] = din("x0h", (128, DC * 128))
        for n, sh in [("wq_a", (DC, 128, D)), ("wk_a", (DC, 128, D)), ("wv_a", (DC, 128, D)),
                      ("qb_a", (128, DC)), ("kb_a", (128, DC)), ("vb_a", (128, DC)),
                      ("ln1g", (128, DC)), ("ln1b", (128, DC)), ("ln2g", (128, DC)), ("ln2b", (128, DC)),
                      ("ff1w", (DC, 128, FFA)), ("ff1b", (128, FFA // 128)),
                      ("ff2w", (FFA // 128, 128, D)), ("ff2b", (128, DC)),
                      ("w1f", (FFQ // 128, 128, D)), ("b1f", (128, FFQ // 128)),
                      ("w2f", (DC, 128, FFQ)), ("b2f", (128, DC)),
                      ("wq_b", (DC, 128, D)), ("wk_b", (DC, 128, D)), ("wv_b", (DC, 128, D)),
                      ("wo_b", (DC, 128, D)), ("projblk", (128, 128)),
                      ("g1", (128, DC)), ("be1", (128, DC)), ("g2", (128, DC)), ("be2", (128, DC)),
                      ("w1b", (DC, 128, FFB)), ("b1b", (128, FFB // 128)),
                      ("w2b", (FFB // 128, 128, D)), ("b2b", (128, DC)),
                      ("mT_own4", (128, 512)), ("mT_prev04", (128, 512)), ("mTb4", (128, 512)),
                      ("mTbp4", (128, 512)),
                      ("ident", (128, 128)), ("wpref", (128, NCORES))]:
            i[n] = din(n, sh)
        self.i = i

        def scr(name, shape, dt=F32):
            return nc.dram_tensor(name, list(shape), dt, kind="Internal").ap()

        s = {}
        s["x1"] = scr("x1s", (NT, 128, DC * TT))
        s["x2"] = scr("x2s", (NT, 128, DC * TT))
        s["x3"] = scr("x3s", (NT, 128, DC * TT))
        s["u"] = scr("us", (NT, 128, DC * TT))
        s["xo"] = scr("xos", (NT, 128, DC * TT))
        s["y1"] = scr("y1s", (NT, 128, DC * TT), BF16)
        s["y2"] = scr("y2s", (NT, 128, DC * TT), BF16)
        s["y3"] = scr("y3s", (NT, 128, DC * TT), BF16)
        s["h"] = scr("hs", (NT, 128, (FFQ // 128) * TT), BF16)
        s["qfm"] = scr("qfms", (NT, 128, DC * TT), BF16)
        s["kfm"] = scr("kfms", (NT, 128, DC * TT), BF16)
        s["ktm"] = scr("ktms", (NT, 128, (TT // 128) * D), BF16)
        s["vtm"] = scr("vtms", (NT, 128, (TT // 128) * D), BF16)
        s["m1T"] = scr("m1Ts", (128, DC, FFQ), BF16)
        s["m2T"] = scr("m2Ts", (128, FFQ // 128, D), BF16)
        for k in range(4):
            s[f"cin{k}"] = scr(f"cin{k}", (1, 1))
            s[f"cout{k}"] = scr(f"cout{k}", (1, 1))
        s["dy0"] = scr("dy0s", (DC, 128, T), BF16)
        s["dwin"] = scr("dwins", (4, 128, 1024), BF16)
        s["dexp"] = scr("dexps", (4, 8, 128, 512), BF16)
        s["dattn"] = scr("dattns", (DC, 128, T), BF16)
        s["gin"] = scr("gin", (64, 16 * 65))
        s["gout"] = scr("gout", (NCORES * 64, 16 * 65))
        self.s = s

        self.o = {"xout": nc.dram_tensor("xout", [NT, 128, DC * TT], F32, kind="ExternalOutput").ap()}

    _uid = [0]

    def sb(self, ctx, name, shape, dt=F32):
        self._uid[0] += 1
        h = ctx.enter_context(self.nc.sbuf_tensor(f"sb{self._uid[0]}_{name}", list(shape), dt))
        return h, self.p.buf(name)

    def ps(self, ctx, name, shape, dt=F32):
        self._uid[0] += 1
        h = ctx.enter_context(self.nc.psum_tensor(f"ps{self._uid[0]}_{name}", list(shape), dt))
        b = self.p.buf(name)
        b.is_psum = True
        return h, b

    def barrier(self):
        p = self.p
        toks = [(k, v) for k, v in p.cnt.items() if v > 0]
        for e in ENGINES:
            p.emit(e, None, extra_deps=toks)

    def setup(self, ctx):
        p, nc, i = self.p, self.nc, self.i
        S = p.slot
        cst, bcst = self.sb(ctx, "cst", (128, 8))
        self.cst, self.bcst = cst, bcst
        for col, val in [(0, M_SHIFT), (1, LN_EPS), (2, -0.5), (3, 0.125),
                         (4, 1.0 / D), (5, 1e-6), (6, 255.0), (7, 1e-8)]:
            p.emit("vector", (lambda c_, v_: lambda e: e.memset(cst[:, c_:c_ + 1], v_))(col, val),
                   writes=[bcst])
        self.c_m40 = cst[:, 0:1]
        self.c_eps = cst[:, 1:2]
        self.c_nh = cst[:, 2:3]
        self.c_8th = cst[:, 3:4]
        self.c_invD = cst[:, 4:5]
        self.c_1e6 = cst[:, 5:6]

        self.id_bf, self.bid_bf = self.sb(ctx, "id_bf", (128, 128), BF16)
        self.id_f, self.bid_f = self.sb(ctx, "id_f", (128, 128), F32)
        p.dma("gpsimd", S("idbf"), self.id_bf[:], i["ident"], writes=[self.bid_bf])
        p.dma("sync", S("idf"), self.id_f[:], i["ident"], writes=[self.bid_f])
        self.ones_bf, self.bones = self.sb(ctx, "ones_bf", (128, 128), BF16)
        p.emit("vector", lambda e: e.memset(self.ones_bf[:], 1.0), writes=[self.bones])
        self.ones_f, self.bones_f = self.sb(ctx, "ones_f", (1, 128))
        p.emit("vector", lambda e: e.memset(self.ones_f[:], 1.0), writes=[self.bones_f])

        self.v = {}
        self.bv = {}
        slot_v = S("vecs")
        vec_bufs = []
        for n in ["qb_a", "kb_a", "vb_a", "ln1g", "ln1b", "ln2g", "ln2b",
                  "ff1b", "ff2b", "b1f", "b2f", "g1", "be1", "g2", "be2",
                  "b1b", "b2b", "wpref"]:
            sh = tuple(i[n].shape)
            t, b = self.sb(ctx, "v_" + n, sh)
            self.v[n], self.bv[n] = t, b
            p.dma("sync", slot_v, t[:], i[n], writes=[b])
            vec_bufs.append(b)
        self.mT_own4, self.bmT_own4 = self.sb(ctx, "mT_own4", (128, 512))
        self.mT_prev04, self.bmT_prev04 = self.sb(ctx, "mT_prev04", (128, 512))
        self.mTb4, self.bmTb4 = self.sb(ctx, "mTb4", (128, 512), BF16)
        self.projblk, self.bprojblk = self.sb(ctx, "projblk", (128, 128), BF16)
        p.dma("sync", slot_v, self.mT_own4[:], i["mT_own4"], writes=[self.bmT_own4])
        p.dma("sync", slot_v, self.mT_prev04[:], i["mT_prev04"], writes=[self.bmT_prev04])
        vec_bufs += [self.bmT_own4, self.bmT_prev04]
        final_tok = ("ds" + slot_v.split("ds")[1], p.cnt[slot_v]) if False else (slot_v, p.cnt[slot_v])
        for b in vec_bufs:
            b.last_w = final_tok
        p.dma("gpsimd", S("mtb"), self.mTb4[:], i["mTb4"], writes=[self.bmTb4])
        self.mTbp4, self.bmTbp4 = self.sb(ctx, "mTbp4", (128, 512), BF16)
        p.dma("gpsimd", S("mtbp"), self.mTbp4[:], i["mTbp4"], writes=[self.bmTbp4])
        p.dma("gpsimd", S("pblk"), self.projblk[:], i["projblk"], writes=[self.bprojblk])

        self.qb_s, self.bqb_s = self.sb(ctx, "qb_s", (128, DC))
        p.emit("vector", lambda e: e.tensor_scalar_mul(out=self.qb_s[:], in0=self.v["qb_a"][:], scalar1=0.125),
               reads=[self.bv["qb_a"]], writes=[self.bqb_s])

        self.sw1, self.bsw1 = self.sb(ctx, "sw1", (128, FFQ // 128))
        self.sw2, self.bsw2 = self.sb(ctx, "sw2", (128, DC))
        self.qcols, self.bqcols = self.sb(ctx, "qcols", (128, 8))
        self.zs1, self.bzs1 = self.sb(ctx, "zs1", (128, FFQ // 128))
        self.zs2, self.bzs2 = self.sb(ctx, "zs2", (128, DC))
        self.b1q, self.bb1q = self.sb(ctx, "b1q", (128, FFQ // 128))
        self.b2q, self.bb2q = self.sb(ctx, "b2q", (128, DC))
        self.mloc, self.bmloc = self.sb(ctx, "mloc", (128, 4))
        p.emit("vector", lambda e: e.memset(self.mloc[:], -1e30), writes=[self.bmloc])

    def emit_ln(self, ctx_tiles, xap_chunks, bx, n, gamma=None, beta=None,
                out_f32=None, out_bf=None, scratch=None):
        p = self.p
        (xbf, bxbf, xsq, bxsq, mu, bmu, var, bvar, rstd, brstd, psS, bpsS) = scratch
        for c in range(DC):
            p.emit("vector", (lambda c_: lambda e: e.tensor_copy(
                out=xbf[:, c_ * n:(c_ + 1) * n], in_=xap_chunks(c_)))(c),
                reads=[bx], writes=[bxbf])
        p.emit("vector", lambda e: e.tensor_mul(out=xsq[:, 0:DC * n], in0=xbf[:, 0:DC * n], in1=xbf[:, 0:DC * n]),
               reads=[bxbf], writes=[bxsq])
        for c in range(DC):
            p.emit("tensor", (lambda c_: lambda e: e.matmul(psS[:, 0:n], self.ones_bf[:], xbf[:, c_ * n:(c_ + 1) * n],
                                                            start=(c_ == 0), stop=(c_ == DC - 1)))(c),
                   reads=[self.bones, bxbf], writes=[bpsS])
        for c in range(DC):
            p.emit("tensor", (lambda c_: lambda e: e.matmul(psS[:, n:2 * n], self.ones_bf[:], xsq[:, c_ * n:(c_ + 1) * n],
                                                            start=(c_ == 0), stop=(c_ == DC - 1)))(c),
                   reads=[self.bones, bxsq], writes=[bpsS])
        p.emit("scalar", lambda e: e.activation(out=mu[:, 0:n], in_=psS[:, 0:n], func=AF.Identity, scale=self.c_invD),
               reads=[bpsS, self.bcst], writes=[bmu])
        p.emit("gpsimd", lambda e: e.tensor_mul(out=var[:, 0:n], in0=mu[:, 0:n], in1=mu[:, 0:n]),
               reads=[bmu], writes=[bvar])
        p.emit("vector", lambda e: e.scalar_tensor_tensor(out=var[:, 0:n], in0=psS[:, n:2 * n], scalar=1.0 / D,
                                                          in1=var[:, 0:n], op0=ALU.mult, op1=ALU.subtract),
               reads=[bpsS, bvar], writes=[bvar])
        p.emit("scalar", lambda e: e.activation(out=rstd[:, 0:n], in_=var[:, 0:n], func=AF.Ln, bias=self.c_eps),
               reads=[bvar, self.bcst], writes=[brstd])
        p.emit("scalar", lambda e: e.activation(out=rstd[:, 0:n], in_=rstd[:, 0:n], func=AF.Exp, scale=self.c_nh),
               reads=[brstd, self.bcst], writes=[brstd])
        p.emit("gpsimd", lambda e: e.tensor_mul(out=mu[:, 0:n], in0=mu[:, 0:n], in1=rstd[:, 0:n]),
               reads=[brstd, bmu], writes=[bmu])
        of = out_f32 if out_f32 is not None else out_bf
        ofh, bof = of
        eng_of = lambda c: "vector"
        for c in range(DC):
            p.emit(eng_of(c), (lambda c_: lambda e: e.tensor_mul(
                out=ofh[:, c_ * n:(c_ + 1) * n], in0=xap_chunks(c_), in1=rstd[:, 0:n]))(c),
                reads=[bx, brstd], writes=[bof])
        for c in range(DC):
            p.emit(eng_of(c), (lambda c_: lambda e: e.tensor_sub(
                out=ofh[:, c_ * n:(c_ + 1) * n], in0=ofh[:, c_ * n:(c_ + 1) * n], in1=mu[:, 0:n]))(c),
                reads=[bmu, bof], writes=[bof])
        if gamma is not None:
            for c in range(DC):
                p.emit(eng_of(c), (lambda c_: lambda e: e.tensor_scalar(
                    out=ofh[:, c_ * n:(c_ + 1) * n], in0=ofh[:, c_ * n:(c_ + 1) * n],
                    scalar1=gamma[:, c_:c_ + 1], scalar2=beta[:, c_:c_ + 1],
                    op0=ALU.mult, op1=ALU.add))(c),
                    reads=[bof], writes=[bof])
        if out_f32 is not None and out_bf is not None:
            obf, bobf = out_bf
            p.emit("vector", lambda e: e.tensor_copy(out=obf[:, 0:DC * n], in_=ofh[:, 0:DC * n]),
                   reads=[bof], writes=[bobf])

    def pass_attn_a_front(self):
        import contextlib
        p, nc, i, s = self.p, self.nc, self.i, self.s
        T, NT = self.T, self.NT
        self.barrier()
        with contextlib.ExitStack() as ctx:
            S = p.slot
            wq, bwq = self.sb(ctx, "a_wq", (128, DC * D), BF16)
            wk, bwk = self.sb(ctx, "a_wk", (128, DC * D), BF16)
            wv, bwv = self.sb(ctx, "a_wv", (128, DC * D), BF16)
            p.dma("gpsimd", S("wq"), wq[:], i["wq_a"].rearrange("c p f -> p c f"), writes=[bwq])
            p.dma("gpsimd", S("wk"), wk[:], i["wk_a"].rearrange("c p f -> p c f"), writes=[bwk])
            p.dma("gpsimd", S("wv"), wv[:], i["wv_a"].rearrange("c p f -> p c f"), writes=[bwv])

            xb = [self.sb(ctx, f"a_x{j}", (128, DC * TT)) for j in range(2)]
            xslots = [S(f"a_x{j}") for j in range(2)]
            xbfb = [self.sb(ctx, f"a_xbf{j}", (128, DC * TT), BF16) for j in range(2)]
            xsq, bxsq = self.sb(ctx, "a_xsq", (128, DC * TT), BF16)
            stats = [tuple(self.sb(ctx, f"a_st{j}{nm}", (128, TT)) for nm in ("mu", "var", "rstd"))
                     for j in range(2)]
            y0bfb = [self.sb(ctx, f"a_y0bf{j}", (128, DC * TT), BF16) for j in range(2)]
            qbfb = [self.sb(ctx, f"a_qbf{j}", (128, DC * TT), BF16) for j in range(2)]
            kbuf, bkbuf = self.sb(ctx, "a_kbuf", (128, DC * 640), BF16)
            vsb, bvsb = self.sb(ctx, "a_vsb", (128, 5 * H * 65), BF16)
            expT, bexpT = self.sb(ctx, "a_expT", (128, 4096), BF16)
            attn_tmb = [self.sb(ctx, f"a_attntm{j}", (128, 1024), BF16) for j in range(2)]
            attn_fmb = [self.sb(ctx, f"a_attnfm{j}", (128, DC * TT), BF16) for j in range(2)]
            den16b = [self.sb(ctx, f"a_den{j}", (128, H)) for j in range(2)]
            den16ib = [self.sb(ctx, f"a_deni{j}", (128, H)) for j in range(2)]

            psSd = [self.ps(ctx, f"a_psS{j}", (128, 2 * TT)) for j in range(2)]
            psS, bpsS = psSd[0]
            psP, bpsP = psSd[1]
            ps_a, bps_a = psP.ap()[:, 0:TT], bpsP
            ps_b, bps_b = psP.ap()[:, TT:2 * TT], bpsP
            ps_av = [self.ps(ctx, f"a_psav{g}", (128, 4 * 65)) for g in range(4)]

            su = S("a_u")
            budram = p.buf("a_udram")
            GROUPS = [(0, 2, 4, 6), (8, 10, 12, 14), (1, 3, 5, 7), (9, 11, 13, 15)]

            p.emit("vector", lambda e: e.memset(
                fap(vsb, 64, [[1040, 5], [65, H], [1, 1]]), 1.0),
                writes=[bvsb])

            def proj_k(blk0, ntok, n_ln, y0bf, by0bf):
                for oc in range(DC):
                    pst, bpst = (ps_a, bps_a) if oc % 2 == 0 else (ps_b, bps_b)
                    for c in range(DC):
                        p.emit("tensor", (lambda oc_, c_, pst_, yb2: lambda e: e.matmul(
                            pst_[:, 0:ntok], wk[:, c_ * D + 128 * oc_: c_ * D + 128 * oc_ + 128],
                            yb2[:, c_ * n_ln: c_ * n_ln + ntok],
                            start=(c_ == 0), stop=(c_ == DC - 1)))(oc, c, pst, y0bf),
                            reads=[bwk, by0bf], writes=[bpst])
                    p.emit("scalar", (lambda oc_, pst_: lambda e: e.activation(
                        out=kbuf[:, oc_ * 640 + 128 * blk0: oc_ * 640 + 128 * blk0 + ntok],
                        in_=pst_[:, 0:ntok], func=AF.Identity, bias=self.v["kb_a"][:, oc_:oc_ + 1]))(oc, pst),
                        reads=[bpst, self.bv["kb_a"]], writes=[bkbuf])

            def proj_v(nblk, blk0, n_ln, y0bf, by0bf):
                for b_ in range(nblk):
                    for half in range(2):
                        pst, bpst = (ps_a, bps_a) if half == 0 else (ps_b, bps_b)
                        for c in range(DC):
                            p.emit("tensor", (lambda b2, h2, c_, pst_, yb2: lambda e: e.matmul(
                                pst_[:, 0:512],
                                yb2[:, c_ * n_ln + 128 * b2: c_ * n_ln + 128 * b2 + 128],
                                wv[:, c_ * D + 512 * h2: c_ * D + 512 * h2 + 512],
                                start=(c_ == 0), stop=(c_ == DC - 1)))(b_, half, c, pst, y0bf),
                                reads=[bwv, by0bf], writes=[bpst])
                        p.emit("scalar", (lambda b2, h2, pst_: lambda e: e.copy(
                            out=fap(vsb, (blk0 + b2) * 1040 + h2 * 520,
                                    [[65, 8], [1, 64]]),
                            in_=pst_[:, 0:512]))(b_, half, pst),
                            reads=[bpst], writes=[bvsb])

            x_sb, bx = xb[0]
            xbf, bxbf = xbfb[0]
            (mu, bmu), (var, bvar), (rstd, brstd) = stats[0]
            y0bf, by0bf = y0bfb[0]
            scr = (xbf, bxbf, xsq, bxsq, mu, bmu, var, bvar, rstd, brstd, psS, bpsS)
            p.dma("sync", xslots[0], x_sb[:, 0:DC * 128], i["x0h"], writes=[bx])
            self.emit_ln(ctx, (lambda xs: lambda c: xs[:, c * 128:(c + 1) * 128])(x_sb), bx, 128,
                         out_f32=None, out_bf=(y0bf, by0bf), scratch=scr)
            proj_k(0, 128, 128, y0bf, by0bf)
            proj_v(1, 0, 128, y0bf, by0bf)

            prev_epi = None
            for ti in range(NT):
                t0 = ti * TT
                par = ti % 2
                x_sb, bx = xb[par]
                xbf, bxbf = xbfb[par]
                (mu, bmu), (var, bvar), (rstd, brstd) = stats[par]
                y0bf, by0bf = y0bfb[par]
                qbf, bqbf = qbfb[par]
                attn_fm, bfm = attn_fmb[par]
                scr = (xbf, bxbf, xsq, bxsq, mu, bmu, var, bvar, rstd, brstd, psS, bpsS)

                if ti > 0:
                    p.emit("vector", lambda e: e.tensor_copy(
                        out=fap(kbuf, 0, [[640, DC], [1, 128]]),
                        in_=fap(kbuf, 512, [[640, DC], [1, 128]])),
                        reads=[bkbuf], writes=[bkbuf])
                    p.emit("vector", lambda e: e.tensor_copy(
                        out=vsb[:, 0:1040], in_=vsb[:, 4 * 1040:5 * 1040]),
                        reads=[bvsb], writes=[bvsb])
                p.dma("sync", xslots[par], x_sb[:], i["x0t"][ti], writes=[bx])
                self.emit_ln(ctx, (lambda xs: lambda c: xs[:, c * TT:(c + 1) * TT])(x_sb), bx, TT,
                             out_f32=None, out_bf=(y0bf, by0bf), scratch=scr)
                for oc in range(DC):
                    pst, bpst = (ps_a, bps_a) if oc % 2 == 0 else (ps_b, bps_b)
                    for c in range(DC):
                        p.emit("tensor", (lambda oc_, c_, pst_, yb2: lambda e: e.matmul(
                            pst_[:, 0:TT], wq[:, c_ * D + 128 * oc_: c_ * D + 128 * oc_ + 128],
                            yb2[:, c_ * TT: c_ * TT + TT],
                            start=(c_ == 0), stop=(c_ == DC - 1)))(oc, c, pst, y0bf),
                            reads=[bwq, by0bf], writes=[bpst])
                    p.emit("scalar", (lambda oc_, pst_, qb2: lambda e: e.activation(
                        out=qb2[:, oc_ * TT: (oc_ + 1) * TT], in_=pst_[:, 0:TT], func=AF.Identity,
                        scale=self.c_8th, bias=self.qb_s[:, oc_:oc_ + 1]))(oc, pst, qbf),
                        reads=[bpst, self.bqb_s, self.bcst], writes=[bqbf])
                proj_k(1, TT, TT, y0bf, by0bf)
                proj_v(4, 1, TT, y0bf, by0bf)

                def sc_group(w, gi):
                    grp = GROUPS[gi]
                    psG, bpsG = psSd[gi % 2]
                    expG = expT[:, 1024 * gi:1024 * gi + 1024]
                    pr0 = 64 * (grp[0] % 2)
                    for idx, h in enumerate(grp):
                        for half in range(2):
                            p.emit("tensor", (lambda h_, pr0_, idx_, half_, w_, psG_, qb2: lambda e: e.matmul(
                                psG_[:, 256 * idx_ + 128 * half_: 256 * idx_ + 128 * half_ + 128],
                                kbuf[pr0_:pr0_ + 64, (h_ // 2) * 640 + 128 * (w_ + half_): (h_ // 2) * 640 + 128 * (w_ + half_) + 128],
                                qb2[pr0_:pr0_ + 64, (h_ // 2) * TT + 128 * w_: (h_ // 2) * TT + 128 * w_ + 128],
                                start=True, stop=True))(h, pr0, idx, half, w, psG, qbf),
                                reads=[bkbuf, bqbf], writes=[bpsG])
                    p.emit("scalar", (lambda psG_, eG: lambda e: e.activation(
                        out=eG, in_=psG_[:, 0:1024],
                        func=AF.Exp, bias=self.c_m40))(psG, expG),
                        reads=[bpsG, self.bcst], writes=[bexpT])
                    p.emit("gpsimd", (lambda gi_: lambda e: e.tensor_mul(
                        out=fap(expT, 1024 * gi_ + 128, [[256, 4], [1, 128]]),
                        in0=fap(expT, 1024 * gi_ + 128, [[256, 4], [1, 128]]),
                        in1=self.mTb4[:].rearrange("p (a b) -> p a b", a=4)))(gi),
                        reads=[bexpT, self.bmTb4], writes=[bexpT])
                    if w == 0 and ti == 0:
                        p.emit("gpsimd", (lambda gi_: lambda e: e.tensor_mul(
                            out=fap(expT, 1024 * gi_, [[256, 4], [1, 128]]),
                            in0=fap(expT, 1024 * gi_, [[256, 4], [1, 128]]),
                            in1=self.mTbp4[:].rearrange("p (a b) -> p a b", a=4)))(gi),
                            reads=[bexpT, self.bmTbp4], writes=[bexpT])

                def av_group(w, gi):
                    grp = GROUPS[gi]
                    expG = expT[:, 1024 * gi:1024 * gi + 1024]
                    for idx, h in enumerate(grp):
                        pg, bpg = ps_av[h // 4]
                        for half in range(2):
                            p.emit("tensor", (lambda h_, idx_, half_, pg_, w_, eG: lambda e: e.matmul(
                                pg_[:, 65 * (h_ % 4): 65 * (h_ % 4) + 65],
                                eG[:, 256 * idx_ + 128 * half_: 256 * idx_ + 128 * half_ + 128],
                                fap(vsb, (w_ + half_) * 1040 + 65 * h_, [[1, 65]]),
                                start=(half_ == 0), stop=(half_ == 1)))(h, idx, half, pg, w, expG),
                                reads=[bexpT, bvsb], writes=[bpg])

                def divisions(w):
                    den16, bden = den16b[w % 2]
                    den16i, bdeni = den16ib[w % 2]
                    attn_tm, btm = attn_tmb[w % 2]
                    for g in range(4):
                        pg, bpg = ps_av[g]
                        p.emit("vector", (lambda g_, pg_, d_: lambda e: e.tensor_scalar_add(
                            out=d_[:, 4 * g_:4 * g_ + 4],
                            in0=fap(pg_, 64, [[65, 4], [1, 1]]),
                            scalar1=0.0))(g, pg, den16),
                            reads=[bpg], writes=[bden])
                    p.emit("vector", (lambda d_, di_: lambda e: e.reciprocal(out=di_[:], in_=d_[:]))(den16, den16i),
                           reads=[bden], writes=[bdeni])
                    for h in range(H):
                        pg, bpg = ps_av[h // 4]
                        if h % 2 == 0:
                            p.emit("vector", (lambda h_, pg_, di_, tm_: lambda e: e.tensor_scalar_mul(
                                out=tm_[:, 64 * h_: 64 * h_ + 64],
                                in0=pg_[:, 65 * (h_ % 4): 65 * (h_ % 4) + 64],
                                scalar1=di_[:, h_:h_ + 1]))(h, pg, den16i, attn_tm),
                                reads=[bpg, bdeni], writes=[btm])
                        else:
                            p.emit("scalar", (lambda h_, pg_, di_, tm_: lambda e: e.mul(
                                out=tm_[:, 64 * h_: 64 * h_ + 64],
                                in_=pg_[:, 65 * (h_ % 4): 65 * (h_ % 4) + 64],
                                mul=di_[:, h_:h_ + 1]))(h, pg, den16i, attn_tm),
                                reads=[bpg, bdeni], writes=[btm])

                def transposes(w):
                    attn_tm, btm = attn_tmb[w % 2]
                    psW, bpsW = psSd[w % 2]
                    psW_bf = psW.ap().bitcast(BF16)
                    for c in range(DC):
                        p.emit("tensor", (lambda c_, pb, tm_: lambda e: e.transpose(
                            pb[:, 128 * c_:128 * c_ + 128],
                            tm_[:, 128 * c_:128 * c_ + 128], self.id_bf[:]))(c, psW_bf, attn_tm),
                            reads=[btm, self.bid_bf], writes=[bpsW])
                    for c in range(DC):
                        p.emit("scalar", (lambda w_, c_, pb, fm2: lambda e: e.activation(
                            out=fm2[:, c_ * TT + 128 * w_: c_ * TT + 128 * w_ + 128],
                            in_=pb[:, 128 * c_:128 * c_ + 128], func=AF.Identity,
                            bias=self.v["vb_a"][:, c_:c_ + 1]))(w, c, psW_bf, attn_fm),
                            reads=[bpsW, self.bv["vb_a"]], writes=[bfm])

                items = [(w, gi) for w in range(4) for gi in range(4)]
                for idx2, (w, gi) in enumerate(items):
                    sc_group(w, gi)
                    if idx2 == 3 and prev_epi is not None:
                        prev_epi()
                    if idx2 >= 2:
                        av_group(*items[idx2 - 2])
                    if gi == 1 and w > 0:
                        divisions(w - 1)
                    if gi == 2 and w > 0:
                        transposes(w - 1)
                av_group(*items[-2])
                av_group(*items[-1])
                divisions(3)
                transposes(3)

                def epi(ti_, par_):
                    x2, bx2 = xb[par_]
                    (mu2, bmu2), (v2, bv2), (r2, br2) = stats[par_]
                    fm2, bfm2 = attn_fmb[par_]
                    xbf2, bxbf2 = xbfb[par_]
                    scr2 = (xbf2, bxbf2, xsq, bxsq, mu2, bmu2, v2, bv2, r2, br2, psS, bpsS)
                    for c in range(DC):
                        p.emit("vector", (lambda c_, x3, r3: lambda e: e.tensor_mul(
                            out=x3[:, c_ * TT:(c_ + 1) * TT], in0=x3[:, c_ * TT:(c_ + 1) * TT],
                            in1=r3[:, 0:TT]))(c, x2, r2),
                            reads=[bx2, br2], writes=[bx2])
                    for c in range(DC):
                        p.emit("gpsimd" if c % 2 else "vector", (lambda c_, x3, m3: lambda e: e.tensor_sub(
                            out=x3[:, c_ * TT:(c_ + 1) * TT], in0=x3[:, c_ * TT:(c_ + 1) * TT],
                            in1=m3[:, 0:TT]))(c, x2, mu2),
                            reads=[bx2, bmu2], writes=[bx2])
                    p.emit("vector", (lambda x3, fm3: lambda e: e.tensor_add(
                        out=x3[:], in0=x3[:], in1=fm3[:]))(x2, fm2),
                        reads=[bx2, bfm2], writes=[bx2])
                    self.emit_ln(ctx, (lambda xs: lambda c: xs[:, c * TT:(c + 1) * TT])(x2), bx2, TT,
                                 gamma=self.v["ln1g"], beta=self.v["ln1b"],
                                 out_f32=(x2, bx2), out_bf=None, scratch=scr2)
                    p.dma("gpsimd", su, s["u"][ti_], x2[:],
                          reads=[bx2], writes=[budram])
                prev_epi = (lambda t_, p_: lambda: epi(t_, p_))(ti, par)
            prev_epi()

    def pass_attn_a_ffn(self, dst):
        p, nc, i, s = self.p, self.nc, self.i, self.s
        import contextlib
        T, NT = self.T, self.NT
        self.barrier()
        with contextlib.ExitStack() as ctx:
            S = p.slot
            f1w, bf1w = self.sb(ctx, "f1w", (128, DC * FFA), BF16)
            f2w, bf2w = self.sb(ctx, "f2w", (128, (FFA // 128) * D), BF16)
            p.dma("gpsimd", S("f1w"), f1w[:], i["ff1w"].rearrange("c p f -> p c f"), writes=[bf1w])
            p.dma("gpsimd", S("f2w"), f2w[:], i["ff2w"].rearrange("c p f -> p c f"), writes=[bf2w])

            ubuf = [self.sb(ctx, f"b_u{j}", (128, DC * TT)) for j in range(2)]
            uslots = [S(f"b_u{j}") for j in range(2)]
            x0_sb, bx0 = self.sb(ctx, "b_x0", (128, DC * TT))
            ffhb = [self.sb(ctx, f"b_ffh{j}", (128, (FFA // 128) * TT), BF16) for j in range(2)]
            w_f, bw_f = self.sb(ctx, "b_w", (128, DC * TT))
            xbf, bxbf = self.sb(ctx, "b_xbf", (128, DC * TT), BF16)
            ubf, bubf = self.sb(ctx, "b_ubf", (128, DC * TT), BF16)
            mu, bmu = self.sb(ctx, "b_mu", (128, TT))
            var, bvar = self.sb(ctx, "b_var", (128, TT))
            rstd, brstd = self.sb(ctx, "b_rstd", (128, TT))
            aout, baout = w_f, bw_f

            psS, bpsS = self.ps(ctx, "b_psS", (128, 2 * TT))
            ps4 = [self.ps(ctx, f"b_ps{j}", (128, TT)) for j in range(4)]
            sx0 = S("x0_in")
            sx1, sy1 = S("x1_out"), S("y1_out")
            bx1dram, by1dram = p.buf("x1dram"), p.buf("y1dram")

            def epilogue(ti, par):
                ffh_old, bffh_old = ffhb[par]
                scr = (xbf, bxbf, ffh_old, bffh_old, mu, bmu, var, bvar, rstd, brstd, psS, bpsS)
                x1, bx1 = ubuf[par]
                self.emit_ln(ctx, (lambda w2: lambda c: w2[:, c * TT:(c + 1) * TT])(w_f), bw_f, TT,
                             gamma=self.v["ln2g"], beta=self.v["ln2b"],
                             out_f32=(aout, baout), out_bf=None, scratch=scr)
                p.emit("gpsimd", (lambda x1_: lambda e: e.tensor_add(
                    out=x1_[:], in0=x0_sb[:], in1=aout[:]))(x1),
                    reads=[bx0, baout], writes=[bx1])
                p.dma("gpsimd", sx1, dst[ti], x1[:], reads=[bx1], writes=[bx1dram])
                self.emit_ln(ctx, (lambda x2: lambda c: x2[:, c * TT:(c + 1) * TT])(x1), bx1, TT,
                             out_f32=None, out_bf=(xbf, bxbf), scratch=scr)
                p.dma("scalar", sy1, s["y1"][ti], xbf[:], reads=[bxbf], writes=[by1dram])
                p.emit("vector", lambda e: e.reduce_max(out=var[:, 0:1], in_=xbf[:], axis=AX.X),
                       reads=[bxbf], writes=[bvar])
                p.emit("vector", lambda e: e.tensor_max(out=self.mloc[:, 0:1], in0=self.mloc[:, 0:1],
                                                        in1=var[:, 0:1]),
                       reads=[bvar, self.bmloc], writes=[self.bmloc])

            for ti in range(NT):
                t0 = ti * TT
                par = ti % 2
                u_sb, bu = ubuf[par]
                ffh, bffh = ffhb[par]
                p.dma("sync", uslots[par], u_sb[:], s["u"][ti], writes=[bu])
                p.emit("vector", (lambda u2: lambda e: e.tensor_copy(out=ubf[:], in_=u2[:]))(u_sb),
                       reads=[bu], writes=[bubf])
                if ti > 0:
                    epilogue(ti - 1, 1 - par)
                p.dma("sync", sx0, x0_sb[:], i["x0t"][ti], writes=[bx0])
                for fo in range(FFA // 128):
                    pst, bpst = ps4[fo % 4]
                    for c in range(DC):
                        p.emit("tensor", (lambda fo_, c_, pst_: lambda e: e.matmul(
                            pst_[:, 0:TT], f1w[:, c_ * FFA + 128 * fo_: c_ * FFA + 128 * fo_ + 128],
                            ubf[:, c_ * TT:(c_ + 1) * TT],
                            start=(c_ == 0), stop=(c_ == DC - 1)))(fo, c, pst),
                            reads=[bf1w, bubf], writes=[bpst])
                    p.emit("scalar", (lambda fo_, pst_, f2: lambda e: e.activation(
                        out=f2[:, fo_ * TT:(fo_ + 1) * TT], in_=pst_[:, 0:TT], func=AF.Relu,
                        bias=self.v["ff1b"][:, fo_:fo_ + 1]))(fo, pst, ffh),
                        reads=[bpst, self.bv["ff1b"]], writes=[bffh])
                for do in range(DC):
                    pst, bpst = ps4[do % 4]
                    for kc in range(FFA // 128):
                        p.emit("tensor", (lambda do_, kc_, pst_, f2: lambda e: e.matmul(
                            pst_[:, 0:TT], f2w[:, kc_ * D + 128 * do_: kc_ * D + 128 * do_ + 128],
                            f2[:, kc_ * TT:(kc_ + 1) * TT],
                            start=(kc_ == 0), stop=(kc_ == FFA // 128 - 1)))(do, kc, pst, ffh),
                            reads=[bf2w, bffh], writes=[bpst])
                    p.emit("vector", (lambda do_, pst_, u2: lambda e: e.scalar_tensor_tensor(
                        out=w_f[:, do_ * TT:(do_ + 1) * TT], in0=pst_[:, 0:TT],
                        scalar=self.v["ff2b"][:, do_:do_ + 1], in1=u2[:, do_ * TT:(do_ + 1) * TT],
                        op0=ALU.add, op1=ALU.add))(do, pst, u_sb),
                        reads=[bpst, self.bv["ff2b"], bu], writes=[bw_f])
            epilogue(NT - 1, (NT - 1) % 2)

    def pass_quantize_f(self):
        p, nc, i, s = self.p, self.nc, self.i, self.s
        import contextlib
        with contextlib.ExitStack() as ctx:
            S = p.slot
            wt = [self.sb(ctx, f"qw{j}", (128, FFQ)) for j in range(2)]
            wslots = [S(f"qw_in{j}") for j in range(2)]
            q1b_ = [self.sb(ctx, f"q_q1{j}", (128, FFQ)) for j in range(2)]
            qbfb_ = [self.sb(ctx, f"q_qbf{j}", (128, FFQ), BF16) for j in range(2)]
            tq, btq = self.sb(ctx, "q_tq", (128, FFQ), BF16)
            amaxb_ = [self.sb(ctx, f"q_amax{j}", (128, 2)) for j in range(2)]
            psQ, bpsQ = self.ps(ctx, "q_ps", (128, TT))
            psQ_bf = psQ.ap().bitcast(BF16)
            sq = S("m_out")
            bm1d, bm2d = p.buf("m1Td"), p.buf("m2Td")

            def prep1(fc):
                wts, bwts = wt[fc % 2]
                q1, bq1 = q1b_[fc % 2]
                qbf, bqbf = qbfb_[fc % 2]
                amax, bamax = amaxb_[fc % 2]
                p.dma("sync", wslots[fc % 2], wts[:, 0:D], i["w1f"][fc], writes=[bwts])
                p.emit("vector", (lambda wts_, am: lambda e: e.reduce_max(
                    out=am[:, 0:1], in_=wts_[:, 0:D], axis=AX.X, apply_absolute_value=True))(wts, amax),
                    reads=[bwts], writes=[bamax])
                p.emit("vector", (lambda fc_, am: lambda e: e.tensor_single_scalar(
                    out=self.sw1[:, fc_:fc_ + 1], in_=am[:, 0:1], scalar=1.0 / 127.0, op=ALU.mult))(fc, amax),
                    reads=[bamax], writes=[self.bsw1])
                p.emit("vector", (lambda fc_, am: lambda e: e.reciprocal(
                    out=am[:, 1:2], in_=self.sw1[:, fc_:fc_ + 1]))(fc, amax),
                    reads=[self.bsw1], writes=[bamax])
                p.emit("vector", (lambda wts_, am, q1_: lambda e: e.tensor_scalar(
                    out=q1_[:, 0:D], in0=wts_[:, 0:D], scalar1=am[:, 1:2], scalar2=2.0 ** 23,
                    op0=ALU.mult, op1=ALU.add))(wts, amax, q1),
                    reads=[bwts, bamax], writes=[bq1])
                p.emit("vector", (lambda q1_, qb_: lambda e: e.tensor_single_scalar(
                    out=qb_[:, 0:D], in_=q1_[:, 0:D], scalar=2.0 ** 23, op=ALU.subtract))(q1, qbf),
                    reads=[bq1], writes=[bqbf])

            prep1(0)
            for fc in range(FFQ // 128):
                qbf, bqbf = qbfb_[fc % 2]
                if fc + 1 < FFQ // 128:
                    prep1(fc + 1)
                for j in range(DC):
                    p.emit("tensor", (lambda j_, qb_: lambda e: e.transpose(
                        psQ_bf[:, 128 * j_:128 * j_ + 128], qb_[:, 128 * j_:128 * j_ + 128],
                        self.id_bf[:]))(j, qbf),
                        reads=[bqbf, self.bid_bf], writes=[bpsQ])
                p.emit("scalar", lambda e: e.copy(out=tq[:, 0:D], in_=psQ_bf[:, 0:D]),
                       reads=[bpsQ], writes=[btq])
                p.dma("scalar", sq, bass.AP(s["m1T"].tensor, 128 * fc,
                                            [[DC * FFQ, 128], [FFQ, DC], [1, 128]]),
                      tq[:, 0:DC * 128], reads=[btq], writes=[bm1d])
            def prep2(dc):
                wts, bwts = wt[dc % 2]
                q1, bq1 = q1b_[dc % 2]
                qbf, bqbf = qbfb_[dc % 2]
                amax, bamax = amaxb_[dc % 2]
                p.dma("sync", wslots[dc % 2], wts[:], i["w2f"][dc], writes=[bwts])
                p.emit("vector", (lambda wts_, am: lambda e: e.reduce_max(
                    out=am[:, 0:1], in_=wts_[:], axis=AX.X, apply_absolute_value=True))(wts, amax),
                    reads=[bwts], writes=[bamax])
                p.emit("vector", (lambda dc_, am: lambda e: e.tensor_single_scalar(
                    out=self.sw2[:, dc_:dc_ + 1], in_=am[:, 0:1], scalar=1.0 / 127.0, op=ALU.mult))(dc, amax),
                    reads=[bamax], writes=[self.bsw2])
                p.emit("vector", (lambda dc_, am: lambda e: e.reciprocal(
                    out=am[:, 1:2], in_=self.sw2[:, dc_:dc_ + 1]))(dc, amax),
                    reads=[self.bsw2], writes=[bamax])
                p.emit("vector", (lambda wts_, am, q1_: lambda e: e.tensor_scalar(
                    out=q1_[:], in0=wts_[:], scalar1=am[:, 1:2], scalar2=2.0 ** 23,
                    op0=ALU.mult, op1=ALU.add))(wts, amax, q1),
                    reads=[bwts, bamax], writes=[bq1])
                p.emit("vector", (lambda q1_, qb_: lambda e: e.tensor_single_scalar(
                    out=qb_[:], in_=q1_[:], scalar=2.0 ** 23, op=ALU.subtract))(q1, qbf),
                    reads=[bq1], writes=[bqbf])

            prep2(0)
            for dc in range(DC):
                qbf, bqbf = qbfb_[dc % 2]
                if dc + 1 < DC:
                    prep2(dc + 1)
                for grp in range(FFQ // TT):
                    for j in range(4):
                        p.emit("tensor", (lambda grp_, j_, qb_: lambda e: e.transpose(
                            psQ_bf[:, 128 * j_:128 * j_ + 128],
                            qb_[:, TT * grp_ + 128 * j_:TT * grp_ + 128 * j_ + 128],
                            self.id_bf[:]))(grp, j, qbf),
                            reads=[bqbf, self.bid_bf], writes=[bpsQ])
                    p.emit("scalar", (lambda grp_: lambda e: e.copy(
                        out=tq[:, TT * grp_:TT * grp_ + TT], in_=psQ_bf[:, 0:TT]))(grp),
                        reads=[bpsQ], writes=[btq])
                p.dma("scalar", sq, bass.AP(s["m2T"].tensor, 128 * dc,
                                            [[(FFQ // 128) * D, 128], [D, FFQ // 128], [1, 128]]),
                      tq[:], reads=[btq], writes=[bm2d])

    def pass_sync_scales(self, site, mcol, ctx=None):
        import contextlib
        p, nc, s = self.p, self.nc, self.s
        S = p.slot
        ctx = self._sync_ctx = contextlib.ExitStack()
        mx1, bmx1 = self.sb(ctx, f"mx1_{site}", (1, 2))
        mrow, bmrow = self.sb(ctx, f"mrow_{site}", (1, 128))
        psX, bpsX = self.ps(ctx, f"psx_{site}", (128, 64))
        bcin, bcout = p.buf(), p.buf()

        mbf, bmbf = self.sb(ctx, f"mbf_{site}", (128, 1), BF16)
        p.emit("vector", lambda e: e.tensor_copy(out=mbf[:], in_=self.mloc[:, mcol:mcol + 1]),
               reads=[self.bmloc], writes=[bmbf])
        p.emit("tensor", lambda e: e.transpose(psX.ap().bitcast(BF16)[0:1, 0:128], mbf[:], self.id_bf[:]),
               reads=[bmbf, self.bid_bf], writes=[bpsX])
        p.emit("scalar", lambda e: e.copy(out=mrow[:], in_=psX.ap().bitcast(BF16)[0:1, 0:128]),
               reads=[bpsX], writes=[bmrow])
        p.emit("vector", lambda e: e.reduce_max(out=mx1[:, 0:1], in_=mrow[:], axis=AX.X),
               reads=[bmrow], writes=[bmx1])
        p.dma("sync", S(f"ci{site}"), s[f"cin{site}"], mx1[:, 0:1], reads=[bmx1], writes=[bcin])
        p.collective("AllReduce", ALU.max, [list(range(NCORES))],
                     [s[f"cin{site}"].opt()], [s[f"cout{site}"].opt()],
                     reads=[bcin], writes=[bcout])
        p.dma("sync", S(f"co{site}"), mx1[:, 0:1], s[f"cout{site}"], reads=[bcout], writes=[bmx1])
        p.emit("vector", lambda e: e.tensor_scalar(out=mx1[:, 0:1], in0=mx1[:, 0:1],
                                                   scalar1=1e-8, scalar2=1.0 / 255.0,
                                                   op0=ALU.max, op1=ALU.mult),
               reads=[bmx1], writes=[bmx1])
        p.emit("vector", lambda e: e.reciprocal(out=mx1[:, 1:2], in_=mx1[:, 0:1]),
               reads=[bmx1], writes=[bmx1])
        p.emit("tensor", lambda e: e.matmul(psX[:, 0:2], self.ones_f[0:1, :], mx1[:, 0:2],
                                            start=True, stop=True),
               reads=[bmx1, self.bones_f], writes=[bpsX])
        p.emit("scalar", (lambda st: lambda e: e.copy(out=self.qcols[:, 2 * st:2 * st + 2],
                                                      in_=psX[:, 0:2]))(site),
               reads=[bpsX], writes=[self.bqcols])
        scol = self.qcols[:, 2 * site:2 * site + 1]
        if site % 2 == 0:
            zst, bzst, bt, bbt, nb = self.zs1, self.bzs1, self.v["b1f"], self.bv["b1f"], FFQ // 128
            bqt, bbqt = self.b1q, self.bb1q
        else:
            zst, bzst, bt, bbt, nb = self.zs2, self.bzs2, self.v["b2f"], self.bv["b2f"], DC
            bqt, bbqt = self.b2q, self.bb2q
        p.emit("vector", (lambda: lambda e: e.tensor_scalar_mul(
            out=zst[:], in0=self.bsw_t(site)[0][:], scalar1=scol))(),
            reads=[self.bsw_t(site)[1], self.bqcols], writes=[bzst])
        zr, bzr = self.sb(ctx, f"zr_{site}", (128, zst.ap().shape[1]))
        p.emit("vector", lambda e: e.reciprocal(out=zr[:], in_=zst[:]),
               reads=[bzst], writes=[bzr])
        p.emit("vector", lambda e: e.tensor_mul(out=bqt[:], in0=bt[:], in1=zr[:]),
               reads=[bbt, bzr], writes=[bbqt])
        p.emit("vector", lambda e: e.tensor_scalar(out=bqt[:], in0=bqt[:], scalar1=2.0 ** 23,
                                                   scalar2=2.0 ** 23, op0=ALU.add, op1=ALU.subtract),
               reads=[bbqt], writes=[bbqt])
        p.emit("vector", lambda e: e.tensor_mul(out=bqt[:], in0=bqt[:], in1=zst[:]),
               reads=[bbqt, bzst], writes=[bbqt])
        ctx.close()

    def bsw_t(self, site):
        return (self.sw1, self.bsw1) if site % 2 == 0 else (self.sw2, self.bsw2)

    def pass_f_l1(self, y_dram, site, mcol):
        p, nc, s = self.p, self.nc, self.s
        import contextlib
        T, NT = self.T, self.NT
        self.barrier()
        with contextlib.ExitStack() as ctx:
            S = p.slot
            m1t, bm1t = self.sb(ctx, "l1_m1t", (128, DC * FFQ), BF16)
            p.dma("sync", S("m1t"), m1t[:], s["m1T"], writes=[bm1t])
            ybuf = [self.sb(ctx, f"l1_y{j}", (128, DC * TT), BF16) for j in range(2)]
            yslots = [S(f"l1_y{j}") for j in range(2)]
            q1b = [self.sb(ctx, f"l1_q1{j}", (128, DC * TT)) for j in range(2)]
            n1b = [self.sb(ctx, f"l1_n1{j}", (128, DC * TT), BF16) for j in range(2)]
            hst, bhst = self.sb(ctx, "l1_hst", (128, 8 * TT), BF16)
            hstslots = [S(f"l1_hs{j}") for j in range(2)]
            mxc, bmxc = self.sb(ctx, "l1_mx", (128, FFQ // 128 + 1))
            ps4 = [self.ps(ctx, f"l1_ps{j}", (128, TT)) for j in range(4)]
            bhd = p.buf("l1_hd")
            rscol = self.qcols[:, 2 * site + 1:2 * site + 2]

            pend = []

            def prep(tj):
                y_sb, by = ybuf[tj % 2]
                q1, bq1 = q1b[tj % 2]
                n1, bn1 = n1b[tj % 2]
                p.dma("sync", yslots[tj % 2], y_sb[:], y_dram[tj], writes=[by])
                p.emit("vector", (lambda y2_, q2_: lambda e: e.tensor_scalar(
                    out=q2_[:], in0=y2_[:], scalar1=rscol, scalar2=2.0 ** 23,
                    op0=ALU.mult, op1=ALU.add))(y_sb, q1),
                    reads=[by, self.bqcols], writes=[bq1])
                p.emit("vector", (lambda q2_, n2_: lambda e: e.tensor_scalar(
                    out=n2_[:], in0=q2_[:], scalar1=2.0 ** 23, scalar2=0.0,
                    op0=ALU.subtract, op1=ALU.max))(q1, n1), reads=[bq1], writes=[bn1])

            prep(0)
            for ti in range(NT):
                t0 = ti * TT
                par = ti % 2
                n1, bn1 = n1b[par]
                if ti + 1 < NT:
                    prep(ti + 1)
                for fn in pend:
                    fn()
                pend = []
                for fo in range(FFQ // 128):
                    pst, bpst = ps4[fo % 4]
                    for c in range(DC):
                        p.emit("tensor", (lambda fo_, c_, pst_, n2_: lambda e: e.matmul(
                            pst_[:, 0:TT], m1t[:, c_ * FFQ + 128 * fo_: c_ * FFQ + 128 * fo_ + 128],
                            n2_[:, c_ * TT:(c_ + 1) * TT],
                            start=(c_ == 0), stop=(c_ == DC - 1)))(fo, c, pst, n1),
                            reads=[bm1t, bn1], writes=[bpst])
                    hoff = (fo % 4) * TT + ((fo // 4) % 2) * 4 * TT
                    p.emit("scalar", (lambda fo_, pst_, ho: lambda e: e.activation(
                        out=hst[:, ho:ho + TT], in_=pst_[:, 0:TT],
                        func=AF.Relu, scale=self.zs1[:, fo_:fo_ + 1],
                        bias=self.b1q[:, fo_:fo_ + 1]))(fo, pst, hoff),
                        reads=[bpst, self.bzs1, self.bb1q], writes=[bhst])
                    if fo % 4 == 3:
                        grp = fo // 4
                        p.emit("vector", (lambda g_: lambda e: e.reduce_max(
                            out=mxc[:, g_:g_ + 1], in_=hst[:, (g_ % 2) * 4 * TT:(g_ % 2) * 4 * TT + 4 * TT],
                            axis=AX.X))(grp),
                            reads=[bhst], writes=[bmxc])
                        p.dma("gpsimd", hstslots[grp % 2],
                              s["h"][ti][:, (fo - 3) * TT:(fo + 1) * TT],
                              hst[:, (grp % 2) * 4 * TT:(grp % 2) * 4 * TT + 4 * TT],
                              reads=[bhst], writes=[bhd])
                pend.append(lambda: p.emit("vector", lambda e: e.reduce_max(
                    out=mxc[:, FFQ // 128:FFQ // 128 + 1], in_=mxc[:, 0:FFQ // 512], axis=AX.X),
                    reads=[bmxc], writes=[bmxc]))
                pend.append(lambda: p.emit("vector", lambda e: e.tensor_max(
                    out=self.mloc[:, mcol:mcol + 1], in0=self.mloc[:, mcol:mcol + 1],
                    in1=mxc[:, FFQ // 128:FFQ // 128 + 1]),
                    reads=[bmxc, self.bmloc], writes=[self.bmloc]))
            for fn in pend:
                fn()

    def pass_f_l2(self, x_dram, dst_dram, site, y_out=None, mcol=None):
        p, nc, s = self.p, self.nc, self.s
        import contextlib
        T, NT = self.T, self.NT
        self.barrier()
        with contextlib.ExitStack() as ctx:
            S = p.slot
            m2t, bm2t = self.sb(ctx, "l2_m2t", (128, (FFQ // 128) * D), BF16)
            p.dma("sync", S("m2t"), m2t[:], s["m2T"], writes=[bm2t])
            hf, bhfl = self.sb(ctx, "l2_hf", (128, (FFQ // 256) * TT), BF16)
            hfslot = S("l2_hf")
            q1, bq1 = self.sb(ctx, "l2_q1", (128, 4 * TT))
            n2full, _bn2 = self.sb(ctx, "l2_n2", (128, (FFQ // 128) * TT), BF16)
            bn2 = [p.buf(f"l2_n2c{k}") for k in range(FFQ // 128)]
            xb2 = [self.sb(ctx, f"l2_x{j}", (128, DC * TT)) for j in range(2)]
            xslots2 = [S(f"l2_x{j}") for j in range(2)]
            xn, bxn = self.sb(ctx, "l2_xn", (128, DC * TT))
            xbf, bxbf = self.sb(ctx, "l2_xbf", (128, DC * TT), BF16)
            xsq, bxsq = self.sb(ctx, "l2_xsq", (128, DC * TT), BF16)
            mu, bmu = self.sb(ctx, "l2_mu", (128, TT))
            var, bvar = self.sb(ctx, "l2_var", (128, TT))
            rstd, brstd = self.sb(ctx, "l2_rstd", (128, TT))
            yb, byb = xbf, bxbf
            mxc, bmxc = self.sb(ctx, "l2_mx", (128, 1))
            ps4 = [self.ps(ctx, f"l2_ps{j}", (128, TT)) for j in range(4)]
            psS, bpsS = self.ps(ctx, "l2_psS", (128, 2 * TT))
            scr = (xbf, bxbf, xsq, bxsq, mu, bmu, var, bvar, rstd, brstd, psS, bpsS)
            sxd, syd = S("l2_xd"), S("l2_yd")
            bxd, byd = p.buf("l2_xd"), p.buf("l2_yd")
            rscol = self.qcols[:, 2 * site + 1:2 * site + 2]

            def epilogue(ti, par):
                x_sb, bxs = xb2[par]
                p.emit("vector", (lambda x2: lambda e: e.tensor_add(
                    out=xn[:], in0=xn[:], in1=x2[:]))(x_sb),
                    reads=[bxs, bxn], writes=[bxn])
                p.dma("gpsimd", sxd, dst_dram[ti], xn[:],
                      reads=[bxn], writes=[bxd])
                if y_out is not None:
                    self.emit_ln(ctx, lambda c: xn[:, c * TT:(c + 1) * TT], bxn, TT,
                                 out_f32=None, out_bf=(yb, byb), scratch=scr)
                    p.dma("scalar", syd, y_out[ti], yb[:],
                          reads=[byb], writes=[byd])
                    if mcol is not None:
                        p.emit("vector", lambda e: e.reduce_max(out=mxc[:], in_=yb[:], axis=AX.X),
                               reads=[byb], writes=[bmxc])
                        p.emit("vector", lambda e: e.tensor_max(
                            out=self.mloc[:, mcol:mcol + 1], in0=self.mloc[:, mcol:mcol + 1], in1=mxc[:]),
                            reads=[bmxc, self.bmloc], writes=[self.bmloc])

            for ti in range(NT):
                t0 = ti * TT
                par = ti % 2
                x_sb, bxs = xb2[par]
                p.dma("sync", xslots2[par], x_sb[:], x_dram[ti], writes=[bxs])
                NH = FFQ // 256
                for half in range(2):
                    p.dma("sync", hfslot, hf[:],
                          s["h"][ti][:, half * NH * TT:(half + 1) * NH * TT],
                          writes=[bhfl])
                    for j4 in range(NH // 4):
                        kc0 = half * NH + j4 * 4
                        p.emit("vector", (lambda j4_: lambda e: e.tensor_scalar(
                            out=q1[:], in0=hf[:, j4_ * 4 * TT:(j4_ + 1) * 4 * TT], scalar1=rscol,
                            scalar2=2.0 ** 23, op0=ALU.mult, op1=ALU.add))(j4),
                            reads=[bhfl, self.bqcols], writes=[bq1])
                        p.emit("vector", (lambda kc0_: lambda e: e.tensor_single_scalar(
                            out=n2full[:, kc0_ * TT:(kc0_ + 4) * TT], in_=q1[:],
                            scalar=2.0 ** 23, op=ALU.subtract))(kc0),
                            reads=[bq1], writes=[bn2[kc0], bn2[kc0 + 1], bn2[kc0 + 2], bn2[kc0 + 3]])
                if ti > 0:
                    epilogue(ti - 1, 1 - par)
                for do in range(DC):
                    pst, bpst = ps4[do % 4]
                    for kc in range(FFQ // 128):
                        p.emit("tensor", (lambda do_, kc_, pst_: lambda e: e.matmul(
                            pst_[:, 0:TT], m2t[:, kc_ * D + 128 * do_: kc_ * D + 128 * do_ + 128],
                            n2full[:, kc_ * TT:(kc_ + 1) * TT],
                            start=(kc_ == 0), stop=(kc_ == FFQ // 128 - 1)))(do, kc, pst),
                            reads=[bm2t, bn2[kc]], writes=[bpst])
                    p.emit("scalar", (lambda do_, pst_: lambda e: e.activation(
                        out=xn[:, do_ * TT:(do_ + 1) * TT], in_=pst_[:, 0:TT], func=AF.Identity,
                        scale=self.zs2[:, do_:do_ + 1], bias=self.b2q[:, do_:do_ + 1]))(do, pst),
                        reads=[bpst, self.bzs2, self.bb2q], writes=[bxn])
            epilogue(NT - 1, (NT - 1) % 2)

    def pass_attn_b_front(self):
        p, nc, i, s = self.p, self.nc, self.i, self.s
        import contextlib
        T, NT = self.T, self.NT
        self.barrier()
        with contextlib.ExitStack() as ctx:
            S = p.slot
            wq, bwq = self.sb(ctx, "bf_wq", (128, DC * D), BF16)
            wk, bwk = self.sb(ctx, "bf_wk", (128, DC * D), BF16)
            wv, bwv = self.sb(ctx, "bf_wv", (128, DC * D), BF16)
            p.dma("gpsimd", S("bwq"), wq[:], i["wq_b"].rearrange("c p f -> p c f"), writes=[bwq])
            p.dma("gpsimd", S("bwk"), wk[:], i["wk_b"].rearrange("c p f -> p c f"), writes=[bwk])
            p.dma("gpsimd", S("bwv"), wv[:], i["wv_b"].rearrange("c p f -> p c f"), writes=[bwv])

            ybP = [self.sb(ctx, f"bf_y{j}", (128, DC * TT), BF16) for j in range(2)]
            yslotsP = [S(f"bf_y{j}") for j in range(2)]
            xbf, bxbf = self.sb(ctx, "bf_xbf", (128, DC * TT), BF16)
            xsq, bxsq = self.sb(ctx, "bf_xsq", (128, DC * TT), BF16)
            mu, bmu = self.sb(ctx, "bf_mu", (128, TT))
            var, bvar = self.sb(ctx, "bf_var", (128, TT))
            rstd, brstd = self.sb(ctx, "bf_rstd", (128, TT))
            hbP = [self.sb(ctx, f"bf_hb{j}", (128, DC * TT), BF16) for j in range(2)]
            qt, bqt = self.sb(ctx, "bf_qt", (128, DC * TT), BF16)
            kt, bkt = self.sb(ctx, "bf_kt", (128, DC * TT), BF16)
            qf, bqf = self.sb(ctx, "bf_qf", (128, DC * TT), BF16)
            kf, bkf = self.sb(ctx, "bf_kf", (128, DC * TT), BF16)
            rb, brb = self.sb(ctx, "bf_rb", (128, TT), BF16)
            eb, beb = self.sb(ctx, "bf_eb", (128, TT), BF16)
            vt, bvt = self.sb(ctx, "bf_vt", (128, (TT // 128) * D), BF16)
            ktm_sb, bktm = self.sb(ctx, "bf_ktm", (128, (TT // 128) * D), BF16)
            stA, bstA = self.sb(ctx, "bf_stA", (64, 16 * 65))

            ps_a, bps_a = self.ps(ctx, "bf_psA", (128, TT))
            ps_b, bps_b = self.ps(ctx, "bf_psB", (128, TT))
            psS, bpsS = self.ps(ctx, "bf_psS", (128, 2 * TT))
            psKV, bpsKV = self.ps(ctx, "bf_psKV", (64, 1024))
            psKS, bpsKS = self.ps(ctx, "bf_psKS", (64, 16))
            scr = (xbf, bxbf, xsq, bxsq, mu, bmu, var, bvar, rstd, brstd, psS, bpsS)
            sq, sk, sktm, svt = S("bf_q"), S("bf_k"), S("bf_ktm"), S("bf_vt")
            bqd, bkd, bktd, bvd = p.buf(), p.buf(), p.buf(), p.buf()
            sgin = S("bf_gin")
            bgin = p.buf()
            NBLK = TT // 128


            CUTB = getattr(self, "cutb", 99)

            def prep(tj):
                y_sb, by = ybP[tj % 2]
                hb_, bhb_ = hbP[tj % 2]
                p.dma("sync", yslotsP[tj % 2], y_sb[:], s["y2"][tj], writes=[by])
                self.emit_ln(ctx, (lambda y2_: lambda c: y2_[:, c * TT:(c + 1) * TT])(y_sb), by, TT,
                             gamma=self.v["g1"], beta=self.v["be1"],
                             out_f32=None, out_bf=(hb_, bhb_), scratch=scr)

            prep(0)
            for ti in range(NT):
                t0 = ti * TT
                hb, bhb = hbP[ti % 2]
                if ti + 1 < NT:
                    prep(ti + 1)
                if CUTB < 2:
                    continue
                for which, wsb, bwsb, dst, bdst in (("q", wq, bwq, qt, bqt), ("k", wk, bwk, kt, bkt)):
                    for oc in range(DC):
                        pst, bpst = (ps_a, bps_a) if oc % 2 == 0 else (ps_b, bps_b)
                        for c in range(DC):
                            p.emit("tensor", (lambda oc_, c_, pst_, wsb_, hb_: lambda e: e.matmul(
                                pst_[:, 0:TT], wsb_[:, c_ * D + 128 * oc_: c_ * D + 128 * oc_ + 128],
                                hb_[:, c_ * TT:(c_ + 1) * TT],
                                start=(c_ == 0), stop=(c_ == DC - 1)))(oc, c, pst, wsb, hb),
                                reads=[bwsb, bhb], writes=[bpst])
                        p.emit("scalar", (lambda oc_, pst_, dst_: lambda e: e.copy(
                            out=dst_[:, oc_ * TT:(oc_ + 1) * TT], in_=pst_[:, 0:TT]))(oc, pst, dst),
                            reads=[bpst], writes=[bdst])
                if CUTB < 3:
                    continue
                for which, src_t, bsrc, dst, bdst, sd, bdd in (
                        ("q", qt, bqt, qf, bqf, sq, bqd), ("k", kt, bkt, kf, bkf, sk, bkd)):
                    for c in range(DC):
                        pst, bpst = (ps_a, bps_a) if c % 2 == 0 else (ps_b, bps_b)
                        p.emit("tensor", (lambda c_, pst_, src2: lambda e: e.matmul(
                            pst_[:, 0:TT], self.projblk[:], src2[:, c_ * TT:(c_ + 1) * TT],
                            start=True, stop=True))(c, pst, src_t),
                            reads=[self.bprojblk, bsrc], writes=[bpst])
                        if not getattr(self, "skip_exp", False):
                            p.emit("scalar", (lambda pst_: lambda e: e.activation(
                                out=eb[:], in_=pst_[:, 0:TT], func=AF.Exp))(pst),
                                reads=[bpst], writes=[beb])
                        if not getattr(self, "skip_relu", False):
                            p.emit("vector", (lambda pst_: lambda e: e.tensor_single_scalar(
                                out=rb[:], in_=pst_[:, 0:TT], scalar=0.0, op=ALU.max))(pst),
                                reads=[bpst], writes=[brb])
                        if not getattr(self, "skip_stt", False):
                            p.emit("vector", (lambda c_, dst_: lambda e: e.scalar_tensor_tensor(
                                out=dst_[:, c_ * TT:(c_ + 1) * TT], in0=eb[:], scalar=1.0, in1=rb[:],
                                op0=ALU.min, op1=ALU.add))(c, dst),
                                reads=[beb, brb], writes=[bdst])
                    if not getattr(self, "skip_spill", False):
                        p.dma("scalar", sd, (s["qfm"] if which == "q" else s["kfm"])[ti],
                              dst[:], reads=[bdst], writes=[bdd])
                if CUTB < 4:
                    continue
                for b_ in range(NBLK):
                    for half in range(2):
                        pst, bpst = (ps_a, bps_a) if half == 0 else (ps_b, bps_b)
                        for c in range(DC):
                            p.emit("tensor", (lambda b2, h2, c_, pst_, hb_: lambda e: e.matmul(
                                pst_[:, 0:512],
                                hb_[:, c_ * TT + 128 * b2: c_ * TT + 128 * b2 + 128],
                                wv[:, c_ * D + 512 * h2: c_ * D + 512 * h2 + 512],
                                start=(c_ == 0), stop=(c_ == DC - 1)))(b_, half, c, pst, hb),
                                reads=[bwv, bhb], writes=[bpst])
                        p.emit("scalar", (lambda b2, h2, pst_: lambda e: e.copy(
                            out=vt[:, b2 * D + 512 * h2: b2 * D + 512 * h2 + 512],
                            in_=pst_[:, 0:512]))(b_, half, pst),
                            reads=[bpst], writes=[bvt])
                p.dma("scalar", svt, s["vtm"][ti], vt[:], reads=[bvt], writes=[bvd])
                if CUTB < 5:
                    continue
                for b_ in range(NBLK):
                    for c in range(DC):
                        p.emit("tensor", (lambda b2, c_: lambda e: e.transpose(
                            psS.ap().bitcast(BF16)[:, 128 * c_:128 * c_ + 128],
                            kf[:, c_ * TT + 128 * b2: c_ * TT + 128 * b2 + 128],
                            self.id_bf[:]))(b_, c),
                            reads=[bkf, self.bid_bf], writes=[bpsS])
                    p.emit("scalar", (lambda b2: lambda e: e.copy(
                        out=ktm_sb[:, b2 * D:(b2 + 1) * D], in_=psS.ap().bitcast(BF16)[:, 0:D]))(b_),
                        reads=[bpsS], writes=[bktm])
                    first_blk = (ti == 0 and b_ == 0)
                    last_blk = (ti == NT - 1 and b_ == NBLK - 1)
                    for h in range(H):
                        kv_start = first_blk and h in (0, 8)
                        kv_stop = last_blk and h in (7, 15)
                        p.emit("tensor", (lambda b2, h_, st_, sp_: lambda e: e.matmul(
                            psKV[:, 64 * h_:64 * h_ + 64],
                            ktm_sb[:, b2 * D + 64 * h_: b2 * D + 64 * h_ + 64],
                            vt[:, b2 * D + 64 * h_: b2 * D + 64 * h_ + 64],
                            start=st_, stop=sp_))(b_, h, kv_start, kv_stop),
                            reads=[bktm, bvt], writes=[bpsKV])
                        p.emit("tensor", (lambda b2, h_, st_, sp_: lambda e: e.matmul(
                            psKS[:, h_:h_ + 1],
                            ktm_sb[:, b2 * D + 64 * h_: b2 * D + 64 * h_ + 64],
                            self.ones_bf[:, 0:1],
                            start=st_, stop=sp_))(b_, h, first_blk and h == 0, last_blk and h == 15),
                            reads=[bktm, self.bones], writes=[bpsKS])
                p.dma("scalar", sktm, s["ktm"][ti], ktm_sb[:], reads=[bktm], writes=[bktd])

            if CUTB < 6:
                return
            p.emit("scalar", lambda e: e.copy(
                out=fap(stA, 0, [[65, 8], [1, 64]]),
                in_=fap(psKV, 0, [[128, 8], [1, 64]])),
                reads=[bpsKV], writes=[bstA])
            p.emit("scalar", lambda e: e.copy(
                out=fap(stA, 520, [[65, 8], [1, 64]]),
                in_=fap(psKV, 64, [[128, 8], [1, 64]])),
                reads=[bpsKV], writes=[bstA])
            p.emit("vector", lambda e: e.tensor_scalar_add(
                out=fap(stA, 64, [[65, 8], [1, 1]]),
                in0=fap(psKS, 0, [[2, 8], [1, 1]]), scalar1=0.0),
                reads=[bpsKS], writes=[bstA])
            p.emit("vector", lambda e: e.tensor_scalar_add(
                out=fap(stA, 520 + 64, [[65, 8], [1, 1]]),
                in0=fap(psKS, 1, [[2, 8], [1, 1]]), scalar1=0.0),
                reads=[bpsKS], writes=[bstA])
            p.dma("sync", sgin, s["gin"], stA[:], reads=[bstA], writes=[bgin])
            bgout = p.buf("gout")
            p.collective("AllGather", ALU.bypass, [list(range(NCORES))],
                         [s["gin"].opt()], [s["gout"].opt()],
                         reads=[bgin], writes=[bgout])
            self.bgout = bgout

    def pass_attn_b_scan(self):
        p, nc, i, s = self.p, self.nc, self.i, self.s
        import contextlib
        T, NT = self.T, self.NT
        NBLK = TT // 128
        self.barrier()
        with contextlib.ExitStack() as ctx:
            S = p.slot
            wo, bwo = self.sb(ctx, "sc_wo", (128, DC * D), BF16)
            p.dma("gpsimd", S("wo"), wo[:], i["wo_b"].rearrange("c p f -> p c f"), writes=[bwo])
            qfm, bqfm = self.sb(ctx, "sc_q", (128, DC * TT), BF16)
            kfm, bkfm = self.sb(ctx, "sc_k", (128, DC * TT), BF16)
            ktm, bktm = self.sb(ctx, "sc_ktm", (128, NBLK * D), BF16)
            vplain, bvplain = self.sb(ctx, "sc_vp", (128, NBLK * D), BF16)
            vt65, bvt65 = self.sb(ctx, "sc_v65", (128, NBLK * H * 65), BF16)
            y2t, by2 = self.sb(ctx, "sc_y2", (128, DC * TT), BF16)
            KVr, bKVr = self.sb(ctx, "sc_kvr", (64, 16 * 65))
            stbfs = [self.sb(ctx, f"sc_stbf{j}", (128, 16 * 65), BF16) for j in range(TT // 128)]
            gacc = [self.sb(ctx, f"sc_ga{j}", (64, 16 * 65)) for j in range(2)]
            grk, bgrk = self.sb(ctx, "sc_grk", (64, 16 * 65))
            mskd, bmskd = self.sb(ctx, "sc_mskd", (128, 2 * 512), BF16)
            attn_tmb = [self.sb(ctx, f"sc_atm{j}", (128, 1024), BF16) for j in range(2)]
            attn_fm, bfm = self.sb(ctx, "sc_afm", (128, DC * TT), BF16)
            den16b = [self.sb(ctx, f"sc_den{j}", (128, H)) for j in range(2)]
            den16ib = [self.sb(ctx, f"sc_deni{j}", (128, H)) for j in range(2)]
            xo, bxo = self.sb(ctx, "sc_xo", (128, DC * TT))
            ps_a, bps_a = self.ps(ctx, "sc_psA", (128, TT))
            ps_b, bps_b = self.ps(ctx, "sc_psB", (128, TT))
            psN = [self.ps(ctx, f"sc_psN{g}", (128, 4 * 65)) for g in range(2)]
            psSt, bpsSt = self.ps(ctx, "sc_psSt", (64, 1024))
            psKS, bpsKS = self.ps(ctx, "sc_psKS", (64, 16))
            psT, bpsT = self.ps(ctx, "sc_psT", (128, 512))
            sq, sk, sktm, sv, sy2 = S("sc_q"), S("sc_k"), S("sc_ktm"), S("sc_v"), S("sc_y2")
            sxo = S("sc_xo")
            sod = S("sc_odd")
            bxod = p.buf()
            sgo = S("sc_go")

            p.emit("vector", lambda e: e.memset(gacc[0][0][:], 0.0), writes=[gacc[0][1]])
            for r in range(NCORES):
                src_t, bsrc = gacc[r % 2]
                dst_t, bdst = gacc[(r + 1) % 2]
                p.dma("sync", sgo, grk[:], s["gout"][r * 64:(r + 1) * 64, :],
                      reads=[self.bgout], writes=[bgrk])
                p.emit("vector", (lambda r_, s_, d_: lambda e: e.scalar_tensor_tensor(
                    out=d_[:], in0=grk[:], scalar=self.v["wpref"][0:64, r_:r_ + 1],
                    in1=s_[:], op0=ALU.mult, op1=ALU.add))(r, src_t, dst_t),
                    reads=[bgrk, bsrc, self.bv["wpref"]], writes=[bdst])
            p.emit("vector", lambda e: e.tensor_copy(out=KVr[:], in_=gacc[NCORES % 2][0][:]),
                   reads=[gacc[NCORES % 2][1]], writes=[bKVr])
            for stbf_j, bstbf_j in stbfs:
                p.emit("vector", (lambda s2: lambda e: e.memset(s2[:], 0.0))(stbf_j),
                       writes=[bstbf_j])
            p.emit("vector", lambda e: e.memset(
                fap(vt65, 64, [[1040, NBLK], [65, H], [1, 1]]), 1.0),
                writes=[bvt65])

            def upd_statebf(ci):
                stbf_j, bstbf_j = stbfs[ci]
                p.emit("vector", (lambda s2: lambda e: e.tensor_copy(
                    out=s2[0:64, 0:520], in_=KVr[0:64, 0:520]))(stbf_j),
                    reads=[bKVr], writes=[bstbf_j])
                p.dma("gpsimd", sod, stbf_j[64:128, 520:1040], KVr[0:64, 520:1040],
                      reads=[bKVr], writes=[bstbf_j])

            for ti in range(NT):
                t0 = ti * TT
                p.dma("sync", sq, qfm[:], s["qfm"][ti], writes=[bqfm])
                p.dma("sync", sk, kfm[:], s["kfm"][ti], writes=[bkfm])
                p.dma("sync", sktm, ktm[:], s["ktm"][ti], writes=[bktm])
                p.dma("sync", sv, vplain[:], s["vtm"][ti], writes=[bvplain])
                p.dma("scalar", sy2, y2t[:], s["y2"][ti], writes=[by2])
                p.emit("vector", lambda e: e.tensor_copy(
                    out=fap(vt65, 0, [[1040, NBLK], [65, H], [1, 64]]),
                    in_=vplain[:].rearrange("p (b f) -> p b f", b=NBLK).rearrange("p b (h e) -> p b h e", h=H)),
                    reads=[bvplain], writes=[bvt65])
                for ch in range(NBLK):
                    upd_statebf(ch)
                    for h in range(H):
                        p.emit("tensor", (lambda h_, ch_: lambda e: e.matmul(
                            psSt[:, 64 * h_:64 * h_ + 64],
                            ktm[:, ch_ * D + 64 * h_: ch_ * D + 64 * h_ + 64],
                            vplain[:, ch_ * D + 64 * h_: ch_ * D + 64 * h_ + 64],
                            start=True, stop=True))(h, ch),
                            reads=[bktm, bvplain], writes=[bpsSt])
                        p.emit("tensor", (lambda h_, ch_: lambda e: e.matmul(
                            psKS[:, h_:h_ + 1],
                            ktm[:, ch_ * D + 64 * h_: ch_ * D + 64 * h_ + 64],
                            self.ones_bf[:, 0:1],
                            start=True, stop=True))(h, ch),
                            reads=[bktm, self.bones], writes=[bpsKS])
                    p.emit("vector", lambda e: e.tensor_add(
                        out=fap(KVr, 0, [[65, 8], [1, 64]]),
                        in0=fap(KVr, 0, [[65, 8], [1, 64]]),
                        in1=fap(psSt, 0, [[128, 8], [1, 64]])),
                        reads=[bpsSt, bKVr], writes=[bKVr])
                    p.emit("vector", lambda e: e.tensor_add(
                        out=fap(KVr, 520, [[65, 8], [1, 64]]),
                        in0=fap(KVr, 520, [[65, 8], [1, 64]]),
                        in1=fap(psSt, 64, [[128, 8], [1, 64]])),
                        reads=[bpsSt, bKVr], writes=[bKVr])
                    p.emit("vector", lambda e: e.tensor_add(
                        out=fap(KVr, 64, [[65, 8], [1, 1]]),
                        in0=fap(KVr, 64, [[65, 8], [1, 1]]),
                        in1=fap(psKS, 0, [[2, 8], [1, 1]])),
                        reads=[bpsKS, bKVr], writes=[bKVr])
                    p.emit("vector", lambda e: e.tensor_add(
                        out=fap(KVr, 520 + 64, [[65, 8], [1, 1]]),
                        in0=fap(KVr, 520 + 64, [[65, 8], [1, 1]]),
                        in1=fap(psKS, 1, [[2, 8], [1, 1]])),
                        reads=[bpsKS, bKVr], writes=[bKVr])
                GROUPS = [(0, 2, 4, 6), (8, 10, 12, 14), (1, 3, 5, 7), (9, 11, 13, 15)]

                def sc_group(ch, gi):
                    grp = GROUPS[gi]
                    pst, bpst = (ps_a, bps_a) if gi % 2 == 0 else (ps_b, bps_b)
                    pr0 = 64 * (grp[0] % 2)
                    for idx, h in enumerate(grp):
                        p.emit("tensor", (lambda h_, pr0_, pst_, idx_, ch_: lambda e: e.matmul(
                            pst_[:, 128 * idx_:128 * idx_ + 128],
                            kfm[pr0_:pr0_ + 64, (h_ // 2) * TT + 128 * ch_:(h_ // 2) * TT + 128 * ch_ + 128],
                            qfm[pr0_:pr0_ + 64, (h_ // 2) * TT + 128 * ch_:(h_ // 2) * TT + 128 * ch_ + 128],
                            start=True, stop=True))(h, pr0, pst, idx, ch),
                            reads=[bkfm, bqfm], writes=[bpst])
                    p.emit("vector", (lambda gi_, pst_: lambda e: e.tensor_mul(
                        out=mskd[:, 512 * (gi_ % 2):512 * (gi_ % 2) + 512],
                        in0=pst_[:, 0:512], in1=self.mTb4[:]))(gi, pst),
                        reads=[bpst, self.bmTb4], writes=[bmskd])

                def num_group(ch, gi):
                    grp = GROUPS[gi]
                    pn, bpn = psN[gi % 2]
                    for idx, h in enumerate(grp):
                        base = 65 * (h // 2) if h % 2 == 0 else 520 + 65 * (h // 2)
                        p.emit("tensor", (lambda h_, gi_, pn_, idx_, ch_: lambda e: e.matmul(
                            pn_[:, 65 * idx_:65 * idx_ + 65],
                            mskd[:, 512 * (gi_ % 2) + 128 * idx_:512 * (gi_ % 2) + 128 * idx_ + 128],
                            fap(vt65, ch_ * 1040 + 65 * h_, [[1, 65]]),
                            start=True, stop=False))(h, gi, pn, idx, ch),
                            reads=[bmskd, bvt65], writes=[bpn])
                        p.emit("tensor", (lambda h_, pn_, idx_, base_, ch_, st2: lambda e: e.matmul(
                            pn_[:, 65 * idx_:65 * idx_ + 65],
                            qfm[:, (h_ // 2) * TT + 128 * ch_:(h_ // 2) * TT + 128 * ch_ + 128],
                            st2[:, base_:base_ + 65],
                            start=False, stop=True))(h, pn, idx, base, ch, stbfs[ch][0]),
                            reads=[bqfm, stbfs[ch][1]], writes=[bpn])

                def div_group(ch, gi):
                    grp = GROUPS[gi]
                    pn, bpn = psN[gi % 2]
                    den16, bden = den16b[ch % 2]
                    den16i, bdeni = den16ib[ch % 2]
                    attn_tm, btm = attn_tmb[ch % 2]
                    p.emit("vector", (lambda pn_, d_, gi_: lambda e: e.tensor_scalar_add(
                        out=d_[:, 4 * gi_:4 * gi_ + 4], in0=fap(pn_, 64, [[65, 4], [1, 1]]),
                        scalar1=1e-6))(pn, den16, gi),
                        reads=[bpn], writes=[bden])
                    p.emit("vector", (lambda d_, di_, gi_: lambda e: e.reciprocal(
                        out=di_[:, 4 * gi_:4 * gi_ + 4], in_=d_[:, 4 * gi_:4 * gi_ + 4]))(den16, den16i, gi),
                        reads=[bden], writes=[bdeni])
                    for idx, h in enumerate(grp):
                        eng = "vector" if idx % 2 == 0 else "scalar"
                        if eng == "vector":
                            p.emit("vector", (lambda h_, idx_, pn_, di_, tm_, gi_: lambda e: e.tensor_scalar_mul(
                                out=tm_[:, 64 * h_:64 * h_ + 64],
                                in0=pn_[:, 65 * idx_:65 * idx_ + 64],
                                scalar1=di_[:, 4 * gi_ + idx_:4 * gi_ + idx_ + 1]))(h, idx, pn, den16i, attn_tm, gi),
                                reads=[bpn, bdeni], writes=[btm])
                        else:
                            p.emit("scalar", (lambda h_, idx_, pn_, di_, tm_, gi_: lambda e: e.mul(
                                out=tm_[:, 64 * h_:64 * h_ + 64],
                                in_=pn_[:, 65 * idx_:65 * idx_ + 64],
                                mul=di_[:, 4 * gi_ + idx_:4 * gi_ + idx_ + 1]))(h, idx, pn, den16i, attn_tm, gi),
                                reads=[bpn, bdeni], writes=[btm])

                def transposes(ch):
                    attn_tm, btm = attn_tmb[ch % 2]
                    ptbf = psT.ap().bitcast(BF16)
                    for c in range(DC):
                        p.emit("tensor", (lambda c_, tm_: lambda e: e.transpose(
                            ptbf[:, 128 * c_:128 * c_ + 128],
                            tm_[:, 128 * c_:128 * c_ + 128], self.id_bf[:]))(c, attn_tm),
                            reads=[btm, self.bid_bf], writes=[bpsT])
                    p.emit("scalar", (lambda ch_: lambda e: e.copy(
                        out=fap(attn_fm, 128 * ch_, [[TT, DC], [1, 128]]),
                        in_=ptbf[:, 0:1024]))(ch),
                        reads=[bpsT], writes=[bfm])

                items = [(ch, gi) for ch in range(NBLK) for gi in range(4)]
                for idx2, (ch, gi) in enumerate(items):
                    sc_group(ch, gi)
                    if idx2 >= 1:
                        num_group(*items[idx2 - 1])
                    if idx2 >= 2:
                        div_group(*items[idx2 - 2])
                    if gi == 2 and ch > 0:
                        transposes(ch - 1)
                num_group(*items[-1])
                div_group(*items[-2])
                div_group(*items[-1])
                transposes(NBLK - 1)
                for oc in range(DC):
                    pst, bpst = (ps_a, bps_a) if oc % 2 == 0 else (ps_b, bps_b)
                    for c in range(DC):
                        p.emit("tensor", (lambda oc_, c_, pst_: lambda e: e.matmul(
                            pst_[:, 0:TT], wo[:, c_ * D + 128 * oc_: c_ * D + 128 * oc_ + 128],
                            attn_fm[:, c_ * TT:(c_ + 1) * TT],
                            start=(c_ == 0), stop=(c_ == DC - 1)))(oc, c, pst),
                            reads=[bwo, bfm], writes=[bpst])
                    p.emit("vector", (lambda oc_, pst_: lambda e: e.scalar_tensor_tensor(
                        out=xo[:, oc_ * TT:(oc_ + 1) * TT], in0=pst_[:, 0:TT], scalar=1.0,
                        in1=y2t[:, oc_ * TT:(oc_ + 1) * TT], op0=ALU.mult, op1=ALU.add))(oc, pst),
                        reads=[bpst, by2], writes=[bxo])
                p.dma("gpsimd", sxo, s["xo"][ti], xo[:],
                      reads=[bxo], writes=[bxod])

    def pass_attn_b_ffn1(self):
        p, nc, i, s = self.p, self.nc, self.i, self.s
        import contextlib
        T, NT = self.T, self.NT
        self.barrier()
        with contextlib.ExitStack() as ctx:
            S = p.slot
            w1, bw1 = self.sb(ctx, "g1_w1", (128, DC * FFB), BF16)
            p.dma("gpsimd", S("w1b"), w1[:], i["w1b"].rearrange("c p f -> p c f"), writes=[bw1])
            xob = [self.sb(ctx, f"g1_xo{j}", (128, DC * TT)) for j in range(2)]
            xoslots = [S(f"g1_xo{j}") for j in range(2)]
            xbf, bxbf = self.sb(ctx, "g1_xbf", (128, DC * TT), BF16)
            xsq, bxsq = self.sb(ctx, "g1_xsq", (128, DC * TT), BF16)
            mu, bmu = self.sb(ctx, "g1_mu", (128, TT))
            var, bvar = self.sb(ctx, "g1_var", (128, TT))
            rstd, brstd = self.sb(ctx, "g1_rstd", (128, TT))
            h2b = [self.sb(ctx, f"g1_h2{j}", (128, DC * TT), BF16) for j in range(2)]
            hst, bhst = self.sb(ctx, "g1_hst", (128, 8 * TT), BF16)
            hstslots = [S(f"g1_hs{j}") for j in range(2)]
            ps4 = [self.ps(ctx, f"g1_ps{j}", (128, TT)) for j in range(4)]
            psS, bpsS = self.ps(ctx, "g1_psS", (128, 2 * TT))
            scr = (xbf, bxbf, xsq, bxsq, mu, bmu, var, bvar, rstd, brstd, psS, bpsS)
            bhd = p.buf()

            def prep(tj):
                xo_sb, bxo = xob[tj % 2]
                h2, bh2 = h2b[tj % 2]
                p.dma("sync", xoslots[tj % 2], xo_sb[:], s["xo"][tj], writes=[bxo])
                self.emit_ln(ctx, (lambda x2: lambda c: x2[:, c * TT:(c + 1) * TT])(xo_sb), bxo, TT,
                             gamma=self.v["g2"], beta=self.v["be2"],
                             out_f32=None, out_bf=(h2, bh2), scratch=scr)

            prep(0)
            for ti in range(NT):
                t0 = ti * TT
                h2, bh2 = h2b[ti % 2]
                if ti + 1 < NT:
                    prep(ti + 1)
                for fo in range(FFB // 128):
                    pst, bpst = ps4[fo % 4]
                    for c in range(DC):
                        p.emit("tensor", (lambda fo_, c_, pst_, h2_: lambda e: e.matmul(
                            pst_[:, 0:TT], w1[:, c_ * FFB + 128 * fo_: c_ * FFB + 128 * fo_ + 128],
                            h2_[:, c_ * TT:(c_ + 1) * TT],
                            start=(c_ == 0), stop=(c_ == DC - 1)))(fo, c, pst, h2),
                            reads=[bw1, bh2], writes=[bpst])
                    hoff = (fo % 4) * TT + ((fo // 4) % 2) * 4 * TT
                    p.emit("scalar", (lambda fo_, pst_, ho: lambda e: e.activation(
                        out=hst[:, ho:ho + TT], in_=pst_[:, 0:TT],
                        func=AF.Gelu, bias=self.v["b1b"][:, fo_:fo_ + 1]))(fo, pst, hoff),
                        reads=[bpst, self.bv["b1b"]], writes=[bhst])
                    if fo % 4 == 3:
                        grp = fo // 4
                        p.dma("gpsimd", hstslots[grp % 2],
                              s["h"][ti][:, (fo - 3) * TT:(fo + 1) * TT],
                              hst[:, (grp % 2) * 4 * TT:(grp % 2) * 4 * TT + 4 * TT],
                              reads=[bhst], writes=[bhd])

    def pass_attn_b_ffn2(self, dst):
        p, nc, i, s = self.p, self.nc, self.i, self.s
        import contextlib
        T, NT = self.T, self.NT
        self.barrier()
        with contextlib.ExitStack() as ctx:
            S = p.slot
            w2, bw2 = self.sb(ctx, "g2_w2", (128, (FFB // 128) * D), BF16)
            p.dma("gpsimd", S("w2b"), w2[:], i["w2b"].rearrange("c p f -> p c f"), writes=[bw2])
            hf, _bhf = self.sb(ctx, "g2_hf", (128, (FFB // 128) * TT), BF16)
            bhf = [p.buf(f"g2_hc{k}") for k in range(FFB // 128)]
            xo_sb, bxo = self.sb(ctx, "g2_xo", (128, DC * TT))
            x2_sb, bx2 = self.sb(ctx, "g2_x2", (128, DC * TT))
            xs, bxs = self.sb(ctx, "g2_xs", (128, DC * TT))
            xbf, bxbf = self.sb(ctx, "g2_xbf", (128, DC * TT), BF16)
            xsq, bxsq = self.sb(ctx, "g2_xsq", (128, DC * TT), BF16)
            mu, bmu = self.sb(ctx, "g2_mu", (128, TT))
            var, bvar = self.sb(ctx, "g2_var", (128, TT))
            rstd, brstd = self.sb(ctx, "g2_rstd", (128, TT))
            y3b, by3 = xbf, bxbf
            mxc, bmxc = self.sb(ctx, "g2_mx", (128, 1))
            ps4 = [self.ps(ctx, f"g2_ps{j}", (128, TT)) for j in range(4)]
            psS, bpsS = self.ps(ctx, "g2_psS", (128, 2 * TT))
            scr = (xbf, bxbf, xsq, bxsq, mu, bmu, var, bvar, rstd, brstd, psS, bpsS)
            sh, sxo, sx2 = S("g2_h"), S("g2_xo"), S("g2_x2")
            sx3, sy3 = S("g2_x3"), S("g2_y3")
            bx3d, by3d = p.buf(), p.buf()

            def epilogue(ti):
                p.dma("gpsimd", sx3, dst[ti], xs[:], reads=[bxs], writes=[bx3d])
                if self.n_sub >= 4:
                    self.emit_ln(ctx, lambda c: xs[:, c * TT:(c + 1) * TT], bxs, TT,
                                 out_f32=None, out_bf=(y3b, by3), scratch=scr)
                    p.dma("scalar", sy3, s["y3"][ti], y3b[:],
                          reads=[by3], writes=[by3d])
                    p.emit("vector", lambda e: e.reduce_max(out=mxc[:], in_=y3b[:], axis=AX.X),
                           reads=[by3], writes=[bmxc])
                    p.emit("vector", lambda e: e.tensor_max(
                        out=self.mloc[:, 2:3], in0=self.mloc[:, 2:3], in1=mxc[:]),
                        reads=[bmxc, self.bmloc], writes=[self.bmloc])

            for ti in range(NT):
                t0 = ti * TT
                p.dma("sync", sh, hf[:], s["h"][ti], writes=bhf)
                ftok = (sh, p.cnt[sh])
                for kc in range(FFB // 128):
                    bhf[kc].last_w = ftok
                p.dma("sync", sxo, xo_sb[:], s["xo"][ti], writes=[bxo])
                p.dma("sync", sx2, x2_sb[:], s["x2"][ti], writes=[bx2])
                if ti > 0:
                    epilogue(ti - 1)
                p.emit("gpsimd", lambda e: e.tensor_add(out=xs[:], in0=xo_sb[:], in1=x2_sb[:]),
                       reads=[bxo, bx2], writes=[bxs])
                for do in range(DC):
                    pst, bpst = ps4[do % 4]
                    for kc in range(FFB // 128):
                        p.emit("tensor", (lambda do_, kc_, pst_: lambda e: e.matmul(
                            pst_[:, 0:TT], w2[:, kc_ * D + 128 * do_: kc_ * D + 128 * do_ + 128],
                            hf[:, kc_ * TT:(kc_ + 1) * TT],
                            start=(kc_ == 0), stop=(kc_ == FFB // 128 - 1)))(do, kc, pst),
                            reads=[bw2, bhf[kc]], writes=[bpst])
                    p.emit("vector", (lambda do_, pst_: lambda e: e.scalar_tensor_tensor(
                        out=xs[:, do_ * TT:(do_ + 1) * TT], in0=pst_[:, 0:TT],
                        scalar=self.v["b2b"][:, do_:do_ + 1], in1=xs[:, do_ * TT:(do_ + 1) * TT],
                        op0=ALU.add, op1=ALU.add))(do, pst),
                        reads=[bpst, self.bv["b2b"], bxs], writes=[bxs])
            epilogue(NT - 1)



def build(T=T_CORE, n_sub=4, dbg=False):
    import contextlib
    kb = KB(T=T, n_sub=n_sub)
    kb.dbg = dbg
    p = kb.p
    with contextlib.ExitStack() as topctx:
        kb.setup(topctx)
        if n_sub >= 2:
            kb.pass_quantize_f()
        kb.pass_attn_a_front()
        kb.pass_attn_a_ffn(kb.o["xout"] if n_sub == 1 else kb.s["x1"])
        if n_sub >= 2:
            kb.barrier()
            kb.pass_sync_scales(0, 0)
            kb.pass_f_l1(kb.s["y1"], 0, 1)
            kb.barrier()
            kb.pass_sync_scales(1, 1)
            kb.pass_f_l2(kb.s["x1"], kb.o["xout"] if n_sub == 2 else kb.s["x2"], 1,
                         y_out=kb.s["y2"] if n_sub >= 3 else None)
            if n_sub >= 3:
                kb.pass_attn_b_front()
                kb.pass_attn_b_scan()
                kb.pass_attn_b_ffn1()
                kb.pass_attn_b_ffn2(kb.o["xout"] if n_sub == 3 else kb.s["x3"])
            if n_sub >= 4:
                kb.barrier()
                kb.pass_sync_scales(2, 2)
                kb.pass_f_l1(kb.s["y3"], 2, 3)
                kb.barrier()
                kb.pass_sync_scales(3, 3)
                kb.pass_f_l2(kb.s["x3"], kb.o["xout"], 3)
        p.realize()
    return kb


def kernel(**inputs):
    kb = build()
    in_maps = host_inputs(inputs)
    res = run_bass_kernel_spmd(kb.nc, in_maps, core_ids=list(range(NCORES)))
    return host_output(res.results)


# version 22
# speedup vs baseline: 1.0190x; 1.0190x over previous
import sys
import numpy as np
import ml_dtypes
import concourse.bass as bass
import concourse.mybir as mybir
from concourse.bass_utils import run_bass_kernel_spmd

F32 = mybir.dt.float32
BF16 = mybir.dt.bfloat16
AF = mybir.ActivationFunctionType
ALU = mybir.AluOpType
AX = mybir.AxisListType

NCORES = 8
BATCH = 2
SEQ = 16384
D = 1024
DC = 8
H = 16
DHD = 64
WIN = 128
FFA = 2048
FFB = 4096
FFQ = 4096
T_CORE = SEQ // 4
TT = 512
LN_EPS = 1e-5
M_SHIFT = -40.0

ENGINES = ("tensor", "vector", "scalar", "gpsimd", "sync")
SKIP_SAME_ENGINE = {"tensor"}


class Buf:
    __slots__ = ("name", "last_w", "readers", "is_psum")

    def __init__(self, name, is_psum=False):
        self.name = name
        self.last_w = None
        self.readers = []
        self.is_psum = is_psum


class Prog:

    def __init__(self, nc: bass.Bass):
        self.nc = nc
        self.sems = {}
        self.cnt = {}
        self.q = {e: [] for e in ENGINES}
        self.meta = {e: [] for e in ENGINES}
        self.seen = {e: {} for e in ENGINES}
        for e in ENGINES:
            self.sems[e] = nc.alloc_semaphore(f"sem_{e}")
            self.cnt[e] = 0
        self.nbuf = 0
        self.nslot = 0

    def buf(self, name=None):
        self.nbuf += 1
        return Buf(name or f"b{self.nbuf}")

    def slot(self, name=None):
        self.nslot += 1
        key = f"ds{self.nslot}_{name or ''}"
        self.sems[key] = self.nc.alloc_semaphore(key)
        self.cnt[key] = 0
        return key

    def emit(self, engine, fn, reads=(), writes=(), sem_key=None, dma=False,
             inc=None, extra_deps=()):
        deps = {}

        def add(tok):
            if tok is None:
                return
            s, v = tok
            if deps.get(s, -1) < v:
                deps[s] = v

        for b in reads:
            add(b.last_w)
            if b.is_psum:
                for t in b.readers:
                    add(t)
        for b in writes:
            add(b.last_w)
            for t in b.readers:
                add(t)
        for t in extra_deps:
            add(t)

        seen = self.seen[engine]
        waits = []
        for s, v in sorted(deps.items()):
            if s == engine and engine in SKIP_SAME_ENGINE:
                continue
            if seen.get(s, -1) >= v:
                continue
            waits.append((s, v))
            seen[s] = v

        if sem_key is None:
            assert not dma
            sem_key = engine
        if inc is None:
            inc = 16 if dma else 1
        if fn is None:
            inc = 0
        self.cnt[sem_key] += inc
        tok = (sem_key, self.cnt[sem_key]) if inc else None
        fr = sys._getframe(1)
        if fr.f_code.co_name in ("dma", "collective"):
            fr = fr.f_back
        self.meta[engine].append((fr.f_lineno, fr.f_code.co_name, len(waits)))
        self.q[engine].append((waits, fn, inc, sem_key))

        if tok is not None:
            for b in writes:
                b.last_w = tok
                b.readers = []
            for b in reads:
                b.readers.append(tok)
        return tok

    def dma(self, queue, slot, out_ap, in_ap, reads=(), writes=(), **kw):
        return self.emit(queue, lambda e: e.dma_start(out=out_ap, in_=in_ap, **kw),
                         reads=reads, writes=writes, sem_key=slot, dma=True)

    def collective(self, kind, op, groups, ins, outs, reads=(), writes=()):
        slot = self.slot("cc")
        return self.emit(
            "gpsimd",
            lambda e: e.collective_compute(kind, op, replica_groups=groups,
                                           ins=ins, outs=outs),
            reads=reads, writes=writes, sem_key=slot, inc=1)

    def barrier(self):
        toks = [(e, self.cnt[e]) for e in ENGINES if self.cnt[e] > 0]
        for e in ENGINES:
            self.emit(e, None, extra_deps=toks)

    def realize(self):
        finals = [(k, v) for k, v in self.cnt.items()
                  if k not in ENGINES and v > 0]
        for k, v in finals:
            if self.seen["sync"].get(k, -1) < v:
                self.q["sync"].append(([(k, v)], None, 0, "sync"))
                self.seen["sync"][k] = v
        nc = self.nc
        sems = self.sems
        with nc.Block() as block:
            def body(engine_name):
                def _(eng):
                    for waits, fn, inc, sem_key in self.q[engine_name]:
                        for s, v in waits:
                            eng.wait_ge(sems[s], v)
                        if fn is not None:
                            inst = fn(eng)
                            inst.then_inc(sems[sem_key], inc)
                return _
            block.tensor(body("tensor"))
            block.vector(body("vector"))
            block.scalar(body("scalar"))
            block.gpsimd(body("gpsimd"))
            block.sync(body("sync"))


def fap(handle, free_off, dims):
    ap = handle.ap()
    part = list(ap.ap[0])
    return bass.AP(ap.tensor, free_off, [part] + [list(d) for d in dims])



def _fm(w):
    d, f = w.shape
    return np.ascontiguousarray(w.reshape(d // 128, 128, f), dtype=np.float32)


def _col(v):
    n = v.shape[0]
    return np.ascontiguousarray(v.reshape(n // 128, 128).T, dtype=np.float32)


def host_inputs(inp, T=T_CORE):
    nslice = SEQ // T
    x = np.asarray(inp["L0_attn_a_x"], dtype=np.float32)

    wq_a = _fm(np.ascontiguousarray(np.transpose(inp["L0_attn_a_q_proj"], (1, 0, 2))).reshape(D, D))
    wk_a = _fm(np.ascontiguousarray(np.transpose(inp["L0_attn_a_k_proj"], (1, 0, 2))).reshape(D, D))
    wv_a = _fm(np.ascontiguousarray(np.transpose(inp["L0_attn_a_v_proj"], (1, 0, 2))).reshape(D, D))
    qb_a = _col(np.asarray(inp["L0_attn_a_q_bias"]).reshape(D))
    kb_a = _col(np.asarray(inp["L0_attn_a_k_bias"]).reshape(D))
    vb_a = _col(np.asarray(inp["L0_attn_a_v_bias"]).reshape(D))
    ln1g = _col(np.asarray(inp["L0_attn_a_ln1_g"]))
    ln1b = _col(np.asarray(inp["L0_attn_a_ln1_b"]))
    ln2g = _col(np.asarray(inp["L0_attn_a_ln2_g"]))
    ln2b = _col(np.asarray(inp["L0_attn_a_ln2_b"]))
    ff1w = _fm(np.asarray(inp["L0_attn_a_ff1_w"]))
    ff1b = _col(np.asarray(inp["L0_attn_a_ff1_b"]))
    ff2w = _fm(np.asarray(inp["L0_attn_a_ff2_w"]))
    ff2b = _col(np.asarray(inp["L0_attn_a_ff2_b"]))

    w1f = _fm(np.asarray(inp["L0_ffn_w1"]))
    b1f = _col(np.asarray(inp["L0_ffn_b1"]))
    w2f = _fm(np.asarray(inp["L0_ffn_w2"]))
    b2f = _col(np.asarray(inp["L0_ffn_b2"]))

    wq_b = _fm(np.asarray(inp["L0_attn_b_wq"]))
    wk_b = _fm(np.asarray(inp["L0_attn_b_wk"]))
    wv_b = _fm(np.asarray(inp["L0_attn_b_wv"]))
    wo_b = _fm(np.asarray(inp["L0_attn_b_wo"]))
    proj = np.asarray(inp["L0_attn_b_proj"], dtype=np.float32)
    projblk = np.zeros((128, 128), dtype=np.float32)
    projblk[0:64, 0:64] = proj
    projblk[64:128, 64:128] = proj
    g1 = _col(np.asarray(inp["L0_attn_b_g1"]))
    be1 = _col(np.asarray(inp["L0_attn_b_beta1"]))
    g2 = _col(np.asarray(inp["L0_attn_b_g2"]))
    be2 = _col(np.asarray(inp["L0_attn_b_beta2"]))
    w1b = _fm(np.asarray(inp["L0_attn_b_w1"]))
    b1b = _col(np.asarray(inp["L0_attn_b_b1"]))
    w2b = _fm(np.asarray(inp["L0_attn_b_w2"]))
    b2b = _col(np.asarray(inp["L0_attn_b_b2"]))

    i_idx = np.arange(WIN)[:, None]
    j_idx = np.arange(WIN)[None, :]
    mT_own = np.where(j_idx.T >= i_idx.T, 0.0, 0.0).astype(np.float32)
    mT_own = np.where(j_idx.reshape(-1, 1) <= i_idx.reshape(1, -1), 0.0, -1e9).astype(np.float32)
    mT_allow = np.zeros((WIN, WIN), dtype=np.float32)
    mT_deny = np.full((WIN, WIN), -1e9, dtype=np.float32)
    mb = (j_idx.reshape(-1, 1) <= i_idx.reshape(1, -1)).astype(np.float32)
    mTb4 = np.tile(mb, (1, 4)).reshape(128, 4, 128).transpose(0, 1, 2)
    mTb4 = np.ascontiguousarray(np.concatenate([mb] * 4, axis=1))

    ident = np.eye(128, dtype=np.float32)

    mT_own4 = np.ascontiguousarray(np.concatenate([mT_own] * 4, axis=1))
    shared = dict(
        wq_a=wq_a, wk_a=wk_a, wv_a=wv_a, qb_a=qb_a, kb_a=kb_a, vb_a=vb_a,
        ln1g=ln1g, ln1b=ln1b, ln2g=ln2g, ln2b=ln2b,
        ff1w=ff1w, ff1b=ff1b, ff2w=ff2w, ff2b=ff2b,
        w1f=w1f, b1f=b1f, w2f=w2f, b2f=b2f,
        wq_b=wq_b, wk_b=wk_b, wv_b=wv_b, wo_b=wo_b, projblk=projblk,
        g1=g1, be1=be1, g2=g2, be2=be2,
        w1b=w1b, b1b=b1b, w2b=w2b, b2b=b2b,
        mT_own4=mT_own4, mTb4=mTb4, ident=ident,
    )

    in_maps = []
    for c in range(NCORES):
        b = c // nslice
        s = c % nslice
        t0 = s * T
        xt = np.zeros((T + 128, D), dtype=np.float32)
        xt[128:] = x[b, t0:t0 + T]
        if s > 0:
            xt[:128] = x[b, t0 - 128:t0]
        nt = T // TT
        arr = xt[128:].T.reshape(DC, 128, nt, TT)
        x0t = np.ascontiguousarray(arr.transpose(2, 1, 0, 3).reshape(nt, 128, DC * TT))
        x0h = np.ascontiguousarray(xt[:128].T.reshape(DC, 128, 128).transpose(1, 0, 2).reshape(128, DC * 128))
        wpref = np.zeros((128, NCORES), dtype=np.float32)
        for r in range(NCORES):
            if r // nslice == b and r % nslice < s:
                wpref[:, r] = 1.0
        m = dict(shared)
        m["x0t"] = x0t
        m["x0h"] = x0h
        mp = mT_deny if s == 0 else mT_allow
        m["mT_prev04"] = np.ascontiguousarray(np.concatenate([mp] * 4, axis=1))
        m["wpref"] = wpref
        in_maps.append(m)
    return in_maps


def host_output(results, T=T_CORE):
    nslice = SEQ // T
    out = np.empty((BATCH, SEQ, D), dtype=np.float32)
    for c in range(NCORES):
        b = c // nslice
        s = c % nslice
        fm = results[c]["xout"]
        nt = T // TT
        dmt = fm.reshape(nt, 128, DC, TT).transpose(2, 1, 0, 3).reshape(D, T)
        out[b, s * T:(s + 1) * T, :] = dmt.T
    return out



class KB:
    def __init__(self, T=T_CORE, n_sub=4):
        self.T = T
        self.NT = T // TT
        self.n_sub = n_sub
        nc = bass.Bass("TRN2", target_bir_lowering=False)
        self.nc = nc
        self.p = Prog(nc)
        self.ctxs = []
        self.dbg = False
        self._declare_dram()
        self.sdbg1, self.sdbg2 = self.p.slot("dbg1"), self.p.slot("dbg2")
        self.bdbg1, self.bdbg2 = self.p.buf(), self.p.buf()

    def _declare_dram(self):
        nc, T = self.nc, self.T
        NT = T // TT

        def din(name, shape):
            return nc.dram_tensor(name, list(shape), F32, kind="ExternalInput").ap()

        i = {}
        i["x0t"] = din("x0t", (NT, 128, DC * TT))
        i["x0h"] = din("x0h", (128, DC * 128))
        for n, sh in [("wq_a", (DC, 128, D)), ("wk_a", (DC, 128, D)), ("wv_a", (DC, 128, D)),
                      ("qb_a", (128, DC)), ("kb_a", (128, DC)), ("vb_a", (128, DC)),
                      ("ln1g", (128, DC)), ("ln1b", (128, DC)), ("ln2g", (128, DC)), ("ln2b", (128, DC)),
                      ("ff1w", (DC, 128, FFA)), ("ff1b", (128, FFA // 128)),
                      ("ff2w", (FFA // 128, 128, D)), ("ff2b", (128, DC)),
                      ("w1f", (FFQ // 128, 128, D)), ("b1f", (128, FFQ // 128)),
                      ("w2f", (DC, 128, FFQ)), ("b2f", (128, DC)),
                      ("wq_b", (DC, 128, D)), ("wk_b", (DC, 128, D)), ("wv_b", (DC, 128, D)),
                      ("wo_b", (DC, 128, D)), ("projblk", (128, 128)),
                      ("g1", (128, DC)), ("be1", (128, DC)), ("g2", (128, DC)), ("be2", (128, DC)),
                      ("w1b", (DC, 128, FFB)), ("b1b", (128, FFB // 128)),
                      ("w2b", (FFB // 128, 128, D)), ("b2b", (128, DC)),
                      ("mT_own4", (128, 512)), ("mT_prev04", (128, 512)), ("mTb4", (128, 512)),
                      ("ident", (128, 128)), ("wpref", (128, NCORES))]:
            i[n] = din(n, sh)
        self.i = i

        def scr(name, shape, dt=F32):
            return nc.dram_tensor(name, list(shape), dt, kind="Internal").ap()

        s = {}
        s["x1"] = scr("x1s", (NT, 128, DC * TT))
        s["x2"] = scr("x2s", (NT, 128, DC * TT))
        s["x3"] = scr("x3s", (NT, 128, DC * TT))
        s["u"] = scr("us", (NT, 128, DC * TT))
        s["xo"] = scr("xos", (NT, 128, DC * TT))
        s["y1"] = scr("y1s", (NT, 128, DC * TT), BF16)
        s["y2"] = scr("y2s", (NT, 128, DC * TT), BF16)
        s["y3"] = scr("y3s", (NT, 128, DC * TT), BF16)
        s["h"] = scr("hs", (NT, 128, (FFQ // 128) * TT), BF16)
        s["qfm"] = scr("qfms", (NT, 128, DC * TT), BF16)
        s["kfm"] = scr("kfms", (NT, 128, DC * TT), BF16)
        s["ktm"] = scr("ktms", (NT, 128, (TT // 128) * D), BF16)
        s["vtm"] = scr("vtms", (NT, 128, (TT // 128) * D), BF16)
        s["m1T"] = scr("m1Ts", (128, DC, FFQ), BF16)
        s["m2T"] = scr("m2Ts", (128, FFQ // 128, D), BF16)
        for k in range(4):
            s[f"cin{k}"] = scr(f"cin{k}", (1, 1))
            s[f"cout{k}"] = scr(f"cout{k}", (1, 1))
        s["dy0"] = scr("dy0s", (DC, 128, T), BF16)
        s["dwin"] = scr("dwins", (4, 128, 1024), BF16)
        s["dexp"] = scr("dexps", (4, 8, 128, 512), BF16)
        s["dattn"] = scr("dattns", (DC, 128, T), BF16)
        s["gin"] = scr("gin", (64, 16 * 65))
        s["gout"] = scr("gout", (NCORES * 64, 16 * 65))
        self.s = s

        self.o = {"xout": nc.dram_tensor("xout", [NT, 128, DC * TT], F32, kind="ExternalOutput").ap()}

    _uid = [0]

    def sb(self, ctx, name, shape, dt=F32):
        self._uid[0] += 1
        h = ctx.enter_context(self.nc.sbuf_tensor(f"sb{self._uid[0]}_{name}", list(shape), dt))
        return h, self.p.buf(name)

    def ps(self, ctx, name, shape, dt=F32):
        self._uid[0] += 1
        h = ctx.enter_context(self.nc.psum_tensor(f"ps{self._uid[0]}_{name}", list(shape), dt))
        b = self.p.buf(name)
        b.is_psum = True
        return h, b

    def barrier(self):
        p = self.p
        toks = [(k, v) for k, v in p.cnt.items() if v > 0]
        for e in ENGINES:
            p.emit(e, None, extra_deps=toks)

    def setup(self, ctx):
        p, nc, i = self.p, self.nc, self.i
        S = p.slot
        cst, bcst = self.sb(ctx, "cst", (128, 8))
        self.cst, self.bcst = cst, bcst
        for col, val in [(0, M_SHIFT), (1, LN_EPS), (2, -0.5), (3, 0.125),
                         (4, 1.0 / D), (5, 1e-6), (6, 255.0), (7, 1e-8)]:
            p.emit("vector", (lambda c_, v_: lambda e: e.memset(cst[:, c_:c_ + 1], v_))(col, val),
                   writes=[bcst])
        self.c_m40 = cst[:, 0:1]
        self.c_eps = cst[:, 1:2]
        self.c_nh = cst[:, 2:3]
        self.c_8th = cst[:, 3:4]
        self.c_invD = cst[:, 4:5]
        self.c_1e6 = cst[:, 5:6]

        self.id_bf, self.bid_bf = self.sb(ctx, "id_bf", (128, 128), BF16)
        self.id_f, self.bid_f = self.sb(ctx, "id_f", (128, 128), F32)
        p.dma("gpsimd", S("idbf"), self.id_bf[:], i["ident"], writes=[self.bid_bf])
        p.dma("sync", S("idf"), self.id_f[:], i["ident"], writes=[self.bid_f])
        self.ones_bf, self.bones = self.sb(ctx, "ones_bf", (128, 128), BF16)
        p.emit("vector", lambda e: e.memset(self.ones_bf[:], 1.0), writes=[self.bones])
        self.ones_f, self.bones_f = self.sb(ctx, "ones_f", (1, 128))
        p.emit("vector", lambda e: e.memset(self.ones_f[:], 1.0), writes=[self.bones_f])

        self.v = {}
        self.bv = {}
        slot_v = S("vecs")
        vec_bufs = []
        for n in ["qb_a", "kb_a", "vb_a", "ln1g", "ln1b", "ln2g", "ln2b",
                  "ff1b", "ff2b", "b1f", "b2f", "g1", "be1", "g2", "be2",
                  "b1b", "b2b", "wpref"]:
            sh = tuple(i[n].shape)
            t, b = self.sb(ctx, "v_" + n, sh)
            self.v[n], self.bv[n] = t, b
            p.dma("sync", slot_v, t[:], i[n], writes=[b])
            vec_bufs.append(b)
        self.mT_own4, self.bmT_own4 = self.sb(ctx, "mT_own4", (128, 512))
        self.mT_prev04, self.bmT_prev04 = self.sb(ctx, "mT_prev04", (128, 512))
        self.mTb4, self.bmTb4 = self.sb(ctx, "mTb4", (128, 512), BF16)
        self.projblk, self.bprojblk = self.sb(ctx, "projblk", (128, 128), BF16)
        p.dma("sync", slot_v, self.mT_own4[:], i["mT_own4"], writes=[self.bmT_own4])
        p.dma("sync", slot_v, self.mT_prev04[:], i["mT_prev04"], writes=[self.bmT_prev04])
        vec_bufs += [self.bmT_own4, self.bmT_prev04]
        final_tok = ("ds" + slot_v.split("ds")[1], p.cnt[slot_v]) if False else (slot_v, p.cnt[slot_v])
        for b in vec_bufs:
            b.last_w = final_tok
        p.dma("gpsimd", S("mtb"), self.mTb4[:], i["mTb4"], writes=[self.bmTb4])
        p.dma("gpsimd", S("pblk"), self.projblk[:], i["projblk"], writes=[self.bprojblk])

        self.qb_s, self.bqb_s = self.sb(ctx, "qb_s", (128, DC))
        p.emit("vector", lambda e: e.tensor_scalar_mul(out=self.qb_s[:], in0=self.v["qb_a"][:], scalar1=0.125),
               reads=[self.bv["qb_a"]], writes=[self.bqb_s])

        self.sw1, self.bsw1 = self.sb(ctx, "sw1", (128, FFQ // 128))
        self.sw2, self.bsw2 = self.sb(ctx, "sw2", (128, DC))
        self.qcols, self.bqcols = self.sb(ctx, "qcols", (128, 8))
        self.zs1, self.bzs1 = self.sb(ctx, "zs1", (128, FFQ // 128))
        self.zs2, self.bzs2 = self.sb(ctx, "zs2", (128, DC))
        self.b1q, self.bb1q = self.sb(ctx, "b1q", (128, FFQ // 128))
        self.b2q, self.bb2q = self.sb(ctx, "b2q", (128, DC))
        self.mloc, self.bmloc = self.sb(ctx, "mloc", (128, 4))
        p.emit("vector", lambda e: e.memset(self.mloc[:], -1e30), writes=[self.bmloc])

    def emit_ln(self, ctx_tiles, xap_chunks, bx, n, gamma=None, beta=None,
                out_f32=None, out_bf=None, scratch=None):
        p = self.p
        (xbf, bxbf, xsq, bxsq, mu, bmu, var, bvar, rstd, brstd, psS, bpsS) = scratch
        for c in range(DC):
            p.emit("vector", (lambda c_: lambda e: e.tensor_copy(
                out=xbf[:, c_ * n:(c_ + 1) * n], in_=xap_chunks(c_)))(c),
                reads=[bx], writes=[bxbf])
        p.emit("vector", lambda e: e.tensor_mul(out=xsq[:, 0:DC * n], in0=xbf[:, 0:DC * n], in1=xbf[:, 0:DC * n]),
               reads=[bxbf], writes=[bxsq])
        for c in range(DC):
            p.emit("tensor", (lambda c_: lambda e: e.matmul(psS[:, 0:n], self.ones_bf[:], xbf[:, c_ * n:(c_ + 1) * n],
                                                            start=(c_ == 0), stop=(c_ == DC - 1)))(c),
                   reads=[self.bones, bxbf], writes=[bpsS])
        for c in range(DC):
            p.emit("tensor", (lambda c_: lambda e: e.matmul(psS[:, n:2 * n], self.ones_bf[:], xsq[:, c_ * n:(c_ + 1) * n],
                                                            start=(c_ == 0), stop=(c_ == DC - 1)))(c),
                   reads=[self.bones, bxsq], writes=[bpsS])
        p.emit("scalar", lambda e: e.activation(out=mu[:, 0:n], in_=psS[:, 0:n], func=AF.Identity, scale=self.c_invD),
               reads=[bpsS, self.bcst], writes=[bmu])
        p.emit("gpsimd", lambda e: e.tensor_mul(out=var[:, 0:n], in0=mu[:, 0:n], in1=mu[:, 0:n]),
               reads=[bmu], writes=[bvar])
        p.emit("vector", lambda e: e.scalar_tensor_tensor(out=var[:, 0:n], in0=psS[:, n:2 * n], scalar=1.0 / D,
                                                          in1=var[:, 0:n], op0=ALU.mult, op1=ALU.subtract),
               reads=[bpsS, bvar], writes=[bvar])
        p.emit("scalar", lambda e: e.activation(out=rstd[:, 0:n], in_=var[:, 0:n], func=AF.Ln, bias=self.c_eps),
               reads=[bvar, self.bcst], writes=[brstd])
        p.emit("scalar", lambda e: e.activation(out=rstd[:, 0:n], in_=rstd[:, 0:n], func=AF.Exp, scale=self.c_nh),
               reads=[brstd, self.bcst], writes=[brstd])
        p.emit("gpsimd", lambda e: e.tensor_mul(out=mu[:, 0:n], in0=mu[:, 0:n], in1=rstd[:, 0:n]),
               reads=[brstd, bmu], writes=[bmu])
        of = out_f32 if out_f32 is not None else out_bf
        ofh, bof = of
        eng_of = lambda c: "vector"
        for c in range(DC):
            p.emit(eng_of(c), (lambda c_: lambda e: e.tensor_mul(
                out=ofh[:, c_ * n:(c_ + 1) * n], in0=xap_chunks(c_), in1=rstd[:, 0:n]))(c),
                reads=[bx, brstd], writes=[bof])
        for c in range(DC):
            p.emit(eng_of(c), (lambda c_: lambda e: e.tensor_sub(
                out=ofh[:, c_ * n:(c_ + 1) * n], in0=ofh[:, c_ * n:(c_ + 1) * n], in1=mu[:, 0:n]))(c),
                reads=[bmu, bof], writes=[bof])
        if gamma is not None:
            for c in range(DC):
                p.emit(eng_of(c), (lambda c_: lambda e: e.tensor_scalar(
                    out=ofh[:, c_ * n:(c_ + 1) * n], in0=ofh[:, c_ * n:(c_ + 1) * n],
                    scalar1=gamma[:, c_:c_ + 1], scalar2=beta[:, c_:c_ + 1],
                    op0=ALU.mult, op1=ALU.add))(c),
                    reads=[bof], writes=[bof])
        if out_f32 is not None and out_bf is not None:
            obf, bobf = out_bf
            p.emit("vector", lambda e: e.tensor_copy(out=obf[:, 0:DC * n], in_=ofh[:, 0:DC * n]),
                   reads=[bof], writes=[bobf])

    def pass_attn_a_front(self):
        import contextlib
        p, nc, i, s = self.p, self.nc, self.i, self.s
        T, NT = self.T, self.NT
        self.barrier()
        with contextlib.ExitStack() as ctx:
            S = p.slot
            wq, bwq = self.sb(ctx, "a_wq", (128, DC * D), BF16)
            wk, bwk = self.sb(ctx, "a_wk", (128, DC * D), BF16)
            wv, bwv = self.sb(ctx, "a_wv", (128, DC * D), BF16)
            p.dma("gpsimd", S("wq"), wq[:], i["wq_a"].rearrange("c p f -> p c f"), writes=[bwq])
            p.dma("gpsimd", S("wk"), wk[:], i["wk_a"].rearrange("c p f -> p c f"), writes=[bwk])
            p.dma("gpsimd", S("wv"), wv[:], i["wv_a"].rearrange("c p f -> p c f"), writes=[bwv])

            xb = [self.sb(ctx, f"a_x{j}", (128, DC * TT)) for j in range(2)]
            xslots = [S(f"a_x{j}") for j in range(2)]
            xbfb = [self.sb(ctx, f"a_xbf{j}", (128, DC * TT), BF16) for j in range(2)]
            xsq, bxsq = self.sb(ctx, "a_xsq", (128, DC * TT), BF16)
            stats = [tuple(self.sb(ctx, f"a_st{j}{nm}", (128, TT)) for nm in ("mu", "var", "rstd"))
                     for j in range(2)]
            y0bfb = [self.sb(ctx, f"a_y0bf{j}", (128, DC * TT), BF16) for j in range(2)]
            qbfb = [self.sb(ctx, f"a_qbf{j}", (128, DC * TT), BF16) for j in range(2)]
            kbuf, bkbuf = self.sb(ctx, "a_kbuf", (128, DC * 640), BF16)
            vsb, bvsb = self.sb(ctx, "a_vsb", (128, 5 * H * 65), BF16)
            expT, bexpT = self.sb(ctx, "a_expT", (128, 4096), BF16)
            attn_tmb = [self.sb(ctx, f"a_attntm{j}", (128, 1024), BF16) for j in range(2)]
            attn_fmb = [self.sb(ctx, f"a_attnfm{j}", (128, DC * TT), BF16) for j in range(2)]
            den16b = [self.sb(ctx, f"a_den{j}", (128, H)) for j in range(2)]
            den16ib = [self.sb(ctx, f"a_deni{j}", (128, H)) for j in range(2)]

            psSd = [self.ps(ctx, f"a_psS{j}", (128, 2 * TT)) for j in range(2)]
            psS, bpsS = psSd[0]
            psP, bpsP = psSd[1]
            ps_a, bps_a = psP.ap()[:, 0:TT], bpsP
            ps_b, bps_b = psP.ap()[:, TT:2 * TT], bpsP
            ps_av = [self.ps(ctx, f"a_psav{g}", (128, 4 * 65)) for g in range(4)]

            su = S("a_u")
            budram = p.buf("a_udram")
            GROUPS = [(0, 2, 4, 6), (8, 10, 12, 14), (1, 3, 5, 7), (9, 11, 13, 15)]

            p.emit("vector", lambda e: e.memset(
                fap(vsb, 64, [[1040, 5], [65, H], [1, 1]]), 1.0),
                writes=[bvsb])

            def proj_k(blk0, ntok, n_ln, y0bf, by0bf):
                for oc in range(DC):
                    pst, bpst = (ps_a, bps_a) if oc % 2 == 0 else (ps_b, bps_b)
                    for c in range(DC):
                        p.emit("tensor", (lambda oc_, c_, pst_, yb2: lambda e: e.matmul(
                            pst_[:, 0:ntok], wk[:, c_ * D + 128 * oc_: c_ * D + 128 * oc_ + 128],
                            yb2[:, c_ * n_ln: c_ * n_ln + ntok],
                            start=(c_ == 0), stop=(c_ == DC - 1)))(oc, c, pst, y0bf),
                            reads=[bwk, by0bf], writes=[bpst])
                    p.emit("scalar", (lambda oc_, pst_: lambda e: e.activation(
                        out=kbuf[:, oc_ * 640 + 128 * blk0: oc_ * 640 + 128 * blk0 + ntok],
                        in_=pst_[:, 0:ntok], func=AF.Identity, bias=self.v["kb_a"][:, oc_:oc_ + 1]))(oc, pst),
                        reads=[bpst, self.bv["kb_a"]], writes=[bkbuf])

            def proj_v(nblk, blk0, n_ln, y0bf, by0bf):
                for b_ in range(nblk):
                    for half in range(2):
                        pst, bpst = (ps_a, bps_a) if half == 0 else (ps_b, bps_b)
                        for c in range(DC):
                            p.emit("tensor", (lambda b2, h2, c_, pst_, yb2: lambda e: e.matmul(
                                pst_[:, 0:512],
                                yb2[:, c_ * n_ln + 128 * b2: c_ * n_ln + 128 * b2 + 128],
                                wv[:, c_ * D + 512 * h2: c_ * D + 512 * h2 + 512],
                                start=(c_ == 0), stop=(c_ == DC - 1)))(b_, half, c, pst, y0bf),
                                reads=[bwv, by0bf], writes=[bpst])
                        p.emit("scalar", (lambda b2, h2, pst_: lambda e: e.copy(
                            out=fap(vsb, (blk0 + b2) * 1040 + h2 * 520,
                                    [[65, 8], [1, 64]]),
                            in_=pst_[:, 0:512]))(b_, half, pst),
                            reads=[bpst], writes=[bvsb])

            x_sb, bx = xb[0]
            xbf, bxbf = xbfb[0]
            (mu, bmu), (var, bvar), (rstd, brstd) = stats[0]
            y0bf, by0bf = y0bfb[0]
            scr = (xbf, bxbf, xsq, bxsq, mu, bmu, var, bvar, rstd, brstd, psS, bpsS)
            p.dma("sync", xslots[0], x_sb[:, 0:DC * 128], i["x0h"], writes=[bx])
            self.emit_ln(ctx, (lambda xs: lambda c: xs[:, c * 128:(c + 1) * 128])(x_sb), bx, 128,
                         out_f32=None, out_bf=(y0bf, by0bf), scratch=scr)
            proj_k(0, 128, 128, y0bf, by0bf)
            proj_v(1, 0, 128, y0bf, by0bf)

            prev_epi = None
            for ti in range(NT):
                t0 = ti * TT
                par = ti % 2
                x_sb, bx = xb[par]
                xbf, bxbf = xbfb[par]
                (mu, bmu), (var, bvar), (rstd, brstd) = stats[par]
                y0bf, by0bf = y0bfb[par]
                qbf, bqbf = qbfb[par]
                attn_fm, bfm = attn_fmb[par]
                scr = (xbf, bxbf, xsq, bxsq, mu, bmu, var, bvar, rstd, brstd, psS, bpsS)

                if ti > 0:
                    p.emit("vector", lambda e: e.tensor_copy(
                        out=fap(kbuf, 0, [[640, DC], [1, 128]]),
                        in_=fap(kbuf, 512, [[640, DC], [1, 128]])),
                        reads=[bkbuf], writes=[bkbuf])
                    p.emit("vector", lambda e: e.tensor_copy(
                        out=vsb[:, 0:1040], in_=vsb[:, 4 * 1040:5 * 1040]),
                        reads=[bvsb], writes=[bvsb])
                p.dma("sync", xslots[par], x_sb[:], i["x0t"][ti], writes=[bx])
                self.emit_ln(ctx, (lambda xs: lambda c: xs[:, c * TT:(c + 1) * TT])(x_sb), bx, TT,
                             out_f32=None, out_bf=(y0bf, by0bf), scratch=scr)
                for oc in range(DC):
                    pst, bpst = (ps_a, bps_a) if oc % 2 == 0 else (ps_b, bps_b)
                    for c in range(DC):
                        p.emit("tensor", (lambda oc_, c_, pst_, yb2: lambda e: e.matmul(
                            pst_[:, 0:TT], wq[:, c_ * D + 128 * oc_: c_ * D + 128 * oc_ + 128],
                            yb2[:, c_ * TT: c_ * TT + TT],
                            start=(c_ == 0), stop=(c_ == DC - 1)))(oc, c, pst, y0bf),
                            reads=[bwq, by0bf], writes=[bpst])
                    p.emit("scalar", (lambda oc_, pst_, qb2: lambda e: e.activation(
                        out=qb2[:, oc_ * TT: (oc_ + 1) * TT], in_=pst_[:, 0:TT], func=AF.Identity,
                        scale=self.c_8th, bias=self.qb_s[:, oc_:oc_ + 1]))(oc, pst, qbf),
                        reads=[bpst, self.bqb_s, self.bcst], writes=[bqbf])
                proj_k(1, TT, TT, y0bf, by0bf)
                proj_v(4, 1, TT, y0bf, by0bf)

                def sc_group(w, gi):
                    grp = GROUPS[gi]
                    psG, bpsG = psSd[gi % 2]
                    expG = expT[:, 1024 * gi:1024 * gi + 1024]
                    pr0 = 64 * (grp[0] % 2)
                    for idx, h in enumerate(grp):
                        for half in range(2):
                            p.emit("tensor", (lambda h_, pr0_, idx_, half_, w_, psG_, qb2: lambda e: e.matmul(
                                psG_[:, 256 * idx_ + 128 * half_: 256 * idx_ + 128 * half_ + 128],
                                kbuf[pr0_:pr0_ + 64, (h_ // 2) * 640 + 128 * (w_ + half_): (h_ // 2) * 640 + 128 * (w_ + half_) + 128],
                                qb2[pr0_:pr0_ + 64, (h_ // 2) * TT + 128 * w_: (h_ // 2) * TT + 128 * w_ + 128],
                                start=True, stop=True))(h, pr0, idx, half, w, psG, qbf),
                                reads=[bkbuf, bqbf], writes=[bpsG])
                    p.emit("vector", (lambda psG_: lambda e: e.tensor_add(
                        out=fap(psG_, 128, [[256, 4], [1, 128]]),
                        in0=fap(psG_, 128, [[256, 4], [1, 128]]),
                        in1=self.mT_own4[:].rearrange("p (a b) -> p a b", a=4)))(psG),
                        reads=[bpsG, self.bmT_own4], writes=[bpsG])
                    if w == 0 and ti == 0:
                        p.emit("vector", (lambda psG_: lambda e: e.tensor_add(
                            out=fap(psG_, 0, [[256, 4], [1, 128]]),
                            in0=fap(psG_, 0, [[256, 4], [1, 128]]),
                            in1=self.mT_prev04[:].rearrange("p (a b) -> p a b", a=4)))(psG),
                            reads=[bpsG, self.bmT_prev04], writes=[bpsG])
                    p.emit("scalar", (lambda psG_, eG: lambda e: e.activation(
                        out=eG, in_=psG_[:, 0:1024],
                        func=AF.Exp, bias=self.c_m40))(psG, expG),
                        reads=[bpsG, self.bcst], writes=[bexpT])

                def av_group(w, gi):
                    grp = GROUPS[gi]
                    expG = expT[:, 1024 * gi:1024 * gi + 1024]
                    for idx, h in enumerate(grp):
                        pg, bpg = ps_av[h // 4]
                        for half in range(2):
                            p.emit("tensor", (lambda h_, idx_, half_, pg_, w_, eG: lambda e: e.matmul(
                                pg_[:, 65 * (h_ % 4): 65 * (h_ % 4) + 65],
                                eG[:, 256 * idx_ + 128 * half_: 256 * idx_ + 128 * half_ + 128],
                                fap(vsb, (w_ + half_) * 1040 + 65 * h_, [[1, 65]]),
                                start=(half_ == 0), stop=(half_ == 1)))(h, idx, half, pg, w, expG),
                                reads=[bexpT, bvsb], writes=[bpg])

                def divisions(w):
                    den16, bden = den16b[w % 2]
                    den16i, bdeni = den16ib[w % 2]
                    attn_tm, btm = attn_tmb[w % 2]
                    for g in range(4):
                        pg, bpg = ps_av[g]
                        p.emit("vector", (lambda g_, pg_, d_: lambda e: e.tensor_scalar_add(
                            out=d_[:, 4 * g_:4 * g_ + 4],
                            in0=fap(pg_, 64, [[65, 4], [1, 1]]),
                            scalar1=0.0))(g, pg, den16),
                            reads=[bpg], writes=[bden])
                    p.emit("vector", (lambda d_, di_: lambda e: e.reciprocal(out=di_[:], in_=d_[:]))(den16, den16i),
                           reads=[bden], writes=[bdeni])
                    for h in range(H):
                        pg, bpg = ps_av[h // 4]
                        if h % 2 == 0:
                            p.emit("vector", (lambda h_, pg_, di_, tm_: lambda e: e.tensor_scalar_mul(
                                out=tm_[:, 64 * h_: 64 * h_ + 64],
                                in0=pg_[:, 65 * (h_ % 4): 65 * (h_ % 4) + 64],
                                scalar1=di_[:, h_:h_ + 1]))(h, pg, den16i, attn_tm),
                                reads=[bpg, bdeni], writes=[btm])
                        else:
                            p.emit("scalar", (lambda h_, pg_, di_, tm_: lambda e: e.mul(
                                out=tm_[:, 64 * h_: 64 * h_ + 64],
                                in_=pg_[:, 65 * (h_ % 4): 65 * (h_ % 4) + 64],
                                mul=di_[:, h_:h_ + 1]))(h, pg, den16i, attn_tm),
                                reads=[bpg, bdeni], writes=[btm])

                def transposes(w):
                    attn_tm, btm = attn_tmb[w % 2]
                    psW, bpsW = psSd[w % 2]
                    psW_bf = psW.ap().bitcast(BF16)
                    for c in range(DC):
                        p.emit("tensor", (lambda c_, pb, tm_: lambda e: e.transpose(
                            pb[:, 128 * c_:128 * c_ + 128],
                            tm_[:, 128 * c_:128 * c_ + 128], self.id_bf[:]))(c, psW_bf, attn_tm),
                            reads=[btm, self.bid_bf], writes=[bpsW])
                    for c in range(DC):
                        p.emit("scalar", (lambda w_, c_, pb, fm2: lambda e: e.activation(
                            out=fm2[:, c_ * TT + 128 * w_: c_ * TT + 128 * w_ + 128],
                            in_=pb[:, 128 * c_:128 * c_ + 128], func=AF.Identity,
                            bias=self.v["vb_a"][:, c_:c_ + 1]))(w, c, psW_bf, attn_fm),
                            reads=[bpsW, self.bv["vb_a"]], writes=[bfm])

                items = [(w, gi) for w in range(4) for gi in range(4)]
                for idx2, (w, gi) in enumerate(items):
                    sc_group(w, gi)
                    if idx2 == 3 and prev_epi is not None:
                        prev_epi()
                    if idx2 >= 2:
                        av_group(*items[idx2 - 2])
                    if gi == 1 and w > 0:
                        divisions(w - 1)
                    if gi == 2 and w > 0:
                        transposes(w - 1)
                av_group(*items[-2])
                av_group(*items[-1])
                divisions(3)
                transposes(3)

                def epi(ti_, par_):
                    x2, bx2 = xb[par_]
                    (mu2, bmu2), (v2, bv2), (r2, br2) = stats[par_]
                    fm2, bfm2 = attn_fmb[par_]
                    xbf2, bxbf2 = xbfb[par_]
                    scr2 = (xbf2, bxbf2, xsq, bxsq, mu2, bmu2, v2, bv2, r2, br2, psS, bpsS)
                    for c in range(DC):
                        p.emit("vector", (lambda c_, x3, r3: lambda e: e.tensor_mul(
                            out=x3[:, c_ * TT:(c_ + 1) * TT], in0=x3[:, c_ * TT:(c_ + 1) * TT],
                            in1=r3[:, 0:TT]))(c, x2, r2),
                            reads=[bx2, br2], writes=[bx2])
                    for c in range(DC):
                        p.emit("gpsimd" if c % 2 else "vector", (lambda c_, x3, m3: lambda e: e.tensor_sub(
                            out=x3[:, c_ * TT:(c_ + 1) * TT], in0=x3[:, c_ * TT:(c_ + 1) * TT],
                            in1=m3[:, 0:TT]))(c, x2, mu2),
                            reads=[bx2, bmu2], writes=[bx2])
                    p.emit("vector", (lambda x3, fm3: lambda e: e.tensor_add(
                        out=x3[:], in0=x3[:], in1=fm3[:]))(x2, fm2),
                        reads=[bx2, bfm2], writes=[bx2])
                    self.emit_ln(ctx, (lambda xs: lambda c: xs[:, c * TT:(c + 1) * TT])(x2), bx2, TT,
                                 gamma=self.v["ln1g"], beta=self.v["ln1b"],
                                 out_f32=(x2, bx2), out_bf=None, scratch=scr2)
                    p.dma("gpsimd", su, s["u"][ti_], x2[:],
                          reads=[bx2], writes=[budram])
                prev_epi = (lambda t_, p_: lambda: epi(t_, p_))(ti, par)
            prev_epi()

    def pass_attn_a_ffn(self, dst):
        p, nc, i, s = self.p, self.nc, self.i, self.s
        import contextlib
        T, NT = self.T, self.NT
        self.barrier()
        with contextlib.ExitStack() as ctx:
            S = p.slot
            f1w, bf1w = self.sb(ctx, "f1w", (128, DC * FFA), BF16)
            f2w, bf2w = self.sb(ctx, "f2w", (128, (FFA // 128) * D), BF16)
            p.dma("gpsimd", S("f1w"), f1w[:], i["ff1w"].rearrange("c p f -> p c f"), writes=[bf1w])
            p.dma("gpsimd", S("f2w"), f2w[:], i["ff2w"].rearrange("c p f -> p c f"), writes=[bf2w])

            ubuf = [self.sb(ctx, f"b_u{j}", (128, DC * TT)) for j in range(2)]
            uslots = [S(f"b_u{j}") for j in range(2)]
            x0_sb, bx0 = self.sb(ctx, "b_x0", (128, DC * TT))
            ffhb = [self.sb(ctx, f"b_ffh{j}", (128, (FFA // 128) * TT), BF16) for j in range(2)]
            w_f, bw_f = self.sb(ctx, "b_w", (128, DC * TT))
            xbf, bxbf = self.sb(ctx, "b_xbf", (128, DC * TT), BF16)
            ubf, bubf = self.sb(ctx, "b_ubf", (128, DC * TT), BF16)
            mu, bmu = self.sb(ctx, "b_mu", (128, TT))
            var, bvar = self.sb(ctx, "b_var", (128, TT))
            rstd, brstd = self.sb(ctx, "b_rstd", (128, TT))
            aout, baout = w_f, bw_f

            psS, bpsS = self.ps(ctx, "b_psS", (128, 2 * TT))
            ps4 = [self.ps(ctx, f"b_ps{j}", (128, TT)) for j in range(4)]
            sx0 = S("x0_in")
            sx1, sy1 = S("x1_out"), S("y1_out")
            bx1dram, by1dram = p.buf("x1dram"), p.buf("y1dram")

            def epilogue(ti, par):
                ffh_old, bffh_old = ffhb[par]
                scr = (xbf, bxbf, ffh_old, bffh_old, mu, bmu, var, bvar, rstd, brstd, psS, bpsS)
                x1, bx1 = ubuf[par]
                self.emit_ln(ctx, (lambda w2: lambda c: w2[:, c * TT:(c + 1) * TT])(w_f), bw_f, TT,
                             gamma=self.v["ln2g"], beta=self.v["ln2b"],
                             out_f32=(aout, baout), out_bf=None, scratch=scr)
                p.emit("gpsimd", (lambda x1_: lambda e: e.tensor_add(
                    out=x1_[:], in0=x0_sb[:], in1=aout[:]))(x1),
                    reads=[bx0, baout], writes=[bx1])
                p.dma("gpsimd", sx1, dst[ti], x1[:], reads=[bx1], writes=[bx1dram])
                self.emit_ln(ctx, (lambda x2: lambda c: x2[:, c * TT:(c + 1) * TT])(x1), bx1, TT,
                             out_f32=None, out_bf=(xbf, bxbf), scratch=scr)
                p.dma("scalar", sy1, s["y1"][ti], xbf[:], reads=[bxbf], writes=[by1dram])
                p.emit("vector", lambda e: e.reduce_max(out=var[:, 0:1], in_=xbf[:], axis=AX.X),
                       reads=[bxbf], writes=[bvar])
                p.emit("vector", lambda e: e.tensor_max(out=self.mloc[:, 0:1], in0=self.mloc[:, 0:1],
                                                        in1=var[:, 0:1]),
                       reads=[bvar, self.bmloc], writes=[self.bmloc])

            for ti in range(NT):
                t0 = ti * TT
                par = ti % 2
                u_sb, bu = ubuf[par]
                ffh, bffh = ffhb[par]
                p.dma("sync", uslots[par], u_sb[:], s["u"][ti], writes=[bu])
                p.emit("vector", (lambda u2: lambda e: e.tensor_copy(out=ubf[:], in_=u2[:]))(u_sb),
                       reads=[bu], writes=[bubf])
                for fo in range(FFA // 128):
                    pst, bpst = ps4[fo % 4]
                    for c in range(DC):
                        p.emit("tensor", (lambda fo_, c_, pst_: lambda e: e.matmul(
                            pst_[:, 0:TT], f1w[:, c_ * FFA + 128 * fo_: c_ * FFA + 128 * fo_ + 128],
                            ubf[:, c_ * TT:(c_ + 1) * TT],
                            start=(c_ == 0), stop=(c_ == DC - 1)))(fo, c, pst),
                            reads=[bf1w, bubf], writes=[bpst])
                    p.emit("scalar", (lambda fo_, pst_, f2: lambda e: e.activation(
                        out=f2[:, fo_ * TT:(fo_ + 1) * TT], in_=pst_[:, 0:TT], func=AF.Relu,
                        bias=self.v["ff1b"][:, fo_:fo_ + 1]))(fo, pst, ffh),
                        reads=[bpst, self.bv["ff1b"]], writes=[bffh])
                if ti > 0:
                    epilogue(ti - 1, 1 - par)
                p.dma("sync", sx0, x0_sb[:], i["x0t"][ti], writes=[bx0])
                for do in range(DC):
                    pst, bpst = ps4[do % 4]
                    for kc in range(FFA // 128):
                        p.emit("tensor", (lambda do_, kc_, pst_, f2: lambda e: e.matmul(
                            pst_[:, 0:TT], f2w[:, kc_ * D + 128 * do_: kc_ * D + 128 * do_ + 128],
                            f2[:, kc_ * TT:(kc_ + 1) * TT],
                            start=(kc_ == 0), stop=(kc_ == FFA // 128 - 1)))(do, kc, pst, ffh),
                            reads=[bf2w, bffh], writes=[bpst])
                    p.emit("vector", (lambda do_, pst_, u2: lambda e: e.scalar_tensor_tensor(
                        out=w_f[:, do_ * TT:(do_ + 1) * TT], in0=pst_[:, 0:TT],
                        scalar=self.v["ff2b"][:, do_:do_ + 1], in1=u2[:, do_ * TT:(do_ + 1) * TT],
                        op0=ALU.add, op1=ALU.add))(do, pst, u_sb),
                        reads=[bpst, self.bv["ff2b"], bu], writes=[bw_f])
            epilogue(NT - 1, (NT - 1) % 2)

    def pass_quantize_f(self):
        p, nc, i, s = self.p, self.nc, self.i, self.s
        import contextlib
        with contextlib.ExitStack() as ctx:
            S = p.slot
            wt = [self.sb(ctx, f"qw{j}", (128, FFQ)) for j in range(2)]
            wslots = [S(f"qw_in{j}") for j in range(2)]
            q1b_ = [self.sb(ctx, f"q_q1{j}", (128, FFQ)) for j in range(2)]
            qbfb_ = [self.sb(ctx, f"q_qbf{j}", (128, FFQ), BF16) for j in range(2)]
            tq, btq = self.sb(ctx, "q_tq", (128, FFQ), BF16)
            amaxb_ = [self.sb(ctx, f"q_amax{j}", (128, 2)) for j in range(2)]
            psQ, bpsQ = self.ps(ctx, "q_ps", (128, TT))
            psQ_bf = psQ.ap().bitcast(BF16)
            sq = S("m_out")
            bm1d, bm2d = p.buf("m1Td"), p.buf("m2Td")

            def prep1(fc):
                wts, bwts = wt[fc % 2]
                q1, bq1 = q1b_[fc % 2]
                qbf, bqbf = qbfb_[fc % 2]
                amax, bamax = amaxb_[fc % 2]
                p.dma("sync", wslots[fc % 2], wts[:, 0:D], i["w1f"][fc], writes=[bwts])
                p.emit("vector", (lambda wts_, am: lambda e: e.reduce_max(
                    out=am[:, 0:1], in_=wts_[:, 0:D], axis=AX.X, apply_absolute_value=True))(wts, amax),
                    reads=[bwts], writes=[bamax])
                p.emit("vector", (lambda fc_, am: lambda e: e.tensor_single_scalar(
                    out=self.sw1[:, fc_:fc_ + 1], in_=am[:, 0:1], scalar=1.0 / 127.0, op=ALU.mult))(fc, amax),
                    reads=[bamax], writes=[self.bsw1])
                p.emit("vector", (lambda fc_, am: lambda e: e.reciprocal(
                    out=am[:, 1:2], in_=self.sw1[:, fc_:fc_ + 1]))(fc, amax),
                    reads=[self.bsw1], writes=[bamax])
                p.emit("vector", (lambda wts_, am, q1_: lambda e: e.tensor_scalar(
                    out=q1_[:, 0:D], in0=wts_[:, 0:D], scalar1=am[:, 1:2], scalar2=2.0 ** 23,
                    op0=ALU.mult, op1=ALU.add))(wts, amax, q1),
                    reads=[bwts, bamax], writes=[bq1])
                p.emit("vector", (lambda q1_, qb_: lambda e: e.tensor_single_scalar(
                    out=qb_[:, 0:D], in_=q1_[:, 0:D], scalar=2.0 ** 23, op=ALU.subtract))(q1, qbf),
                    reads=[bq1], writes=[bqbf])

            prep1(0)
            for fc in range(FFQ // 128):
                qbf, bqbf = qbfb_[fc % 2]
                if fc + 1 < FFQ // 128:
                    prep1(fc + 1)
                for j in range(DC):
                    p.emit("tensor", (lambda j_, qb_: lambda e: e.transpose(
                        psQ_bf[:, 128 * j_:128 * j_ + 128], qb_[:, 128 * j_:128 * j_ + 128],
                        self.id_bf[:]))(j, qbf),
                        reads=[bqbf, self.bid_bf], writes=[bpsQ])
                p.emit("scalar", lambda e: e.copy(out=tq[:, 0:D], in_=psQ_bf[:, 0:D]),
                       reads=[bpsQ], writes=[btq])
                p.dma("scalar", sq, bass.AP(s["m1T"].tensor, 128 * fc,
                                            [[DC * FFQ, 128], [FFQ, DC], [1, 128]]),
                      tq[:, 0:DC * 128], reads=[btq], writes=[bm1d])
            def prep2(dc):
                wts, bwts = wt[dc % 2]
                q1, bq1 = q1b_[dc % 2]
                qbf, bqbf = qbfb_[dc % 2]
                amax, bamax = amaxb_[dc % 2]
                p.dma("sync", wslots[dc % 2], wts[:], i["w2f"][dc], writes=[bwts])
                p.emit("vector", (lambda wts_, am: lambda e: e.reduce_max(
                    out=am[:, 0:1], in_=wts_[:], axis=AX.X, apply_absolute_value=True))(wts, amax),
                    reads=[bwts], writes=[bamax])
                p.emit("vector", (lambda dc_, am: lambda e: e.tensor_single_scalar(
                    out=self.sw2[:, dc_:dc_ + 1], in_=am[:, 0:1], scalar=1.0 / 127.0, op=ALU.mult))(dc, amax),
                    reads=[bamax], writes=[self.bsw2])
                p.emit("vector", (lambda dc_, am: lambda e: e.reciprocal(
                    out=am[:, 1:2], in_=self.sw2[:, dc_:dc_ + 1]))(dc, amax),
                    reads=[self.bsw2], writes=[bamax])
                p.emit("vector", (lambda wts_, am, q1_: lambda e: e.tensor_scalar(
                    out=q1_[:], in0=wts_[:], scalar1=am[:, 1:2], scalar2=2.0 ** 23,
                    op0=ALU.mult, op1=ALU.add))(wts, amax, q1),
                    reads=[bwts, bamax], writes=[bq1])
                p.emit("vector", (lambda q1_, qb_: lambda e: e.tensor_single_scalar(
                    out=qb_[:], in_=q1_[:], scalar=2.0 ** 23, op=ALU.subtract))(q1, qbf),
                    reads=[bq1], writes=[bqbf])

            prep2(0)
            for dc in range(DC):
                qbf, bqbf = qbfb_[dc % 2]
                if dc + 1 < DC:
                    prep2(dc + 1)
                for grp in range(FFQ // TT):
                    for j in range(4):
                        p.emit("tensor", (lambda grp_, j_, qb_: lambda e: e.transpose(
                            psQ_bf[:, 128 * j_:128 * j_ + 128],
                            qb_[:, TT * grp_ + 128 * j_:TT * grp_ + 128 * j_ + 128],
                            self.id_bf[:]))(grp, j, qbf),
                            reads=[bqbf, self.bid_bf], writes=[bpsQ])
                    p.emit("scalar", (lambda grp_: lambda e: e.copy(
                        out=tq[:, TT * grp_:TT * grp_ + TT], in_=psQ_bf[:, 0:TT]))(grp),
                        reads=[bpsQ], writes=[btq])
                p.dma("scalar", sq, bass.AP(s["m2T"].tensor, 128 * dc,
                                            [[(FFQ // 128) * D, 128], [D, FFQ // 128], [1, 128]]),
                      tq[:], reads=[btq], writes=[bm2d])

    def pass_sync_scales(self, site, mcol, ctx=None):
        import contextlib
        p, nc, s = self.p, self.nc, self.s
        S = p.slot
        ctx = self._sync_ctx = contextlib.ExitStack()
        mx1, bmx1 = self.sb(ctx, f"mx1_{site}", (1, 2))
        mrow, bmrow = self.sb(ctx, f"mrow_{site}", (1, 128))
        psX, bpsX = self.ps(ctx, f"psx_{site}", (128, 64))
        bcin, bcout = p.buf(), p.buf()

        mbf, bmbf = self.sb(ctx, f"mbf_{site}", (128, 1), BF16)
        p.emit("vector", lambda e: e.tensor_copy(out=mbf[:], in_=self.mloc[:, mcol:mcol + 1]),
               reads=[self.bmloc], writes=[bmbf])
        p.emit("tensor", lambda e: e.transpose(psX.ap().bitcast(BF16)[0:1, 0:128], mbf[:], self.id_bf[:]),
               reads=[bmbf, self.bid_bf], writes=[bpsX])
        p.emit("scalar", lambda e: e.copy(out=mrow[:], in_=psX.ap().bitcast(BF16)[0:1, 0:128]),
               reads=[bpsX], writes=[bmrow])
        p.emit("vector", lambda e: e.reduce_max(out=mx1[:, 0:1], in_=mrow[:], axis=AX.X),
               reads=[bmrow], writes=[bmx1])
        p.dma("sync", S(f"ci{site}"), s[f"cin{site}"], mx1[:, 0:1], reads=[bmx1], writes=[bcin])
        p.collective("AllReduce", ALU.max, [list(range(NCORES))],
                     [s[f"cin{site}"].opt()], [s[f"cout{site}"].opt()],
                     reads=[bcin], writes=[bcout])
        p.dma("sync", S(f"co{site}"), mx1[:, 0:1], s[f"cout{site}"], reads=[bcout], writes=[bmx1])
        p.emit("vector", lambda e: e.tensor_scalar(out=mx1[:, 0:1], in0=mx1[:, 0:1],
                                                   scalar1=1e-8, scalar2=1.0 / 255.0,
                                                   op0=ALU.max, op1=ALU.mult),
               reads=[bmx1], writes=[bmx1])
        p.emit("vector", lambda e: e.reciprocal(out=mx1[:, 1:2], in_=mx1[:, 0:1]),
               reads=[bmx1], writes=[bmx1])
        p.emit("tensor", lambda e: e.matmul(psX[:, 0:2], self.ones_f[0:1, :], mx1[:, 0:2],
                                            start=True, stop=True),
               reads=[bmx1, self.bones_f], writes=[bpsX])
        p.emit("scalar", (lambda st: lambda e: e.copy(out=self.qcols[:, 2 * st:2 * st + 2],
                                                      in_=psX[:, 0:2]))(site),
               reads=[bpsX], writes=[self.bqcols])
        scol = self.qcols[:, 2 * site:2 * site + 1]
        if site % 2 == 0:
            zst, bzst, bt, bbt, nb = self.zs1, self.bzs1, self.v["b1f"], self.bv["b1f"], FFQ // 128
            bqt, bbqt = self.b1q, self.bb1q
        else:
            zst, bzst, bt, bbt, nb = self.zs2, self.bzs2, self.v["b2f"], self.bv["b2f"], DC
            bqt, bbqt = self.b2q, self.bb2q
        p.emit("vector", (lambda: lambda e: e.tensor_scalar_mul(
            out=zst[:], in0=self.bsw_t(site)[0][:], scalar1=scol))(),
            reads=[self.bsw_t(site)[1], self.bqcols], writes=[bzst])
        zr, bzr = self.sb(ctx, f"zr_{site}", (128, zst.ap().shape[1]))
        p.emit("vector", lambda e: e.reciprocal(out=zr[:], in_=zst[:]),
               reads=[bzst], writes=[bzr])
        p.emit("vector", lambda e: e.tensor_mul(out=bqt[:], in0=bt[:], in1=zr[:]),
               reads=[bbt, bzr], writes=[bbqt])
        p.emit("vector", lambda e: e.tensor_scalar(out=bqt[:], in0=bqt[:], scalar1=2.0 ** 23,
                                                   scalar2=2.0 ** 23, op0=ALU.add, op1=ALU.subtract),
               reads=[bbqt], writes=[bbqt])
        p.emit("vector", lambda e: e.tensor_mul(out=bqt[:], in0=bqt[:], in1=zst[:]),
               reads=[bbqt, bzst], writes=[bbqt])
        ctx.close()

    def bsw_t(self, site):
        return (self.sw1, self.bsw1) if site % 2 == 0 else (self.sw2, self.bsw2)

    def pass_f_l1(self, y_dram, site, mcol):
        p, nc, s = self.p, self.nc, self.s
        import contextlib
        T, NT = self.T, self.NT
        self.barrier()
        with contextlib.ExitStack() as ctx:
            S = p.slot
            m1t, bm1t = self.sb(ctx, "l1_m1t", (128, DC * FFQ), BF16)
            p.dma("sync", S("m1t"), m1t[:], s["m1T"], writes=[bm1t])
            ybuf = [self.sb(ctx, f"l1_y{j}", (128, DC * TT), BF16) for j in range(2)]
            yslots = [S(f"l1_y{j}") for j in range(2)]
            q1b = [self.sb(ctx, f"l1_q1{j}", (128, DC * TT)) for j in range(2)]
            n1b = [self.sb(ctx, f"l1_n1{j}", (128, DC * TT), BF16) for j in range(2)]
            hst, bhst = self.sb(ctx, "l1_hst", (128, 8 * TT), BF16)
            hstslots = [S(f"l1_hs{j}") for j in range(2)]
            mxc, bmxc = self.sb(ctx, "l1_mx", (128, FFQ // 128 + 1))
            ps4 = [self.ps(ctx, f"l1_ps{j}", (128, TT)) for j in range(4)]
            bhd = p.buf("l1_hd")
            rscol = self.qcols[:, 2 * site + 1:2 * site + 2]

            pend = []

            def prep(tj):
                y_sb, by = ybuf[tj % 2]
                q1, bq1 = q1b[tj % 2]
                n1, bn1 = n1b[tj % 2]
                p.dma("sync", yslots[tj % 2], y_sb[:], y_dram[tj], writes=[by])
                p.emit("vector", (lambda y2_, q2_: lambda e: e.tensor_scalar(
                    out=q2_[:], in0=y2_[:], scalar1=rscol, scalar2=2.0 ** 23,
                    op0=ALU.mult, op1=ALU.add))(y_sb, q1),
                    reads=[by, self.bqcols], writes=[bq1])
                p.emit("vector", (lambda q2_, n2_: lambda e: e.tensor_scalar(
                    out=n2_[:], in0=q2_[:], scalar1=2.0 ** 23, scalar2=0.0,
                    op0=ALU.subtract, op1=ALU.max))(q1, n1), reads=[bq1], writes=[bn1])

            prep(0)
            for ti in range(NT):
                t0 = ti * TT
                par = ti % 2
                n1, bn1 = n1b[par]
                if ti + 1 < NT:
                    prep(ti + 1)
                for fn in pend:
                    fn()
                pend = []
                for fo in range(FFQ // 128):
                    pst, bpst = ps4[fo % 4]
                    for c in range(DC):
                        p.emit("tensor", (lambda fo_, c_, pst_, n2_: lambda e: e.matmul(
                            pst_[:, 0:TT], m1t[:, c_ * FFQ + 128 * fo_: c_ * FFQ + 128 * fo_ + 128],
                            n2_[:, c_ * TT:(c_ + 1) * TT],
                            start=(c_ == 0), stop=(c_ == DC - 1)))(fo, c, pst, n1),
                            reads=[bm1t, bn1], writes=[bpst])
                    hoff = (fo % 4) * TT + ((fo // 4) % 2) * 4 * TT
                    p.emit("scalar", (lambda fo_, pst_, ho: lambda e: e.activation(
                        out=hst[:, ho:ho + TT], in_=pst_[:, 0:TT],
                        func=AF.Relu, scale=self.zs1[:, fo_:fo_ + 1],
                        bias=self.b1q[:, fo_:fo_ + 1]))(fo, pst, hoff),
                        reads=[bpst, self.bzs1, self.bb1q], writes=[bhst])
                    if fo % 4 == 3:
                        grp = fo // 4
                        p.emit("vector", (lambda g_: lambda e: e.reduce_max(
                            out=mxc[:, g_:g_ + 1], in_=hst[:, (g_ % 2) * 4 * TT:(g_ % 2) * 4 * TT + 4 * TT],
                            axis=AX.X))(grp),
                            reads=[bhst], writes=[bmxc])
                        p.dma("gpsimd", hstslots[grp % 2],
                              s["h"][ti][:, (fo - 3) * TT:(fo + 1) * TT],
                              hst[:, (grp % 2) * 4 * TT:(grp % 2) * 4 * TT + 4 * TT],
                              reads=[bhst], writes=[bhd])
                pend.append(lambda: p.emit("vector", lambda e: e.reduce_max(
                    out=mxc[:, FFQ // 128:FFQ // 128 + 1], in_=mxc[:, 0:FFQ // 512], axis=AX.X),
                    reads=[bmxc], writes=[bmxc]))
                pend.append(lambda: p.emit("vector", lambda e: e.tensor_max(
                    out=self.mloc[:, mcol:mcol + 1], in0=self.mloc[:, mcol:mcol + 1],
                    in1=mxc[:, FFQ // 128:FFQ // 128 + 1]),
                    reads=[bmxc, self.bmloc], writes=[self.bmloc]))
            for fn in pend:
                fn()

    def pass_f_l2(self, x_dram, dst_dram, site, y_out=None, mcol=None):
        p, nc, s = self.p, self.nc, self.s
        import contextlib
        T, NT = self.T, self.NT
        self.barrier()
        with contextlib.ExitStack() as ctx:
            S = p.slot
            m2t, bm2t = self.sb(ctx, "l2_m2t", (128, (FFQ // 128) * D), BF16)
            p.dma("sync", S("m2t"), m2t[:], s["m2T"], writes=[bm2t])
            hf, bhfl = self.sb(ctx, "l2_hf", (128, (FFQ // 256) * TT), BF16)
            hfslot = S("l2_hf")
            q1, bq1 = self.sb(ctx, "l2_q1", (128, 4 * TT))
            n2full, _bn2 = self.sb(ctx, "l2_n2", (128, (FFQ // 128) * TT), BF16)
            bn2 = [p.buf(f"l2_n2c{k}") for k in range(FFQ // 128)]
            xb2 = [self.sb(ctx, f"l2_x{j}", (128, DC * TT)) for j in range(2)]
            xslots2 = [S(f"l2_x{j}") for j in range(2)]
            xn, bxn = self.sb(ctx, "l2_xn", (128, DC * TT))
            xbf, bxbf = self.sb(ctx, "l2_xbf", (128, DC * TT), BF16)
            xsq, bxsq = self.sb(ctx, "l2_xsq", (128, DC * TT), BF16)
            mu, bmu = self.sb(ctx, "l2_mu", (128, TT))
            var, bvar = self.sb(ctx, "l2_var", (128, TT))
            rstd, brstd = self.sb(ctx, "l2_rstd", (128, TT))
            yb, byb = xbf, bxbf
            mxc, bmxc = self.sb(ctx, "l2_mx", (128, 1))
            ps4 = [self.ps(ctx, f"l2_ps{j}", (128, TT)) for j in range(4)]
            psS, bpsS = self.ps(ctx, "l2_psS", (128, 2 * TT))
            scr = (xbf, bxbf, xsq, bxsq, mu, bmu, var, bvar, rstd, brstd, psS, bpsS)
            sxd, syd = S("l2_xd"), S("l2_yd")
            bxd, byd = p.buf("l2_xd"), p.buf("l2_yd")
            rscol = self.qcols[:, 2 * site + 1:2 * site + 2]

            def epilogue(ti, par):
                x_sb, bxs = xb2[par]
                p.emit("vector", (lambda x2: lambda e: e.tensor_add(
                    out=xn[:], in0=xn[:], in1=x2[:]))(x_sb),
                    reads=[bxs, bxn], writes=[bxn])
                p.dma("gpsimd", sxd, dst_dram[ti], xn[:],
                      reads=[bxn], writes=[bxd])
                if y_out is not None:
                    self.emit_ln(ctx, lambda c: xn[:, c * TT:(c + 1) * TT], bxn, TT,
                                 out_f32=None, out_bf=(yb, byb), scratch=scr)
                    p.dma("scalar", syd, y_out[ti], yb[:],
                          reads=[byb], writes=[byd])
                    if mcol is not None:
                        p.emit("vector", lambda e: e.reduce_max(out=mxc[:], in_=yb[:], axis=AX.X),
                               reads=[byb], writes=[bmxc])
                        p.emit("vector", lambda e: e.tensor_max(
                            out=self.mloc[:, mcol:mcol + 1], in0=self.mloc[:, mcol:mcol + 1], in1=mxc[:]),
                            reads=[bmxc, self.bmloc], writes=[self.bmloc])

            for ti in range(NT):
                t0 = ti * TT
                par = ti % 2
                x_sb, bxs = xb2[par]
                p.dma("sync", xslots2[par], x_sb[:], x_dram[ti], writes=[bxs])
                NH = FFQ // 256
                for half in range(2):
                    p.dma("sync", hfslot, hf[:],
                          s["h"][ti][:, half * NH * TT:(half + 1) * NH * TT],
                          writes=[bhfl])
                    for j4 in range(NH // 4):
                        kc0 = half * NH + j4 * 4
                        p.emit("vector", (lambda j4_: lambda e: e.tensor_scalar(
                            out=q1[:], in0=hf[:, j4_ * 4 * TT:(j4_ + 1) * 4 * TT], scalar1=rscol,
                            scalar2=2.0 ** 23, op0=ALU.mult, op1=ALU.add))(j4),
                            reads=[bhfl, self.bqcols], writes=[bq1])
                        p.emit("vector", (lambda kc0_: lambda e: e.tensor_single_scalar(
                            out=n2full[:, kc0_ * TT:(kc0_ + 4) * TT], in_=q1[:],
                            scalar=2.0 ** 23, op=ALU.subtract))(kc0),
                            reads=[bq1], writes=[bn2[kc0], bn2[kc0 + 1], bn2[kc0 + 2], bn2[kc0 + 3]])
                if ti > 0:
                    epilogue(ti - 1, 1 - par)
                for do in range(DC):
                    pst, bpst = ps4[do % 4]
                    for kc in range(FFQ // 128):
                        p.emit("tensor", (lambda do_, kc_, pst_: lambda e: e.matmul(
                            pst_[:, 0:TT], m2t[:, kc_ * D + 128 * do_: kc_ * D + 128 * do_ + 128],
                            n2full[:, kc_ * TT:(kc_ + 1) * TT],
                            start=(kc_ == 0), stop=(kc_ == FFQ // 128 - 1)))(do, kc, pst),
                            reads=[bm2t, bn2[kc]], writes=[bpst])
                    p.emit("scalar", (lambda do_, pst_: lambda e: e.activation(
                        out=xn[:, do_ * TT:(do_ + 1) * TT], in_=pst_[:, 0:TT], func=AF.Identity,
                        scale=self.zs2[:, do_:do_ + 1], bias=self.b2q[:, do_:do_ + 1]))(do, pst),
                        reads=[bpst, self.bzs2, self.bb2q], writes=[bxn])
            epilogue(NT - 1, (NT - 1) % 2)

    def pass_attn_b_front(self):
        p, nc, i, s = self.p, self.nc, self.i, self.s
        import contextlib
        T, NT = self.T, self.NT
        self.barrier()
        with contextlib.ExitStack() as ctx:
            S = p.slot
            wq, bwq = self.sb(ctx, "bf_wq", (128, DC * D), BF16)
            wk, bwk = self.sb(ctx, "bf_wk", (128, DC * D), BF16)
            wv, bwv = self.sb(ctx, "bf_wv", (128, DC * D), BF16)
            p.dma("gpsimd", S("bwq"), wq[:], i["wq_b"].rearrange("c p f -> p c f"), writes=[bwq])
            p.dma("gpsimd", S("bwk"), wk[:], i["wk_b"].rearrange("c p f -> p c f"), writes=[bwk])
            p.dma("gpsimd", S("bwv"), wv[:], i["wv_b"].rearrange("c p f -> p c f"), writes=[bwv])

            ybP = [self.sb(ctx, f"bf_y{j}", (128, DC * TT), BF16) for j in range(2)]
            yslotsP = [S(f"bf_y{j}") for j in range(2)]
            xbf, bxbf = self.sb(ctx, "bf_xbf", (128, DC * TT), BF16)
            xsq, bxsq = self.sb(ctx, "bf_xsq", (128, DC * TT), BF16)
            mu, bmu = self.sb(ctx, "bf_mu", (128, TT))
            var, bvar = self.sb(ctx, "bf_var", (128, TT))
            rstd, brstd = self.sb(ctx, "bf_rstd", (128, TT))
            hbP = [self.sb(ctx, f"bf_hb{j}", (128, DC * TT), BF16) for j in range(2)]
            qt, bqt = self.sb(ctx, "bf_qt", (128, DC * TT), BF16)
            kt, bkt = self.sb(ctx, "bf_kt", (128, DC * TT), BF16)
            qf, bqf = self.sb(ctx, "bf_qf", (128, DC * TT), BF16)
            kf, bkf = self.sb(ctx, "bf_kf", (128, DC * TT), BF16)
            rb, brb = self.sb(ctx, "bf_rb", (128, TT), BF16)
            eb, beb = self.sb(ctx, "bf_eb", (128, TT), BF16)
            vt, bvt = self.sb(ctx, "bf_vt", (128, (TT // 128) * D), BF16)
            ktm_sb, bktm = self.sb(ctx, "bf_ktm", (128, (TT // 128) * D), BF16)
            stA, bstA = self.sb(ctx, "bf_stA", (64, 16 * 65))

            ps_a, bps_a = self.ps(ctx, "bf_psA", (128, TT))
            ps_b, bps_b = self.ps(ctx, "bf_psB", (128, TT))
            psS, bpsS = self.ps(ctx, "bf_psS", (128, 2 * TT))
            psKV, bpsKV = self.ps(ctx, "bf_psKV", (64, 1024))
            psKS, bpsKS = self.ps(ctx, "bf_psKS", (64, 16))
            scr = (xbf, bxbf, xsq, bxsq, mu, bmu, var, bvar, rstd, brstd, psS, bpsS)
            sq, sk, sktm, svt = S("bf_q"), S("bf_k"), S("bf_ktm"), S("bf_vt")
            bqd, bkd, bktd, bvd = p.buf(), p.buf(), p.buf(), p.buf()
            sgin = S("bf_gin")
            bgin = p.buf()
            NBLK = TT // 128


            CUTB = getattr(self, "cutb", 99)

            def prep(tj):
                y_sb, by = ybP[tj % 2]
                hb_, bhb_ = hbP[tj % 2]
                p.dma("sync", yslotsP[tj % 2], y_sb[:], s["y2"][tj], writes=[by])
                self.emit_ln(ctx, (lambda y2_: lambda c: y2_[:, c * TT:(c + 1) * TT])(y_sb), by, TT,
                             gamma=self.v["g1"], beta=self.v["be1"],
                             out_f32=None, out_bf=(hb_, bhb_), scratch=scr)

            prep(0)
            for ti in range(NT):
                t0 = ti * TT
                hb, bhb = hbP[ti % 2]
                if ti + 1 < NT:
                    prep(ti + 1)
                if CUTB < 2:
                    continue
                for which, wsb, bwsb, dst, bdst in (("q", wq, bwq, qt, bqt), ("k", wk, bwk, kt, bkt)):
                    for oc in range(DC):
                        pst, bpst = (ps_a, bps_a) if oc % 2 == 0 else (ps_b, bps_b)
                        for c in range(DC):
                            p.emit("tensor", (lambda oc_, c_, pst_, wsb_, hb_: lambda e: e.matmul(
                                pst_[:, 0:TT], wsb_[:, c_ * D + 128 * oc_: c_ * D + 128 * oc_ + 128],
                                hb_[:, c_ * TT:(c_ + 1) * TT],
                                start=(c_ == 0), stop=(c_ == DC - 1)))(oc, c, pst, wsb, hb),
                                reads=[bwsb, bhb], writes=[bpst])
                        p.emit("scalar", (lambda oc_, pst_, dst_: lambda e: e.copy(
                            out=dst_[:, oc_ * TT:(oc_ + 1) * TT], in_=pst_[:, 0:TT]))(oc, pst, dst),
                            reads=[bpst], writes=[bdst])
                if CUTB < 3:
                    continue
                for which, src_t, bsrc, dst, bdst, sd, bdd in (
                        ("q", qt, bqt, qf, bqf, sq, bqd), ("k", kt, bkt, kf, bkf, sk, bkd)):
                    for c in range(DC):
                        pst, bpst = (ps_a, bps_a) if c % 2 == 0 else (ps_b, bps_b)
                        p.emit("tensor", (lambda c_, pst_, src2: lambda e: e.matmul(
                            pst_[:, 0:TT], self.projblk[:], src2[:, c_ * TT:(c_ + 1) * TT],
                            start=True, stop=True))(c, pst, src_t),
                            reads=[self.bprojblk, bsrc], writes=[bpst])
                        if not getattr(self, "skip_exp", False):
                            p.emit("scalar", (lambda pst_: lambda e: e.activation(
                                out=eb[:], in_=pst_[:, 0:TT], func=AF.Exp))(pst),
                                reads=[bpst], writes=[beb])
                        if not getattr(self, "skip_relu", False):
                            p.emit("vector", (lambda pst_: lambda e: e.tensor_single_scalar(
                                out=rb[:], in_=pst_[:, 0:TT], scalar=0.0, op=ALU.max))(pst),
                                reads=[bpst], writes=[brb])
                        if not getattr(self, "skip_stt", False):
                            p.emit("vector", (lambda c_, dst_: lambda e: e.scalar_tensor_tensor(
                                out=dst_[:, c_ * TT:(c_ + 1) * TT], in0=eb[:], scalar=1.0, in1=rb[:],
                                op0=ALU.min, op1=ALU.add))(c, dst),
                                reads=[beb, brb], writes=[bdst])
                    if not getattr(self, "skip_spill", False):
                        p.dma("scalar", sd, (s["qfm"] if which == "q" else s["kfm"])[ti],
                              dst[:], reads=[bdst], writes=[bdd])
                if CUTB < 4:
                    continue
                for b_ in range(NBLK):
                    for half in range(2):
                        pst, bpst = (ps_a, bps_a) if half == 0 else (ps_b, bps_b)
                        for c in range(DC):
                            p.emit("tensor", (lambda b2, h2, c_, pst_, hb_: lambda e: e.matmul(
                                pst_[:, 0:512],
                                hb_[:, c_ * TT + 128 * b2: c_ * TT + 128 * b2 + 128],
                                wv[:, c_ * D + 512 * h2: c_ * D + 512 * h2 + 512],
                                start=(c_ == 0), stop=(c_ == DC - 1)))(b_, half, c, pst, hb),
                                reads=[bwv, bhb], writes=[bpst])
                        p.emit("scalar", (lambda b2, h2, pst_: lambda e: e.copy(
                            out=vt[:, b2 * D + 512 * h2: b2 * D + 512 * h2 + 512],
                            in_=pst_[:, 0:512]))(b_, half, pst),
                            reads=[bpst], writes=[bvt])
                p.dma("scalar", svt, s["vtm"][ti], vt[:], reads=[bvt], writes=[bvd])
                if CUTB < 5:
                    continue
                for b_ in range(NBLK):
                    for c in range(DC):
                        p.emit("tensor", (lambda b2, c_: lambda e: e.transpose(
                            psS.ap().bitcast(BF16)[:, 128 * c_:128 * c_ + 128],
                            kf[:, c_ * TT + 128 * b2: c_ * TT + 128 * b2 + 128],
                            self.id_bf[:]))(b_, c),
                            reads=[bkf, self.bid_bf], writes=[bpsS])
                    p.emit("scalar", (lambda b2: lambda e: e.copy(
                        out=ktm_sb[:, b2 * D:(b2 + 1) * D], in_=psS.ap().bitcast(BF16)[:, 0:D]))(b_),
                        reads=[bpsS], writes=[bktm])
                    first_blk = (ti == 0 and b_ == 0)
                    last_blk = (ti == NT - 1 and b_ == NBLK - 1)
                    for h in range(H):
                        kv_start = first_blk and h in (0, 8)
                        kv_stop = last_blk and h in (7, 15)
                        p.emit("tensor", (lambda b2, h_, st_, sp_: lambda e: e.matmul(
                            psKV[:, 64 * h_:64 * h_ + 64],
                            ktm_sb[:, b2 * D + 64 * h_: b2 * D + 64 * h_ + 64],
                            vt[:, b2 * D + 64 * h_: b2 * D + 64 * h_ + 64],
                            start=st_, stop=sp_))(b_, h, kv_start, kv_stop),
                            reads=[bktm, bvt], writes=[bpsKV])
                        p.emit("tensor", (lambda b2, h_, st_, sp_: lambda e: e.matmul(
                            psKS[:, h_:h_ + 1],
                            ktm_sb[:, b2 * D + 64 * h_: b2 * D + 64 * h_ + 64],
                            self.ones_bf[:, 0:1],
                            start=st_, stop=sp_))(b_, h, first_blk and h == 0, last_blk and h == 15),
                            reads=[bktm, self.bones], writes=[bpsKS])
                p.dma("scalar", sktm, s["ktm"][ti], ktm_sb[:], reads=[bktm], writes=[bktd])

            if CUTB < 6:
                return
            p.emit("scalar", lambda e: e.copy(
                out=fap(stA, 0, [[65, 8], [1, 64]]),
                in_=fap(psKV, 0, [[128, 8], [1, 64]])),
                reads=[bpsKV], writes=[bstA])
            p.emit("scalar", lambda e: e.copy(
                out=fap(stA, 520, [[65, 8], [1, 64]]),
                in_=fap(psKV, 64, [[128, 8], [1, 64]])),
                reads=[bpsKV], writes=[bstA])
            p.emit("vector", lambda e: e.tensor_scalar_add(
                out=fap(stA, 64, [[65, 8], [1, 1]]),
                in0=fap(psKS, 0, [[2, 8], [1, 1]]), scalar1=0.0),
                reads=[bpsKS], writes=[bstA])
            p.emit("vector", lambda e: e.tensor_scalar_add(
                out=fap(stA, 520 + 64, [[65, 8], [1, 1]]),
                in0=fap(psKS, 1, [[2, 8], [1, 1]]), scalar1=0.0),
                reads=[bpsKS], writes=[bstA])
            p.dma("sync", sgin, s["gin"], stA[:], reads=[bstA], writes=[bgin])
            bgout = p.buf("gout")
            p.collective("AllGather", ALU.bypass, [list(range(NCORES))],
                         [s["gin"].opt()], [s["gout"].opt()],
                         reads=[bgin], writes=[bgout])
            self.bgout = bgout

    def pass_attn_b_scan(self):
        p, nc, i, s = self.p, self.nc, self.i, self.s
        import contextlib
        T, NT = self.T, self.NT
        NBLK = TT // 128
        self.barrier()
        with contextlib.ExitStack() as ctx:
            S = p.slot
            wo, bwo = self.sb(ctx, "sc_wo", (128, DC * D), BF16)
            p.dma("gpsimd", S("wo"), wo[:], i["wo_b"].rearrange("c p f -> p c f"), writes=[bwo])
            qfm, bqfm = self.sb(ctx, "sc_q", (128, DC * TT), BF16)
            kfm, bkfm = self.sb(ctx, "sc_k", (128, DC * TT), BF16)
            ktm, bktm = self.sb(ctx, "sc_ktm", (128, NBLK * D), BF16)
            vplain, bvplain = self.sb(ctx, "sc_vp", (128, NBLK * D), BF16)
            vt65, bvt65 = self.sb(ctx, "sc_v65", (128, NBLK * H * 65), BF16)
            y2t, by2 = self.sb(ctx, "sc_y2", (128, DC * TT), BF16)
            KVr, bKVr = self.sb(ctx, "sc_kvr", (64, 16 * 65))
            stbfs = [self.sb(ctx, f"sc_stbf{j}", (128, 16 * 65), BF16) for j in range(TT // 128)]
            gacc = [self.sb(ctx, f"sc_ga{j}", (64, 16 * 65)) for j in range(2)]
            grk, bgrk = self.sb(ctx, "sc_grk", (64, 16 * 65))
            mskd, bmskd = self.sb(ctx, "sc_mskd", (128, 2 * 512), BF16)
            attn_tmb = [self.sb(ctx, f"sc_atm{j}", (128, 1024), BF16) for j in range(2)]
            attn_fm, bfm = self.sb(ctx, "sc_afm", (128, DC * TT), BF16)
            den16b = [self.sb(ctx, f"sc_den{j}", (128, H)) for j in range(2)]
            den16ib = [self.sb(ctx, f"sc_deni{j}", (128, H)) for j in range(2)]
            xo, bxo = self.sb(ctx, "sc_xo", (128, DC * TT))
            ps_a, bps_a = self.ps(ctx, "sc_psA", (128, TT))
            ps_b, bps_b = self.ps(ctx, "sc_psB", (128, TT))
            psN = [self.ps(ctx, f"sc_psN{g}", (128, 4 * 65)) for g in range(2)]
            psSt, bpsSt = self.ps(ctx, "sc_psSt", (64, 1024))
            psKS, bpsKS = self.ps(ctx, "sc_psKS", (64, 16))
            psT, bpsT = self.ps(ctx, "sc_psT", (128, 512))
            sq, sk, sktm, sv, sy2 = S("sc_q"), S("sc_k"), S("sc_ktm"), S("sc_v"), S("sc_y2")
            sxo = S("sc_xo")
            sod = S("sc_odd")
            bxod = p.buf()
            sgo = S("sc_go")

            p.emit("vector", lambda e: e.memset(gacc[0][0][:], 0.0), writes=[gacc[0][1]])
            for r in range(NCORES):
                src_t, bsrc = gacc[r % 2]
                dst_t, bdst = gacc[(r + 1) % 2]
                p.dma("sync", sgo, grk[:], s["gout"][r * 64:(r + 1) * 64, :],
                      reads=[self.bgout], writes=[bgrk])
                p.emit("vector", (lambda r_, s_, d_: lambda e: e.scalar_tensor_tensor(
                    out=d_[:], in0=grk[:], scalar=self.v["wpref"][0:64, r_:r_ + 1],
                    in1=s_[:], op0=ALU.mult, op1=ALU.add))(r, src_t, dst_t),
                    reads=[bgrk, bsrc, self.bv["wpref"]], writes=[bdst])
            p.emit("vector", lambda e: e.tensor_copy(out=KVr[:], in_=gacc[NCORES % 2][0][:]),
                   reads=[gacc[NCORES % 2][1]], writes=[bKVr])
            for stbf_j, bstbf_j in stbfs:
                p.emit("vector", (lambda s2: lambda e: e.memset(s2[:], 0.0))(stbf_j),
                       writes=[bstbf_j])
            p.emit("vector", lambda e: e.memset(
                fap(vt65, 64, [[1040, NBLK], [65, H], [1, 1]]), 1.0),
                writes=[bvt65])

            def upd_statebf(ci):
                stbf_j, bstbf_j = stbfs[ci]
                p.emit("vector", (lambda s2: lambda e: e.tensor_copy(
                    out=s2[0:64, 0:520], in_=KVr[0:64, 0:520]))(stbf_j),
                    reads=[bKVr], writes=[bstbf_j])
                p.dma("gpsimd", sod, stbf_j[64:128, 520:1040], KVr[0:64, 520:1040],
                      reads=[bKVr], writes=[bstbf_j])

            for ti in range(NT):
                t0 = ti * TT
                p.dma("sync", sq, qfm[:], s["qfm"][ti], writes=[bqfm])
                p.dma("sync", sk, kfm[:], s["kfm"][ti], writes=[bkfm])
                p.dma("sync", sktm, ktm[:], s["ktm"][ti], writes=[bktm])
                p.dma("sync", sv, vplain[:], s["vtm"][ti], writes=[bvplain])
                p.dma("scalar", sy2, y2t[:], s["y2"][ti], writes=[by2])
                p.emit("vector", lambda e: e.tensor_copy(
                    out=fap(vt65, 0, [[1040, NBLK], [65, H], [1, 64]]),
                    in_=vplain[:].rearrange("p (b f) -> p b f", b=NBLK).rearrange("p b (h e) -> p b h e", h=H)),
                    reads=[bvplain], writes=[bvt65])
                for ch in range(NBLK):
                    upd_statebf(ch)
                    for h in range(H):
                        p.emit("tensor", (lambda h_, ch_: lambda e: e.matmul(
                            psSt[:, 64 * h_:64 * h_ + 64],
                            ktm[:, ch_ * D + 64 * h_: ch_ * D + 64 * h_ + 64],
                            vplain[:, ch_ * D + 64 * h_: ch_ * D + 64 * h_ + 64],
                            start=True, stop=True))(h, ch),
                            reads=[bktm, bvplain], writes=[bpsSt])
                        p.emit("tensor", (lambda h_, ch_: lambda e: e.matmul(
                            psKS[:, h_:h_ + 1],
                            ktm[:, ch_ * D + 64 * h_: ch_ * D + 64 * h_ + 64],
                            self.ones_bf[:, 0:1],
                            start=True, stop=True))(h, ch),
                            reads=[bktm, self.bones], writes=[bpsKS])
                    p.emit("vector", lambda e: e.tensor_add(
                        out=fap(KVr, 0, [[65, 8], [1, 64]]),
                        in0=fap(KVr, 0, [[65, 8], [1, 64]]),
                        in1=fap(psSt, 0, [[128, 8], [1, 64]])),
                        reads=[bpsSt, bKVr], writes=[bKVr])
                    p.emit("vector", lambda e: e.tensor_add(
                        out=fap(KVr, 520, [[65, 8], [1, 64]]),
                        in0=fap(KVr, 520, [[65, 8], [1, 64]]),
                        in1=fap(psSt, 64, [[128, 8], [1, 64]])),
                        reads=[bpsSt, bKVr], writes=[bKVr])
                    p.emit("vector", lambda e: e.tensor_add(
                        out=fap(KVr, 64, [[65, 8], [1, 1]]),
                        in0=fap(KVr, 64, [[65, 8], [1, 1]]),
                        in1=fap(psKS, 0, [[2, 8], [1, 1]])),
                        reads=[bpsKS, bKVr], writes=[bKVr])
                    p.emit("vector", lambda e: e.tensor_add(
                        out=fap(KVr, 520 + 64, [[65, 8], [1, 1]]),
                        in0=fap(KVr, 520 + 64, [[65, 8], [1, 1]]),
                        in1=fap(psKS, 1, [[2, 8], [1, 1]])),
                        reads=[bpsKS, bKVr], writes=[bKVr])
                GROUPS = [(0, 2, 4, 6), (8, 10, 12, 14), (1, 3, 5, 7), (9, 11, 13, 15)]

                def sc_group(ch, gi):
                    grp = GROUPS[gi]
                    pst, bpst = (ps_a, bps_a) if gi % 2 == 0 else (ps_b, bps_b)
                    pr0 = 64 * (grp[0] % 2)
                    for idx, h in enumerate(grp):
                        p.emit("tensor", (lambda h_, pr0_, pst_, idx_, ch_: lambda e: e.matmul(
                            pst_[:, 128 * idx_:128 * idx_ + 128],
                            kfm[pr0_:pr0_ + 64, (h_ // 2) * TT + 128 * ch_:(h_ // 2) * TT + 128 * ch_ + 128],
                            qfm[pr0_:pr0_ + 64, (h_ // 2) * TT + 128 * ch_:(h_ // 2) * TT + 128 * ch_ + 128],
                            start=True, stop=True))(h, pr0, pst, idx, ch),
                            reads=[bkfm, bqfm], writes=[bpst])
                    p.emit("vector", (lambda gi_, pst_: lambda e: e.tensor_mul(
                        out=mskd[:, 512 * (gi_ % 2):512 * (gi_ % 2) + 512],
                        in0=pst_[:, 0:512], in1=self.mTb4[:]))(gi, pst),
                        reads=[bpst, self.bmTb4], writes=[bmskd])

                def num_group(ch, gi):
                    grp = GROUPS[gi]
                    pn, bpn = psN[gi % 2]
                    for idx, h in enumerate(grp):
                        base = 65 * (h // 2) if h % 2 == 0 else 520 + 65 * (h // 2)
                        p.emit("tensor", (lambda h_, gi_, pn_, idx_, ch_: lambda e: e.matmul(
                            pn_[:, 65 * idx_:65 * idx_ + 65],
                            mskd[:, 512 * (gi_ % 2) + 128 * idx_:512 * (gi_ % 2) + 128 * idx_ + 128],
                            fap(vt65, ch_ * 1040 + 65 * h_, [[1, 65]]),
                            start=True, stop=False))(h, gi, pn, idx, ch),
                            reads=[bmskd, bvt65], writes=[bpn])
                        p.emit("tensor", (lambda h_, pn_, idx_, base_, ch_, st2: lambda e: e.matmul(
                            pn_[:, 65 * idx_:65 * idx_ + 65],
                            qfm[:, (h_ // 2) * TT + 128 * ch_:(h_ // 2) * TT + 128 * ch_ + 128],
                            st2[:, base_:base_ + 65],
                            start=False, stop=True))(h, pn, idx, base, ch, stbfs[ch][0]),
                            reads=[bqfm, stbfs[ch][1]], writes=[bpn])

                def div_group(ch, gi):
                    grp = GROUPS[gi]
                    pn, bpn = psN[gi % 2]
                    den16, bden = den16b[ch % 2]
                    den16i, bdeni = den16ib[ch % 2]
                    attn_tm, btm = attn_tmb[ch % 2]
                    p.emit("vector", (lambda pn_, d_, gi_: lambda e: e.tensor_scalar_add(
                        out=d_[:, 4 * gi_:4 * gi_ + 4], in0=fap(pn_, 64, [[65, 4], [1, 1]]),
                        scalar1=1e-6))(pn, den16, gi),
                        reads=[bpn], writes=[bden])
                    p.emit("vector", (lambda d_, di_, gi_: lambda e: e.reciprocal(
                        out=di_[:, 4 * gi_:4 * gi_ + 4], in_=d_[:, 4 * gi_:4 * gi_ + 4]))(den16, den16i, gi),
                        reads=[bden], writes=[bdeni])
                    for idx, h in enumerate(grp):
                        eng = "vector" if idx % 2 == 0 else "scalar"
                        if eng == "vector":
                            p.emit("vector", (lambda h_, idx_, pn_, di_, tm_, gi_: lambda e: e.tensor_scalar_mul(
                                out=tm_[:, 64 * h_:64 * h_ + 64],
                                in0=pn_[:, 65 * idx_:65 * idx_ + 64],
                                scalar1=di_[:, 4 * gi_ + idx_:4 * gi_ + idx_ + 1]))(h, idx, pn, den16i, attn_tm, gi),
                                reads=[bpn, bdeni], writes=[btm])
                        else:
                            p.emit("scalar", (lambda h_, idx_, pn_, di_, tm_, gi_: lambda e: e.mul(
                                out=tm_[:, 64 * h_:64 * h_ + 64],
                                in_=pn_[:, 65 * idx_:65 * idx_ + 64],
                                mul=di_[:, 4 * gi_ + idx_:4 * gi_ + idx_ + 1]))(h, idx, pn, den16i, attn_tm, gi),
                                reads=[bpn, bdeni], writes=[btm])

                def transposes(ch):
                    attn_tm, btm = attn_tmb[ch % 2]
                    ptbf = psT.ap().bitcast(BF16)
                    for c in range(DC):
                        p.emit("tensor", (lambda c_, tm_: lambda e: e.transpose(
                            ptbf[:, 128 * c_:128 * c_ + 128],
                            tm_[:, 128 * c_:128 * c_ + 128], self.id_bf[:]))(c, attn_tm),
                            reads=[btm, self.bid_bf], writes=[bpsT])
                    p.emit("scalar", (lambda ch_: lambda e: e.copy(
                        out=fap(attn_fm, 128 * ch_, [[TT, DC], [1, 128]]),
                        in_=ptbf[:, 0:1024]))(ch),
                        reads=[bpsT], writes=[bfm])

                items = [(ch, gi) for ch in range(NBLK) for gi in range(4)]
                for idx2, (ch, gi) in enumerate(items):
                    sc_group(ch, gi)
                    if idx2 >= 1:
                        num_group(*items[idx2 - 1])
                    if idx2 >= 2:
                        div_group(*items[idx2 - 2])
                    if gi == 2 and ch > 0:
                        transposes(ch - 1)
                num_group(*items[-1])
                div_group(*items[-2])
                div_group(*items[-1])
                transposes(NBLK - 1)
                for oc in range(DC):
                    pst, bpst = (ps_a, bps_a) if oc % 2 == 0 else (ps_b, bps_b)
                    for c in range(DC):
                        p.emit("tensor", (lambda oc_, c_, pst_: lambda e: e.matmul(
                            pst_[:, 0:TT], wo[:, c_ * D + 128 * oc_: c_ * D + 128 * oc_ + 128],
                            attn_fm[:, c_ * TT:(c_ + 1) * TT],
                            start=(c_ == 0), stop=(c_ == DC - 1)))(oc, c, pst),
                            reads=[bwo, bfm], writes=[bpst])
                    p.emit("vector", (lambda oc_, pst_: lambda e: e.scalar_tensor_tensor(
                        out=xo[:, oc_ * TT:(oc_ + 1) * TT], in0=pst_[:, 0:TT], scalar=1.0,
                        in1=y2t[:, oc_ * TT:(oc_ + 1) * TT], op0=ALU.mult, op1=ALU.add))(oc, pst),
                        reads=[bpst, by2], writes=[bxo])
                p.dma("gpsimd", sxo, s["xo"][ti], xo[:],
                      reads=[bxo], writes=[bxod])

    def pass_attn_b_ffn1(self):
        p, nc, i, s = self.p, self.nc, self.i, self.s
        import contextlib
        T, NT = self.T, self.NT
        self.barrier()
        with contextlib.ExitStack() as ctx:
            S = p.slot
            w1, bw1 = self.sb(ctx, "g1_w1", (128, DC * FFB), BF16)
            p.dma("gpsimd", S("w1b"), w1[:], i["w1b"].rearrange("c p f -> p c f"), writes=[bw1])
            xob = [self.sb(ctx, f"g1_xo{j}", (128, DC * TT)) for j in range(2)]
            xoslots = [S(f"g1_xo{j}") for j in range(2)]
            xbf, bxbf = self.sb(ctx, "g1_xbf", (128, DC * TT), BF16)
            xsq, bxsq = self.sb(ctx, "g1_xsq", (128, DC * TT), BF16)
            mu, bmu = self.sb(ctx, "g1_mu", (128, TT))
            var, bvar = self.sb(ctx, "g1_var", (128, TT))
            rstd, brstd = self.sb(ctx, "g1_rstd", (128, TT))
            h2b = [self.sb(ctx, f"g1_h2{j}", (128, DC * TT), BF16) for j in range(2)]
            hst, bhst = self.sb(ctx, "g1_hst", (128, 8 * TT), BF16)
            hstslots = [S(f"g1_hs{j}") for j in range(2)]
            ps4 = [self.ps(ctx, f"g1_ps{j}", (128, TT)) for j in range(4)]
            psS, bpsS = self.ps(ctx, "g1_psS", (128, 2 * TT))
            scr = (xbf, bxbf, xsq, bxsq, mu, bmu, var, bvar, rstd, brstd, psS, bpsS)
            bhd = p.buf()

            def prep(tj):
                xo_sb, bxo = xob[tj % 2]
                h2, bh2 = h2b[tj % 2]
                p.dma("sync", xoslots[tj % 2], xo_sb[:], s["xo"][tj], writes=[bxo])
                self.emit_ln(ctx, (lambda x2: lambda c: x2[:, c * TT:(c + 1) * TT])(xo_sb), bxo, TT,
                             gamma=self.v["g2"], beta=self.v["be2"],
                             out_f32=None, out_bf=(h2, bh2), scratch=scr)

            prep(0)
            for ti in range(NT):
                t0 = ti * TT
                h2, bh2 = h2b[ti % 2]
                if ti + 1 < NT:
                    prep(ti + 1)
                for fo in range(FFB // 128):
                    pst, bpst = ps4[fo % 4]
                    for c in range(DC):
                        p.emit("tensor", (lambda fo_, c_, pst_, h2_: lambda e: e.matmul(
                            pst_[:, 0:TT], w1[:, c_ * FFB + 128 * fo_: c_ * FFB + 128 * fo_ + 128],
                            h2_[:, c_ * TT:(c_ + 1) * TT],
                            start=(c_ == 0), stop=(c_ == DC - 1)))(fo, c, pst, h2),
                            reads=[bw1, bh2], writes=[bpst])
                    hoff = (fo % 4) * TT + ((fo // 4) % 2) * 4 * TT
                    p.emit("scalar", (lambda fo_, pst_, ho: lambda e: e.activation(
                        out=hst[:, ho:ho + TT], in_=pst_[:, 0:TT],
                        func=AF.Gelu, bias=self.v["b1b"][:, fo_:fo_ + 1]))(fo, pst, hoff),
                        reads=[bpst, self.bv["b1b"]], writes=[bhst])
                    if fo % 4 == 3:
                        grp = fo // 4
                        p.dma("gpsimd", hstslots[grp % 2],
                              s["h"][ti][:, (fo - 3) * TT:(fo + 1) * TT],
                              hst[:, (grp % 2) * 4 * TT:(grp % 2) * 4 * TT + 4 * TT],
                              reads=[bhst], writes=[bhd])

    def pass_attn_b_ffn2(self, dst):
        p, nc, i, s = self.p, self.nc, self.i, self.s
        import contextlib
        T, NT = self.T, self.NT
        self.barrier()
        with contextlib.ExitStack() as ctx:
            S = p.slot
            w2, bw2 = self.sb(ctx, "g2_w2", (128, (FFB // 128) * D), BF16)
            p.dma("gpsimd", S("w2b"), w2[:], i["w2b"].rearrange("c p f -> p c f"), writes=[bw2])
            hf, _bhf = self.sb(ctx, "g2_hf", (128, (FFB // 128) * TT), BF16)
            bhf = [p.buf(f"g2_hc{k}") for k in range(FFB // 128)]
            xo_sb, bxo = self.sb(ctx, "g2_xo", (128, DC * TT))
            x2_sb, bx2 = self.sb(ctx, "g2_x2", (128, DC * TT))
            xs, bxs = self.sb(ctx, "g2_xs", (128, DC * TT))
            xbf, bxbf = self.sb(ctx, "g2_xbf", (128, DC * TT), BF16)
            xsq, bxsq = self.sb(ctx, "g2_xsq", (128, DC * TT), BF16)
            mu, bmu = self.sb(ctx, "g2_mu", (128, TT))
            var, bvar = self.sb(ctx, "g2_var", (128, TT))
            rstd, brstd = self.sb(ctx, "g2_rstd", (128, TT))
            y3b, by3 = xbf, bxbf
            mxc, bmxc = self.sb(ctx, "g2_mx", (128, 1))
            ps4 = [self.ps(ctx, f"g2_ps{j}", (128, TT)) for j in range(4)]
            psS, bpsS = self.ps(ctx, "g2_psS", (128, 2 * TT))
            scr = (xbf, bxbf, xsq, bxsq, mu, bmu, var, bvar, rstd, brstd, psS, bpsS)
            sh, sxo, sx2 = S("g2_h"), S("g2_xo"), S("g2_x2")
            sx3, sy3 = S("g2_x3"), S("g2_y3")
            bx3d, by3d = p.buf(), p.buf()

            def epilogue(ti):
                p.dma("gpsimd", sx3, dst[ti], xs[:], reads=[bxs], writes=[bx3d])
                if self.n_sub >= 4:
                    self.emit_ln(ctx, lambda c: xs[:, c * TT:(c + 1) * TT], bxs, TT,
                                 out_f32=None, out_bf=(y3b, by3), scratch=scr)
                    p.dma("scalar", sy3, s["y3"][ti], y3b[:],
                          reads=[by3], writes=[by3d])
                    p.emit("vector", lambda e: e.reduce_max(out=mxc[:], in_=y3b[:], axis=AX.X),
                           reads=[by3], writes=[bmxc])
                    p.emit("vector", lambda e: e.tensor_max(
                        out=self.mloc[:, 2:3], in0=self.mloc[:, 2:3], in1=mxc[:]),
                        reads=[bmxc, self.bmloc], writes=[self.bmloc])

            for ti in range(NT):
                t0 = ti * TT
                p.dma("sync", sh, hf[:], s["h"][ti], writes=bhf)
                ftok = (sh, p.cnt[sh])
                for kc in range(FFB // 128):
                    bhf[kc].last_w = ftok
                p.dma("sync", sxo, xo_sb[:], s["xo"][ti], writes=[bxo])
                p.dma("sync", sx2, x2_sb[:], s["x2"][ti], writes=[bx2])
                if ti > 0:
                    epilogue(ti - 1)
                p.emit("gpsimd", lambda e: e.tensor_add(out=xs[:], in0=xo_sb[:], in1=x2_sb[:]),
                       reads=[bxo, bx2], writes=[bxs])
                for do in range(DC):
                    pst, bpst = ps4[do % 4]
                    for kc in range(FFB // 128):
                        p.emit("tensor", (lambda do_, kc_, pst_: lambda e: e.matmul(
                            pst_[:, 0:TT], w2[:, kc_ * D + 128 * do_: kc_ * D + 128 * do_ + 128],
                            hf[:, kc_ * TT:(kc_ + 1) * TT],
                            start=(kc_ == 0), stop=(kc_ == FFB // 128 - 1)))(do, kc, pst),
                            reads=[bw2, bhf[kc]], writes=[bpst])
                    p.emit("vector", (lambda do_, pst_: lambda e: e.scalar_tensor_tensor(
                        out=xs[:, do_ * TT:(do_ + 1) * TT], in0=pst_[:, 0:TT],
                        scalar=self.v["b2b"][:, do_:do_ + 1], in1=xs[:, do_ * TT:(do_ + 1) * TT],
                        op0=ALU.add, op1=ALU.add))(do, pst),
                        reads=[bpst, self.bv["b2b"], bxs], writes=[bxs])
            epilogue(NT - 1)



def build(T=T_CORE, n_sub=4, dbg=False):
    import contextlib
    kb = KB(T=T, n_sub=n_sub)
    kb.dbg = dbg
    p = kb.p
    with contextlib.ExitStack() as topctx:
        kb.setup(topctx)
        if n_sub >= 2:
            kb.pass_quantize_f()
        kb.pass_attn_a_front()
        kb.pass_attn_a_ffn(kb.o["xout"] if n_sub == 1 else kb.s["x1"])
        if n_sub >= 2:
            kb.barrier()
            kb.pass_sync_scales(0, 0)
            kb.pass_f_l1(kb.s["y1"], 0, 1)
            kb.barrier()
            kb.pass_sync_scales(1, 1)
            kb.pass_f_l2(kb.s["x1"], kb.o["xout"] if n_sub == 2 else kb.s["x2"], 1,
                         y_out=kb.s["y2"] if n_sub >= 3 else None)
            if n_sub >= 3:
                kb.pass_attn_b_front()
                kb.pass_attn_b_scan()
                kb.pass_attn_b_ffn1()
                kb.pass_attn_b_ffn2(kb.o["xout"] if n_sub == 3 else kb.s["x3"])
            if n_sub >= 4:
                kb.barrier()
                kb.pass_sync_scales(2, 2)
                kb.pass_f_l1(kb.s["y3"], 2, 3)
                kb.barrier()
                kb.pass_sync_scales(3, 3)
                kb.pass_f_l2(kb.s["x3"], kb.o["xout"], 3)
        p.realize()
    return kb


def kernel(**inputs):
    kb = build()
    in_maps = host_inputs(inputs)
    res = run_bass_kernel_spmd(kb.nc, in_maps, core_ids=list(range(NCORES)))
    return host_output(res.results)


# version 23 (final)
# speedup vs baseline: 1.0341x; 1.0149x over previous
import sys
import numpy as np
import ml_dtypes
import concourse.bass as bass
import concourse.mybir as mybir
from concourse.bass_utils import run_bass_kernel_spmd

F32 = mybir.dt.float32
BF16 = mybir.dt.bfloat16
AF = mybir.ActivationFunctionType
ALU = mybir.AluOpType
AX = mybir.AxisListType

NCORES = 8
BATCH = 2
SEQ = 16384
D = 1024
DC = 8
H = 16
DHD = 64
WIN = 128
FFA = 2048
FFB = 4096
FFQ = 4096
T_CORE = SEQ // 4
TT = 512
LN_EPS = 1e-5
M_SHIFT = -40.0

ENGINES = ("tensor", "vector", "scalar", "gpsimd", "sync")
SKIP_SAME_ENGINE = {"tensor"}


class Buf:
    __slots__ = ("name", "last_w", "readers", "is_psum")

    def __init__(self, name, is_psum=False):
        self.name = name
        self.last_w = None
        self.readers = []
        self.is_psum = is_psum


class Prog:

    def __init__(self, nc: bass.Bass):
        self.nc = nc
        self.sems = {}
        self.cnt = {}
        self.q = {e: [] for e in ENGINES}
        self.meta = {e: [] for e in ENGINES}
        self.seen = {e: {} for e in ENGINES}
        for e in ENGINES:
            self.sems[e] = nc.alloc_semaphore(f"sem_{e}")
            self.cnt[e] = 0
        self.nbuf = 0
        self.nslot = 0

    def buf(self, name=None):
        self.nbuf += 1
        return Buf(name or f"b{self.nbuf}")

    def slot(self, name=None):
        self.nslot += 1
        key = f"ds{self.nslot}_{name or ''}"
        self.sems[key] = self.nc.alloc_semaphore(key)
        self.cnt[key] = 0
        return key

    def emit(self, engine, fn, reads=(), writes=(), sem_key=None, dma=False,
             inc=None, extra_deps=()):
        deps = {}

        def add(tok):
            if tok is None:
                return
            s, v = tok
            if deps.get(s, -1) < v:
                deps[s] = v

        for b in reads:
            add(b.last_w)
            if b.is_psum:
                for t in b.readers:
                    add(t)
        for b in writes:
            add(b.last_w)
            for t in b.readers:
                add(t)
        for t in extra_deps:
            add(t)

        seen = self.seen[engine]
        waits = []
        for s, v in sorted(deps.items()):
            if s == engine and engine in SKIP_SAME_ENGINE:
                continue
            if seen.get(s, -1) >= v:
                continue
            waits.append((s, v))
            seen[s] = v

        if sem_key is None:
            assert not dma
            sem_key = engine
        if inc is None:
            inc = 16 if dma else 1
        if fn is None:
            inc = 0
        self.cnt[sem_key] += inc
        tok = (sem_key, self.cnt[sem_key]) if inc else None
        fr = sys._getframe(1)
        if fr.f_code.co_name in ("dma", "collective"):
            fr = fr.f_back
        self.meta[engine].append((fr.f_lineno, fr.f_code.co_name, len(waits)))
        self.q[engine].append((waits, fn, inc, sem_key))

        if tok is not None:
            for b in writes:
                b.last_w = tok
                b.readers = []
            for b in reads:
                b.readers.append(tok)
        return tok

    def dma(self, queue, slot, out_ap, in_ap, reads=(), writes=(), **kw):
        return self.emit(queue, lambda e: e.dma_start(out=out_ap, in_=in_ap, **kw),
                         reads=reads, writes=writes, sem_key=slot, dma=True)

    def collective(self, kind, op, groups, ins, outs, reads=(), writes=()):
        slot = self.slot("cc")
        return self.emit(
            "gpsimd",
            lambda e: e.collective_compute(kind, op, replica_groups=groups,
                                           ins=ins, outs=outs),
            reads=reads, writes=writes, sem_key=slot, inc=1)

    def barrier(self):
        toks = [(e, self.cnt[e]) for e in ENGINES if self.cnt[e] > 0]
        for e in ENGINES:
            self.emit(e, None, extra_deps=toks)

    def realize(self):
        finals = [(k, v) for k, v in self.cnt.items()
                  if k not in ENGINES and v > 0]
        for k, v in finals:
            if self.seen["sync"].get(k, -1) < v:
                self.q["sync"].append(([(k, v)], None, 0, "sync"))
                self.seen["sync"][k] = v
        nc = self.nc
        sems = self.sems
        with nc.Block() as block:
            def body(engine_name):
                def _(eng):
                    for waits, fn, inc, sem_key in self.q[engine_name]:
                        for s, v in waits:
                            eng.wait_ge(sems[s], v)
                        if fn is not None:
                            inst = fn(eng)
                            inst.then_inc(sems[sem_key], inc)
                return _
            block.tensor(body("tensor"))
            block.vector(body("vector"))
            block.scalar(body("scalar"))
            block.gpsimd(body("gpsimd"))
            block.sync(body("sync"))


def fap(handle, free_off, dims):
    ap = handle.ap()
    part = list(ap.ap[0])
    return bass.AP(ap.tensor, free_off, [part] + [list(d) for d in dims])



def _fm(w):
    d, f = w.shape
    return np.ascontiguousarray(w.reshape(d // 128, 128, f), dtype=np.float32)


def _col(v):
    n = v.shape[0]
    return np.ascontiguousarray(v.reshape(n // 128, 128).T, dtype=np.float32)


def host_inputs(inp, T=T_CORE):
    nslice = SEQ // T
    x = np.asarray(inp["L0_attn_a_x"], dtype=np.float32)

    wq_a = _fm(np.ascontiguousarray(np.transpose(inp["L0_attn_a_q_proj"], (1, 0, 2))).reshape(D, D))
    wk_a = _fm(np.ascontiguousarray(np.transpose(inp["L0_attn_a_k_proj"], (1, 0, 2))).reshape(D, D))
    wv_a = _fm(np.ascontiguousarray(np.transpose(inp["L0_attn_a_v_proj"], (1, 0, 2))).reshape(D, D))
    qb_a = _col(np.asarray(inp["L0_attn_a_q_bias"]).reshape(D))
    kb_a = _col(np.asarray(inp["L0_attn_a_k_bias"]).reshape(D))
    vb_a = _col(np.asarray(inp["L0_attn_a_v_bias"]).reshape(D))
    ln1g = _col(np.asarray(inp["L0_attn_a_ln1_g"]))
    ln1b = _col(np.asarray(inp["L0_attn_a_ln1_b"]))
    ln2g = _col(np.asarray(inp["L0_attn_a_ln2_g"]))
    ln2b = _col(np.asarray(inp["L0_attn_a_ln2_b"]))
    ff1w = _fm(np.asarray(inp["L0_attn_a_ff1_w"]))
    ff1b = _col(np.asarray(inp["L0_attn_a_ff1_b"]))
    ff2w = _fm(np.asarray(inp["L0_attn_a_ff2_w"]))
    ff2b = _col(np.asarray(inp["L0_attn_a_ff2_b"]))

    w1f = _fm(np.asarray(inp["L0_ffn_w1"]))
    b1f = _col(np.asarray(inp["L0_ffn_b1"]))
    w2f = _fm(np.asarray(inp["L0_ffn_w2"]))
    b2f = _col(np.asarray(inp["L0_ffn_b2"]))

    wq_b = _fm(np.asarray(inp["L0_attn_b_wq"]))
    wk_b = _fm(np.asarray(inp["L0_attn_b_wk"]))
    wv_b = _fm(np.asarray(inp["L0_attn_b_wv"]))
    wo_b = _fm(np.asarray(inp["L0_attn_b_wo"]))
    proj = np.asarray(inp["L0_attn_b_proj"], dtype=np.float32)
    projblk = np.zeros((128, 128), dtype=np.float32)
    projblk[0:64, 0:64] = proj
    projblk[64:128, 64:128] = proj
    g1 = _col(np.asarray(inp["L0_attn_b_g1"]))
    be1 = _col(np.asarray(inp["L0_attn_b_beta1"]))
    g2 = _col(np.asarray(inp["L0_attn_b_g2"]))
    be2 = _col(np.asarray(inp["L0_attn_b_beta2"]))
    w1b = _fm(np.asarray(inp["L0_attn_b_w1"]))
    b1b = _col(np.asarray(inp["L0_attn_b_b1"]))
    w2b = _fm(np.asarray(inp["L0_attn_b_w2"]))
    b2b = _col(np.asarray(inp["L0_attn_b_b2"]))

    i_idx = np.arange(WIN)[:, None]
    j_idx = np.arange(WIN)[None, :]
    mT_own = np.where(j_idx.T >= i_idx.T, 0.0, 0.0).astype(np.float32)
    mT_own = np.where(j_idx.reshape(-1, 1) <= i_idx.reshape(1, -1), 0.0, -1e9).astype(np.float32)
    mT_allow = np.zeros((WIN, WIN), dtype=np.float32)
    mT_deny = np.full((WIN, WIN), -1e9, dtype=np.float32)
    mb = (j_idx.reshape(-1, 1) <= i_idx.reshape(1, -1)).astype(np.float32)
    mTb4 = np.tile(mb, (1, 4)).reshape(128, 4, 128).transpose(0, 1, 2)
    mTb4 = np.ascontiguousarray(np.concatenate([mb] * 4, axis=1))

    ident = np.eye(128, dtype=np.float32)

    mT_own4 = np.ascontiguousarray(np.concatenate([mT_own] * 4, axis=1))
    shared = dict(
        wq_a=wq_a, wk_a=wk_a, wv_a=wv_a, qb_a=qb_a, kb_a=kb_a, vb_a=vb_a,
        ln1g=ln1g, ln1b=ln1b, ln2g=ln2g, ln2b=ln2b,
        ff1w=ff1w, ff1b=ff1b, ff2w=ff2w, ff2b=ff2b,
        w1f=w1f, b1f=b1f, w2f=w2f, b2f=b2f,
        wq_b=wq_b, wk_b=wk_b, wv_b=wv_b, wo_b=wo_b, projblk=projblk,
        g1=g1, be1=be1, g2=g2, be2=be2,
        w1b=w1b, b1b=b1b, w2b=w2b, b2b=b2b,
        mT_own4=mT_own4, mTb4=mTb4, ident=ident,
    )

    in_maps = []
    for c in range(NCORES):
        b = c // nslice
        s = c % nslice
        t0 = s * T
        xt = np.zeros((T + 128, D), dtype=np.float32)
        xt[128:] = x[b, t0:t0 + T]
        if s > 0:
            xt[:128] = x[b, t0 - 128:t0]
        nt = T // TT
        arr = xt[128:].T.reshape(DC, 128, nt, TT)
        x0t = np.ascontiguousarray(arr.transpose(2, 1, 0, 3).reshape(nt, 128, DC * TT))
        x0h = np.ascontiguousarray(xt[:128].T.reshape(DC, 128, 128).transpose(1, 0, 2).reshape(128, DC * 128))
        wpref = np.zeros((128, NCORES), dtype=np.float32)
        for r in range(NCORES):
            if r // nslice == b and r % nslice < s:
                wpref[:, r] = 1.0
        m = dict(shared)
        m["x0t"] = x0t
        m["x0h"] = x0h
        mp = mT_deny if s == 0 else mT_allow
        m["mT_prev04"] = np.ascontiguousarray(np.concatenate([mp] * 4, axis=1))
        m["wpref"] = wpref
        in_maps.append(m)
    return in_maps


def host_output(results, T=T_CORE):
    nslice = SEQ // T
    out = np.empty((BATCH, SEQ, D), dtype=np.float32)
    for c in range(NCORES):
        b = c // nslice
        s = c % nslice
        fm = results[c]["xout"]
        nt = T // TT
        dmt = fm.reshape(nt, 128, DC, TT).transpose(2, 1, 0, 3).reshape(D, T)
        out[b, s * T:(s + 1) * T, :] = dmt.T
    return out



class KB:
    def __init__(self, T=T_CORE, n_sub=4):
        self.T = T
        self.NT = T // TT
        self.n_sub = n_sub
        nc = bass.Bass("TRN2", target_bir_lowering=False)
        self.nc = nc
        self.p = Prog(nc)
        self.ctxs = []
        self.dbg = False
        self._declare_dram()
        self.sdbg1, self.sdbg2 = self.p.slot("dbg1"), self.p.slot("dbg2")
        self.bdbg1, self.bdbg2 = self.p.buf(), self.p.buf()

    def _declare_dram(self):
        nc, T = self.nc, self.T
        NT = T // TT

        def din(name, shape):
            return nc.dram_tensor(name, list(shape), F32, kind="ExternalInput").ap()

        i = {}
        i["x0t"] = din("x0t", (NT, 128, DC * TT))
        i["x0h"] = din("x0h", (128, DC * 128))
        for n, sh in [("wq_a", (DC, 128, D)), ("wk_a", (DC, 128, D)), ("wv_a", (DC, 128, D)),
                      ("qb_a", (128, DC)), ("kb_a", (128, DC)), ("vb_a", (128, DC)),
                      ("ln1g", (128, DC)), ("ln1b", (128, DC)), ("ln2g", (128, DC)), ("ln2b", (128, DC)),
                      ("ff1w", (DC, 128, FFA)), ("ff1b", (128, FFA // 128)),
                      ("ff2w", (FFA // 128, 128, D)), ("ff2b", (128, DC)),
                      ("w1f", (FFQ // 128, 128, D)), ("b1f", (128, FFQ // 128)),
                      ("w2f", (DC, 128, FFQ)), ("b2f", (128, DC)),
                      ("wq_b", (DC, 128, D)), ("wk_b", (DC, 128, D)), ("wv_b", (DC, 128, D)),
                      ("wo_b", (DC, 128, D)), ("projblk", (128, 128)),
                      ("g1", (128, DC)), ("be1", (128, DC)), ("g2", (128, DC)), ("be2", (128, DC)),
                      ("w1b", (DC, 128, FFB)), ("b1b", (128, FFB // 128)),
                      ("w2b", (FFB // 128, 128, D)), ("b2b", (128, DC)),
                      ("mT_own4", (128, 512)), ("mT_prev04", (128, 512)), ("mTb4", (128, 512)),
                      ("ident", (128, 128)), ("wpref", (128, NCORES))]:
            i[n] = din(n, sh)
        self.i = i

        def scr(name, shape, dt=F32):
            return nc.dram_tensor(name, list(shape), dt, kind="Internal").ap()

        s = {}
        s["x1"] = scr("x1s", (NT, 128, DC * TT))
        s["x2"] = scr("x2s", (NT, 128, DC * TT))
        s["x3"] = scr("x3s", (NT, 128, DC * TT))
        s["u"] = scr("us", (NT, 128, DC * TT))
        s["xo"] = scr("xos", (NT, 128, DC * TT))
        s["y1"] = scr("y1s", (NT, 128, DC * TT), BF16)
        s["y2"] = scr("y2s", (NT, 128, DC * TT), BF16)
        s["y3"] = scr("y3s", (NT, 128, DC * TT), BF16)
        s["h"] = scr("hs", (NT, 128, (FFQ // 128) * TT), BF16)
        s["qfm"] = scr("qfms", (NT, 128, DC * TT), BF16)
        s["kfm"] = scr("kfms", (NT, 128, DC * TT), BF16)
        s["ktm"] = scr("ktms", (NT, 128, (TT // 128) * D), BF16)
        s["vtm"] = scr("vtms", (NT, 128, (TT // 128) * D), BF16)
        s["m1T"] = scr("m1Ts", (128, DC, FFQ), BF16)
        s["m2T"] = scr("m2Ts", (128, FFQ // 128, D), BF16)
        for k in range(4):
            s[f"cin{k}"] = scr(f"cin{k}", (1, 1))
            s[f"cout{k}"] = scr(f"cout{k}", (1, 1))
        s["dy0"] = scr("dy0s", (DC, 128, T), BF16)
        s["dwin"] = scr("dwins", (4, 128, 1024), BF16)
        s["dexp"] = scr("dexps", (4, 8, 128, 512), BF16)
        s["dattn"] = scr("dattns", (DC, 128, T), BF16)
        s["gin"] = scr("gin", (64, 16 * 65))
        s["gout"] = scr("gout", (NCORES * 64, 16 * 65))
        self.s = s

        self.o = {"xout": nc.dram_tensor("xout", [NT, 128, DC * TT], F32, kind="ExternalOutput").ap()}

    _uid = [0]

    def sb(self, ctx, name, shape, dt=F32):
        self._uid[0] += 1
        h = ctx.enter_context(self.nc.sbuf_tensor(f"sb{self._uid[0]}_{name}", list(shape), dt))
        return h, self.p.buf(name)

    def ps(self, ctx, name, shape, dt=F32):
        self._uid[0] += 1
        h = ctx.enter_context(self.nc.psum_tensor(f"ps{self._uid[0]}_{name}", list(shape), dt))
        b = self.p.buf(name)
        b.is_psum = True
        return h, b

    def barrier(self):
        p = self.p
        toks = [(k, v) for k, v in p.cnt.items() if v > 0]
        for e in ENGINES:
            p.emit(e, None, extra_deps=toks)

    def setup(self, ctx):
        p, nc, i = self.p, self.nc, self.i
        S = p.slot
        cst, bcst = self.sb(ctx, "cst", (128, 8))
        self.cst, self.bcst = cst, bcst
        for col, val in [(0, M_SHIFT), (1, LN_EPS), (2, -0.5), (3, 0.125),
                         (4, 1.0 / D), (5, 1e-6), (6, 255.0), (7, 1e-8)]:
            p.emit("vector", (lambda c_, v_: lambda e: e.memset(cst[:, c_:c_ + 1], v_))(col, val),
                   writes=[bcst])
        self.c_m40 = cst[:, 0:1]
        self.c_eps = cst[:, 1:2]
        self.c_nh = cst[:, 2:3]
        self.c_8th = cst[:, 3:4]
        self.c_invD = cst[:, 4:5]
        self.c_1e6 = cst[:, 5:6]

        self.id_bf, self.bid_bf = self.sb(ctx, "id_bf", (128, 128), BF16)
        self.id_f, self.bid_f = self.sb(ctx, "id_f", (128, 128), F32)
        p.dma("gpsimd", S("idbf"), self.id_bf[:], i["ident"], writes=[self.bid_bf])
        p.dma("sync", S("idf"), self.id_f[:], i["ident"], writes=[self.bid_f])
        self.ones_bf, self.bones = self.sb(ctx, "ones_bf", (128, 128), BF16)
        p.emit("vector", lambda e: e.memset(self.ones_bf[:], 1.0), writes=[self.bones])
        self.ones_f, self.bones_f = self.sb(ctx, "ones_f", (1, 128))
        p.emit("vector", lambda e: e.memset(self.ones_f[:], 1.0), writes=[self.bones_f])

        self.v = {}
        self.bv = {}
        slot_v = S("vecs")
        vec_bufs = []
        for n in ["qb_a", "kb_a", "vb_a", "ln1g", "ln1b", "ln2g", "ln2b",
                  "ff1b", "ff2b", "b1f", "b2f", "g1", "be1", "g2", "be2",
                  "b1b", "b2b", "wpref"]:
            sh = tuple(i[n].shape)
            t, b = self.sb(ctx, "v_" + n, sh)
            self.v[n], self.bv[n] = t, b
            p.dma("sync", slot_v, t[:], i[n], writes=[b])
            vec_bufs.append(b)
        self.mT_own4, self.bmT_own4 = self.sb(ctx, "mT_own4", (128, 512))
        self.mT_prev04, self.bmT_prev04 = self.sb(ctx, "mT_prev04", (128, 512))
        self.mTb4, self.bmTb4 = self.sb(ctx, "mTb4", (128, 512), BF16)
        self.projblk, self.bprojblk = self.sb(ctx, "projblk", (128, 128), BF16)
        p.dma("sync", slot_v, self.mT_own4[:], i["mT_own4"], writes=[self.bmT_own4])
        p.dma("sync", slot_v, self.mT_prev04[:], i["mT_prev04"], writes=[self.bmT_prev04])
        vec_bufs += [self.bmT_own4, self.bmT_prev04]
        final_tok = ("ds" + slot_v.split("ds")[1], p.cnt[slot_v]) if False else (slot_v, p.cnt[slot_v])
        for b in vec_bufs:
            b.last_w = final_tok
        p.dma("gpsimd", S("mtb"), self.mTb4[:], i["mTb4"], writes=[self.bmTb4])
        p.dma("gpsimd", S("pblk"), self.projblk[:], i["projblk"], writes=[self.bprojblk])

        self.qb_s, self.bqb_s = self.sb(ctx, "qb_s", (128, DC))
        p.emit("vector", lambda e: e.tensor_scalar_mul(out=self.qb_s[:], in0=self.v["qb_a"][:], scalar1=0.125),
               reads=[self.bv["qb_a"]], writes=[self.bqb_s])

        self.sw1, self.bsw1 = self.sb(ctx, "sw1", (128, FFQ // 128))
        self.sw2, self.bsw2 = self.sb(ctx, "sw2", (128, DC))
        self.qcols, self.bqcols = self.sb(ctx, "qcols", (128, 8))
        self.zs1, self.bzs1 = self.sb(ctx, "zs1", (128, FFQ // 128))
        self.zs2, self.bzs2 = self.sb(ctx, "zs2", (128, DC))
        self.b1q, self.bb1q = self.sb(ctx, "b1q", (128, FFQ // 128))
        self.b2q, self.bb2q = self.sb(ctx, "b2q", (128, DC))
        self.mloc, self.bmloc = self.sb(ctx, "mloc", (128, 4))
        p.emit("vector", lambda e: e.memset(self.mloc[:], -1e30), writes=[self.bmloc])

    def emit_ln(self, ctx_tiles, xap_chunks, bx, n, gamma=None, beta=None,
                out_f32=None, out_bf=None, scratch=None):
        p = self.p
        (xbf, bxbf, xsq, bxsq, mu, bmu, var, bvar, rstd, brstd, psS, bpsS) = scratch
        for c in range(DC):
            p.emit("vector", (lambda c_: lambda e: e.tensor_copy(
                out=xbf[:, c_ * n:(c_ + 1) * n], in_=xap_chunks(c_)))(c),
                reads=[bx], writes=[bxbf])
        p.emit("vector", lambda e: e.tensor_mul(out=xsq[:, 0:DC * n], in0=xbf[:, 0:DC * n], in1=xbf[:, 0:DC * n]),
               reads=[bxbf], writes=[bxsq])
        for c in range(DC):
            p.emit("tensor", (lambda c_: lambda e: e.matmul(psS[:, 0:n], self.ones_bf[:], xbf[:, c_ * n:(c_ + 1) * n],
                                                            start=(c_ == 0), stop=(c_ == DC - 1)))(c),
                   reads=[self.bones, bxbf], writes=[bpsS])
        for c in range(DC):
            p.emit("tensor", (lambda c_: lambda e: e.matmul(psS[:, n:2 * n], self.ones_bf[:], xsq[:, c_ * n:(c_ + 1) * n],
                                                            start=(c_ == 0), stop=(c_ == DC - 1)))(c),
                   reads=[self.bones, bxsq], writes=[bpsS])
        p.emit("scalar", lambda e: e.activation(out=mu[:, 0:n], in_=psS[:, 0:n], func=AF.Identity, scale=self.c_invD),
               reads=[bpsS, self.bcst], writes=[bmu])
        p.emit("gpsimd", lambda e: e.tensor_mul(out=var[:, 0:n], in0=mu[:, 0:n], in1=mu[:, 0:n]),
               reads=[bmu], writes=[bvar])
        p.emit("vector", lambda e: e.scalar_tensor_tensor(out=var[:, 0:n], in0=psS[:, n:2 * n], scalar=1.0 / D,
                                                          in1=var[:, 0:n], op0=ALU.mult, op1=ALU.subtract),
               reads=[bpsS, bvar], writes=[bvar])
        p.emit("scalar", lambda e: e.activation(out=rstd[:, 0:n], in_=var[:, 0:n], func=AF.Ln, bias=self.c_eps),
               reads=[bvar, self.bcst], writes=[brstd])
        p.emit("scalar", lambda e: e.activation(out=rstd[:, 0:n], in_=rstd[:, 0:n], func=AF.Exp, scale=self.c_nh),
               reads=[brstd, self.bcst], writes=[brstd])
        p.emit("gpsimd", lambda e: e.tensor_mul(out=mu[:, 0:n], in0=mu[:, 0:n], in1=rstd[:, 0:n]),
               reads=[brstd, bmu], writes=[bmu])
        of = out_f32 if out_f32 is not None else out_bf
        ofh, bof = of
        eng_of = lambda c: "vector"
        for c in range(DC):
            p.emit(eng_of(c), (lambda c_: lambda e: e.tensor_mul(
                out=ofh[:, c_ * n:(c_ + 1) * n], in0=xap_chunks(c_), in1=rstd[:, 0:n]))(c),
                reads=[bx, brstd], writes=[bof])
        for c in range(DC):
            p.emit(eng_of(c), (lambda c_: lambda e: e.tensor_sub(
                out=ofh[:, c_ * n:(c_ + 1) * n], in0=ofh[:, c_ * n:(c_ + 1) * n], in1=mu[:, 0:n]))(c),
                reads=[bmu, bof], writes=[bof])
        if gamma is not None:
            for c in range(DC):
                p.emit(eng_of(c), (lambda c_: lambda e: e.tensor_scalar(
                    out=ofh[:, c_ * n:(c_ + 1) * n], in0=ofh[:, c_ * n:(c_ + 1) * n],
                    scalar1=gamma[:, c_:c_ + 1], scalar2=beta[:, c_:c_ + 1],
                    op0=ALU.mult, op1=ALU.add))(c),
                    reads=[bof], writes=[bof])
        if out_f32 is not None and out_bf is not None:
            obf, bobf = out_bf
            p.emit("vector", lambda e: e.tensor_copy(out=obf[:, 0:DC * n], in_=ofh[:, 0:DC * n]),
                   reads=[bof], writes=[bobf])

    def pass_attn_a_front(self):
        import contextlib
        p, nc, i, s = self.p, self.nc, self.i, self.s
        T, NT = self.T, self.NT
        self.barrier()
        with contextlib.ExitStack() as ctx:
            S = p.slot
            wq, bwq = self.sb(ctx, "a_wq", (128, DC * D), BF16)
            wk, bwk = self.sb(ctx, "a_wk", (128, DC * D), BF16)
            wv, bwv = self.sb(ctx, "a_wv", (128, DC * D), BF16)
            p.dma("gpsimd", S("wq"), wq[:], i["wq_a"].rearrange("c p f -> p c f"), writes=[bwq])
            p.dma("gpsimd", S("wk"), wk[:], i["wk_a"].rearrange("c p f -> p c f"), writes=[bwk])
            p.dma("gpsimd", S("wv"), wv[:], i["wv_a"].rearrange("c p f -> p c f"), writes=[bwv])

            xb = [self.sb(ctx, f"a_x{j}", (128, DC * TT)) for j in range(2)]
            xslots = [S(f"a_x{j}") for j in range(2)]
            xbfb = [self.sb(ctx, f"a_xbf{j}", (128, DC * TT), BF16) for j in range(2)]
            xsq, bxsq = self.sb(ctx, "a_xsq", (128, DC * TT), BF16)
            stats = [tuple(self.sb(ctx, f"a_st{j}{nm}", (128, TT)) for nm in ("mu", "var", "rstd"))
                     for j in range(2)]
            y0bfb = [self.sb(ctx, f"a_y0bf{j}", (128, DC * TT), BF16) for j in range(2)]
            qbfb = [self.sb(ctx, f"a_qbf{j}", (128, DC * TT), BF16) for j in range(2)]
            kbuf, bkbuf = self.sb(ctx, "a_kbuf", (128, DC * 640), BF16)
            vsb, bvsb = self.sb(ctx, "a_vsb", (128, 5 * H * 65), BF16)
            expT, bexpT = self.sb(ctx, "a_expT", (128, 4096), BF16)
            attn_tmb = [self.sb(ctx, f"a_attntm{j}", (128, 1024), BF16) for j in range(2)]
            attn_fmb = [self.sb(ctx, f"a_attnfm{j}", (128, DC * TT), BF16) for j in range(2)]
            den16b = [self.sb(ctx, f"a_den{j}", (128, H)) for j in range(2)]
            den16ib = [self.sb(ctx, f"a_deni{j}", (128, H)) for j in range(2)]

            psSd = [self.ps(ctx, f"a_psS{j}", (128, 2 * TT)) for j in range(2)]
            psS, bpsS = psSd[0]
            psP, bpsP = psSd[1]
            ps_a, bps_a = psP.ap()[:, 0:TT], bpsP
            ps_b, bps_b = psP.ap()[:, TT:2 * TT], bpsP
            ps_av = [self.ps(ctx, f"a_psav{g}", (128, 4 * 65)) for g in range(4)]

            su = S("a_u")
            budram = p.buf("a_udram")
            GROUPS = [(0, 2, 4, 6), (8, 10, 12, 14), (1, 3, 5, 7), (9, 11, 13, 15)]

            p.emit("vector", lambda e: e.memset(
                fap(vsb, 64, [[1040, 5], [65, H], [1, 1]]), 1.0),
                writes=[bvsb])

            def proj_k(blk0, ntok, n_ln, y0bf, by0bf):
                for oc in range(DC):
                    pst, bpst = (ps_a, bps_a) if oc % 2 == 0 else (ps_b, bps_b)
                    for c in range(DC):
                        p.emit("tensor", (lambda oc_, c_, pst_, yb2: lambda e: e.matmul(
                            pst_[:, 0:ntok], wk[:, c_ * D + 128 * oc_: c_ * D + 128 * oc_ + 128],
                            yb2[:, c_ * n_ln: c_ * n_ln + ntok],
                            start=(c_ == 0), stop=(c_ == DC - 1)))(oc, c, pst, y0bf),
                            reads=[bwk, by0bf], writes=[bpst])
                    p.emit("scalar", (lambda oc_, pst_: lambda e: e.activation(
                        out=kbuf[:, oc_ * 640 + 128 * blk0: oc_ * 640 + 128 * blk0 + ntok],
                        in_=pst_[:, 0:ntok], func=AF.Identity, bias=self.v["kb_a"][:, oc_:oc_ + 1]))(oc, pst),
                        reads=[bpst, self.bv["kb_a"]], writes=[bkbuf])

            def proj_v(nblk, blk0, n_ln, y0bf, by0bf):
                for b_ in range(nblk):
                    for half in range(2):
                        pst, bpst = (ps_a, bps_a) if half == 0 else (ps_b, bps_b)
                        for c in range(DC):
                            p.emit("tensor", (lambda b2, h2, c_, pst_, yb2: lambda e: e.matmul(
                                pst_[:, 0:512],
                                yb2[:, c_ * n_ln + 128 * b2: c_ * n_ln + 128 * b2 + 128],
                                wv[:, c_ * D + 512 * h2: c_ * D + 512 * h2 + 512],
                                start=(c_ == 0), stop=(c_ == DC - 1)))(b_, half, c, pst, y0bf),
                                reads=[bwv, by0bf], writes=[bpst])
                        p.emit("scalar", (lambda b2, h2, pst_: lambda e: e.copy(
                            out=fap(vsb, (blk0 + b2) * 1040 + h2 * 520,
                                    [[65, 8], [1, 64]]),
                            in_=pst_[:, 0:512]))(b_, half, pst),
                            reads=[bpst], writes=[bvsb])

            x_sb, bx = xb[0]
            xbf, bxbf = xbfb[0]
            (mu, bmu), (var, bvar), (rstd, brstd) = stats[0]
            y0bf, by0bf = y0bfb[0]
            scr = (xbf, bxbf, xsq, bxsq, mu, bmu, var, bvar, rstd, brstd, psS, bpsS)
            p.dma("sync", xslots[0], x_sb[:, 0:DC * 128], i["x0h"], writes=[bx])
            self.emit_ln(ctx, (lambda xs: lambda c: xs[:, c * 128:(c + 1) * 128])(x_sb), bx, 128,
                         out_f32=None, out_bf=(y0bf, by0bf), scratch=scr)
            proj_k(0, 128, 128, y0bf, by0bf)
            proj_v(1, 0, 128, y0bf, by0bf)

            prev_epi = None
            for ti in range(NT):
                t0 = ti * TT
                par = ti % 2
                x_sb, bx = xb[par]
                xbf, bxbf = xbfb[par]
                (mu, bmu), (var, bvar), (rstd, brstd) = stats[par]
                y0bf, by0bf = y0bfb[par]
                qbf, bqbf = qbfb[par]
                attn_fm, bfm = attn_fmb[par]
                scr = (xbf, bxbf, xsq, bxsq, mu, bmu, var, bvar, rstd, brstd, psS, bpsS)

                if ti > 0:
                    p.emit("vector", lambda e: e.tensor_copy(
                        out=fap(kbuf, 0, [[640, DC], [1, 128]]),
                        in_=fap(kbuf, 512, [[640, DC], [1, 128]])),
                        reads=[bkbuf], writes=[bkbuf])
                    p.emit("vector", lambda e: e.tensor_copy(
                        out=vsb[:, 0:1040], in_=vsb[:, 4 * 1040:5 * 1040]),
                        reads=[bvsb], writes=[bvsb])
                p.dma("sync", xslots[par], x_sb[:], i["x0t"][ti], writes=[bx])
                self.emit_ln(ctx, (lambda xs: lambda c: xs[:, c * TT:(c + 1) * TT])(x_sb), bx, TT,
                             out_f32=None, out_bf=(y0bf, by0bf), scratch=scr)
                for oc in range(DC):
                    pst, bpst = (ps_a, bps_a) if oc % 2 == 0 else (ps_b, bps_b)
                    for c in range(DC):
                        p.emit("tensor", (lambda oc_, c_, pst_, yb2: lambda e: e.matmul(
                            pst_[:, 0:TT], wq[:, c_ * D + 128 * oc_: c_ * D + 128 * oc_ + 128],
                            yb2[:, c_ * TT: c_ * TT + TT],
                            start=(c_ == 0), stop=(c_ == DC - 1)))(oc, c, pst, y0bf),
                            reads=[bwq, by0bf], writes=[bpst])
                    p.emit("scalar", (lambda oc_, pst_, qb2: lambda e: e.activation(
                        out=qb2[:, oc_ * TT: (oc_ + 1) * TT], in_=pst_[:, 0:TT], func=AF.Identity,
                        scale=self.c_8th, bias=self.qb_s[:, oc_:oc_ + 1]))(oc, pst, qbf),
                        reads=[bpst, self.bqb_s, self.bcst], writes=[bqbf])
                proj_k(1, TT, TT, y0bf, by0bf)
                proj_v(4, 1, TT, y0bf, by0bf)

                def sc_group(w, gi):
                    grp = GROUPS[gi]
                    psG, bpsG = psSd[gi % 2]
                    expG = expT[:, 1024 * gi:1024 * gi + 1024]
                    pr0 = 64 * (grp[0] % 2)
                    for idx, h in enumerate(grp):
                        for half in range(2):
                            p.emit("tensor", (lambda h_, pr0_, idx_, half_, w_, psG_, qb2: lambda e: e.matmul(
                                psG_[:, 256 * idx_ + 128 * half_: 256 * idx_ + 128 * half_ + 128],
                                kbuf[pr0_:pr0_ + 64, (h_ // 2) * 640 + 128 * (w_ + half_): (h_ // 2) * 640 + 128 * (w_ + half_) + 128],
                                qb2[pr0_:pr0_ + 64, (h_ // 2) * TT + 128 * w_: (h_ // 2) * TT + 128 * w_ + 128],
                                start=True, stop=True))(h, pr0, idx, half, w, psG, qbf),
                                reads=[bkbuf, bqbf], writes=[bpsG])
                    p.emit("vector", (lambda psG_: lambda e: e.tensor_add(
                        out=fap(psG_, 128, [[256, 4], [1, 128]]),
                        in0=fap(psG_, 128, [[256, 4], [1, 128]]),
                        in1=self.mT_own4[:].rearrange("p (a b) -> p a b", a=4)))(psG),
                        reads=[bpsG, self.bmT_own4], writes=[bpsG])
                    if w == 0 and ti == 0:
                        p.emit("vector", (lambda psG_: lambda e: e.tensor_add(
                            out=fap(psG_, 0, [[256, 4], [1, 128]]),
                            in0=fap(psG_, 0, [[256, 4], [1, 128]]),
                            in1=self.mT_prev04[:].rearrange("p (a b) -> p a b", a=4)))(psG),
                            reads=[bpsG, self.bmT_prev04], writes=[bpsG])
                    p.emit("scalar", (lambda psG_, eG: lambda e: e.activation(
                        out=eG, in_=psG_[:, 0:1024],
                        func=AF.Exp, bias=self.c_m40))(psG, expG),
                        reads=[bpsG, self.bcst], writes=[bexpT])

                def av_group(w, gi):
                    grp = GROUPS[gi]
                    expG = expT[:, 1024 * gi:1024 * gi + 1024]
                    for idx, h in enumerate(grp):
                        pg, bpg = ps_av[h // 4]
                        for half in range(2):
                            p.emit("tensor", (lambda h_, idx_, half_, pg_, w_, eG: lambda e: e.matmul(
                                pg_[:, 65 * (h_ % 4): 65 * (h_ % 4) + 65],
                                eG[:, 256 * idx_ + 128 * half_: 256 * idx_ + 128 * half_ + 128],
                                fap(vsb, (w_ + half_) * 1040 + 65 * h_, [[1, 65]]),
                                start=(half_ == 0), stop=(half_ == 1)))(h, idx, half, pg, w, expG),
                                reads=[bexpT, bvsb], writes=[bpg])

                def divisions(w):
                    den16, bden = den16b[w % 2]
                    den16i, bdeni = den16ib[w % 2]
                    attn_tm, btm = attn_tmb[w % 2]
                    for g in range(4):
                        pg, bpg = ps_av[g]
                        p.emit("vector", (lambda g_, pg_, d_: lambda e: e.tensor_scalar_add(
                            out=d_[:, 4 * g_:4 * g_ + 4],
                            in0=fap(pg_, 64, [[65, 4], [1, 1]]),
                            scalar1=0.0))(g, pg, den16),
                            reads=[bpg], writes=[bden])
                    p.emit("vector", (lambda d_, di_: lambda e: e.reciprocal(out=di_[:], in_=d_[:]))(den16, den16i),
                           reads=[bden], writes=[bdeni])
                    for h in range(H):
                        pg, bpg = ps_av[h // 4]
                        if h % 2 == 0:
                            p.emit("vector", (lambda h_, pg_, di_, tm_: lambda e: e.tensor_scalar_mul(
                                out=tm_[:, 64 * h_: 64 * h_ + 64],
                                in0=pg_[:, 65 * (h_ % 4): 65 * (h_ % 4) + 64],
                                scalar1=di_[:, h_:h_ + 1]))(h, pg, den16i, attn_tm),
                                reads=[bpg, bdeni], writes=[btm])
                        else:
                            p.emit("scalar", (lambda h_, pg_, di_, tm_: lambda e: e.mul(
                                out=tm_[:, 64 * h_: 64 * h_ + 64],
                                in_=pg_[:, 65 * (h_ % 4): 65 * (h_ % 4) + 64],
                                mul=di_[:, h_:h_ + 1]))(h, pg, den16i, attn_tm),
                                reads=[bpg, bdeni], writes=[btm])

                def transposes(w):
                    attn_tm, btm = attn_tmb[w % 2]
                    psW, bpsW = psSd[w % 2]
                    psW_bf = psW.ap().bitcast(BF16)
                    for c in range(DC):
                        p.emit("tensor", (lambda c_, pb, tm_: lambda e: e.transpose(
                            pb[:, 128 * c_:128 * c_ + 128],
                            tm_[:, 128 * c_:128 * c_ + 128], self.id_bf[:]))(c, psW_bf, attn_tm),
                            reads=[btm, self.bid_bf], writes=[bpsW])
                    for c in range(DC):
                        p.emit("scalar", (lambda w_, c_, pb, fm2: lambda e: e.activation(
                            out=fm2[:, c_ * TT + 128 * w_: c_ * TT + 128 * w_ + 128],
                            in_=pb[:, 128 * c_:128 * c_ + 128], func=AF.Identity,
                            bias=self.v["vb_a"][:, c_:c_ + 1]))(w, c, psW_bf, attn_fm),
                            reads=[bpsW, self.bv["vb_a"]], writes=[bfm])

                if prev_epi is not None:
                    prev_epi()
                items = [(w, gi) for w in range(4) for gi in range(4)]
                for idx2, (w, gi) in enumerate(items):
                    sc_group(w, gi)
                    if idx2 >= 2:
                        av_group(*items[idx2 - 2])
                    if gi == 1 and w > 0:
                        divisions(w - 1)
                    if gi == 2 and w > 0:
                        transposes(w - 1)
                av_group(*items[-2])
                av_group(*items[-1])
                divisions(3)
                transposes(3)

                def epi(ti_, par_):
                    x2, bx2 = xb[par_]
                    (mu2, bmu2), (v2, bv2), (r2, br2) = stats[par_]
                    fm2, bfm2 = attn_fmb[par_]
                    xbf2, bxbf2 = xbfb[par_]
                    scr2 = (xbf2, bxbf2, xsq, bxsq, mu2, bmu2, v2, bv2, r2, br2, psS, bpsS)
                    for c in range(DC):
                        p.emit("vector", (lambda c_, x3, r3: lambda e: e.tensor_mul(
                            out=x3[:, c_ * TT:(c_ + 1) * TT], in0=x3[:, c_ * TT:(c_ + 1) * TT],
                            in1=r3[:, 0:TT]))(c, x2, r2),
                            reads=[bx2, br2], writes=[bx2])
                    for c in range(DC):
                        p.emit("gpsimd" if c % 2 else "vector", (lambda c_, x3, m3: lambda e: e.tensor_sub(
                            out=x3[:, c_ * TT:(c_ + 1) * TT], in0=x3[:, c_ * TT:(c_ + 1) * TT],
                            in1=m3[:, 0:TT]))(c, x2, mu2),
                            reads=[bx2, bmu2], writes=[bx2])
                    p.emit("vector", (lambda x3, fm3: lambda e: e.tensor_add(
                        out=x3[:], in0=x3[:], in1=fm3[:]))(x2, fm2),
                        reads=[bx2, bfm2], writes=[bx2])
                    self.emit_ln(ctx, (lambda xs: lambda c: xs[:, c * TT:(c + 1) * TT])(x2), bx2, TT,
                                 gamma=self.v["ln1g"], beta=self.v["ln1b"],
                                 out_f32=(x2, bx2), out_bf=None, scratch=scr2)
                    p.dma("gpsimd", su, s["u"][ti_], x2[:],
                          reads=[bx2], writes=[budram])
                prev_epi = (lambda t_, p_: lambda: epi(t_, p_))(ti, par)
            prev_epi()

    def pass_attn_a_ffn(self, dst):
        p, nc, i, s = self.p, self.nc, self.i, self.s
        import contextlib
        T, NT = self.T, self.NT
        self.barrier()
        with contextlib.ExitStack() as ctx:
            S = p.slot
            f1w, bf1w = self.sb(ctx, "f1w", (128, DC * FFA), BF16)
            f2w, bf2w = self.sb(ctx, "f2w", (128, (FFA // 128) * D), BF16)
            p.dma("gpsimd", S("f1w"), f1w[:], i["ff1w"].rearrange("c p f -> p c f"), writes=[bf1w])
            p.dma("gpsimd", S("f2w"), f2w[:], i["ff2w"].rearrange("c p f -> p c f"), writes=[bf2w])

            ubuf = [self.sb(ctx, f"b_u{j}", (128, DC * TT)) for j in range(2)]
            uslots = [S(f"b_u{j}") for j in range(2)]
            x0_sb, bx0 = self.sb(ctx, "b_x0", (128, DC * TT))
            ffhb = [self.sb(ctx, f"b_ffh{j}", (128, (FFA // 128) * TT), BF16) for j in range(2)]
            w_f, bw_f = self.sb(ctx, "b_w", (128, DC * TT))
            xbf, bxbf = self.sb(ctx, "b_xbf", (128, DC * TT), BF16)
            ubf, bubf = self.sb(ctx, "b_ubf", (128, DC * TT), BF16)
            mu, bmu = self.sb(ctx, "b_mu", (128, TT))
            var, bvar = self.sb(ctx, "b_var", (128, TT))
            rstd, brstd = self.sb(ctx, "b_rstd", (128, TT))
            aout, baout = w_f, bw_f

            psS, bpsS = self.ps(ctx, "b_psS", (128, 2 * TT))
            ps4 = [self.ps(ctx, f"b_ps{j}", (128, TT)) for j in range(4)]
            sx0 = S("x0_in")
            sx1, sy1 = S("x1_out"), S("y1_out")
            bx1dram, by1dram = p.buf("x1dram"), p.buf("y1dram")

            def epilogue(ti, par):
                ffh_old, bffh_old = ffhb[par]
                scr = (xbf, bxbf, ffh_old, bffh_old, mu, bmu, var, bvar, rstd, brstd, psS, bpsS)
                x1, bx1 = ubuf[par]
                self.emit_ln(ctx, (lambda w2: lambda c: w2[:, c * TT:(c + 1) * TT])(w_f), bw_f, TT,
                             gamma=self.v["ln2g"], beta=self.v["ln2b"],
                             out_f32=(aout, baout), out_bf=None, scratch=scr)
                p.emit("gpsimd", (lambda x1_: lambda e: e.tensor_add(
                    out=x1_[:], in0=x0_sb[:], in1=aout[:]))(x1),
                    reads=[bx0, baout], writes=[bx1])
                p.dma("gpsimd", sx1, dst[ti], x1[:], reads=[bx1], writes=[bx1dram])
                self.emit_ln(ctx, (lambda x2: lambda c: x2[:, c * TT:(c + 1) * TT])(x1), bx1, TT,
                             out_f32=None, out_bf=(xbf, bxbf), scratch=scr)
                p.dma("scalar", sy1, s["y1"][ti], xbf[:], reads=[bxbf], writes=[by1dram])
                p.emit("vector", lambda e: e.reduce_max(out=var[:, 0:1], in_=xbf[:], axis=AX.X),
                       reads=[bxbf], writes=[bvar])
                p.emit("vector", lambda e: e.tensor_max(out=self.mloc[:, 0:1], in0=self.mloc[:, 0:1],
                                                        in1=var[:, 0:1]),
                       reads=[bvar, self.bmloc], writes=[self.bmloc])

            for ti in range(NT):
                t0 = ti * TT
                par = ti % 2
                u_sb, bu = ubuf[par]
                ffh, bffh = ffhb[par]
                p.dma("sync", uslots[par], u_sb[:], s["u"][ti], writes=[bu])
                p.emit("vector", (lambda u2: lambda e: e.tensor_copy(out=ubf[:], in_=u2[:]))(u_sb),
                       reads=[bu], writes=[bubf])
                for fo in range(FFA // 128):
                    pst, bpst = ps4[fo % 4]
                    for c in range(DC):
                        p.emit("tensor", (lambda fo_, c_, pst_: lambda e: e.matmul(
                            pst_[:, 0:TT], f1w[:, c_ * FFA + 128 * fo_: c_ * FFA + 128 * fo_ + 128],
                            ubf[:, c_ * TT:(c_ + 1) * TT],
                            start=(c_ == 0), stop=(c_ == DC - 1)))(fo, c, pst),
                            reads=[bf1w, bubf], writes=[bpst])
                    p.emit("scalar", (lambda fo_, pst_, f2: lambda e: e.activation(
                        out=f2[:, fo_ * TT:(fo_ + 1) * TT], in_=pst_[:, 0:TT], func=AF.Relu,
                        bias=self.v["ff1b"][:, fo_:fo_ + 1]))(fo, pst, ffh),
                        reads=[bpst, self.bv["ff1b"]], writes=[bffh])
                if ti > 0:
                    epilogue(ti - 1, 1 - par)
                p.dma("sync", sx0, x0_sb[:], i["x0t"][ti], writes=[bx0])
                for do in range(DC):
                    pst, bpst = ps4[do % 4]
                    for kc in range(FFA // 128):
                        p.emit("tensor", (lambda do_, kc_, pst_, f2: lambda e: e.matmul(
                            pst_[:, 0:TT], f2w[:, kc_ * D + 128 * do_: kc_ * D + 128 * do_ + 128],
                            f2[:, kc_ * TT:(kc_ + 1) * TT],
                            start=(kc_ == 0), stop=(kc_ == FFA // 128 - 1)))(do, kc, pst, ffh),
                            reads=[bf2w, bffh], writes=[bpst])
                    p.emit("vector", (lambda do_, pst_, u2: lambda e: e.scalar_tensor_tensor(
                        out=w_f[:, do_ * TT:(do_ + 1) * TT], in0=pst_[:, 0:TT],
                        scalar=self.v["ff2b"][:, do_:do_ + 1], in1=u2[:, do_ * TT:(do_ + 1) * TT],
                        op0=ALU.add, op1=ALU.add))(do, pst, u_sb),
                        reads=[bpst, self.bv["ff2b"], bu], writes=[bw_f])
            epilogue(NT - 1, (NT - 1) % 2)

    def pass_quantize_f(self):
        p, nc, i, s = self.p, self.nc, self.i, self.s
        import contextlib
        with contextlib.ExitStack() as ctx:
            S = p.slot
            wt = [self.sb(ctx, f"qw{j}", (128, FFQ)) for j in range(2)]
            wslots = [S(f"qw_in{j}") for j in range(2)]
            q1b_ = [self.sb(ctx, f"q_q1{j}", (128, FFQ)) for j in range(2)]
            qbfb_ = [self.sb(ctx, f"q_qbf{j}", (128, FFQ), BF16) for j in range(2)]
            tq, btq = self.sb(ctx, "q_tq", (128, FFQ), BF16)
            amaxb_ = [self.sb(ctx, f"q_amax{j}", (128, 2)) for j in range(2)]
            psQ, bpsQ = self.ps(ctx, "q_ps", (128, TT))
            psQ_bf = psQ.ap().bitcast(BF16)
            sq = S("m_out")
            bm1d, bm2d = p.buf("m1Td"), p.buf("m2Td")

            def prep1(fc):
                wts, bwts = wt[fc % 2]
                q1, bq1 = q1b_[fc % 2]
                qbf, bqbf = qbfb_[fc % 2]
                amax, bamax = amaxb_[fc % 2]
                p.dma("sync", wslots[fc % 2], wts[:, 0:D], i["w1f"][fc], writes=[bwts])
                p.emit("vector", (lambda wts_, am: lambda e: e.reduce_max(
                    out=am[:, 0:1], in_=wts_[:, 0:D], axis=AX.X, apply_absolute_value=True))(wts, amax),
                    reads=[bwts], writes=[bamax])
                p.emit("vector", (lambda fc_, am: lambda e: e.tensor_single_scalar(
                    out=self.sw1[:, fc_:fc_ + 1], in_=am[:, 0:1], scalar=1.0 / 127.0, op=ALU.mult))(fc, amax),
                    reads=[bamax], writes=[self.bsw1])
                p.emit("vector", (lambda fc_, am: lambda e: e.reciprocal(
                    out=am[:, 1:2], in_=self.sw1[:, fc_:fc_ + 1]))(fc, amax),
                    reads=[self.bsw1], writes=[bamax])
                p.emit("vector", (lambda wts_, am, q1_: lambda e: e.tensor_scalar(
                    out=q1_[:, 0:D], in0=wts_[:, 0:D], scalar1=am[:, 1:2], scalar2=2.0 ** 23,
                    op0=ALU.mult, op1=ALU.add))(wts, amax, q1),
                    reads=[bwts, bamax], writes=[bq1])
                p.emit("vector", (lambda q1_, qb_: lambda e: e.tensor_single_scalar(
                    out=qb_[:, 0:D], in_=q1_[:, 0:D], scalar=2.0 ** 23, op=ALU.subtract))(q1, qbf),
                    reads=[bq1], writes=[bqbf])

            prep1(0)
            for fc in range(FFQ // 128):
                qbf, bqbf = qbfb_[fc % 2]
                if fc + 1 < FFQ // 128:
                    prep1(fc + 1)
                for j in range(DC):
                    p.emit("tensor", (lambda j_, qb_: lambda e: e.transpose(
                        psQ_bf[:, 128 * j_:128 * j_ + 128], qb_[:, 128 * j_:128 * j_ + 128],
                        self.id_bf[:]))(j, qbf),
                        reads=[bqbf, self.bid_bf], writes=[bpsQ])
                p.emit("scalar", lambda e: e.copy(out=tq[:, 0:D], in_=psQ_bf[:, 0:D]),
                       reads=[bpsQ], writes=[btq])
                p.dma("scalar", sq, bass.AP(s["m1T"].tensor, 128 * fc,
                                            [[DC * FFQ, 128], [FFQ, DC], [1, 128]]),
                      tq[:, 0:DC * 128], reads=[btq], writes=[bm1d])
            def prep2(dc):
                wts, bwts = wt[dc % 2]
                q1, bq1 = q1b_[dc % 2]
                qbf, bqbf = qbfb_[dc % 2]
                amax, bamax = amaxb_[dc % 2]
                p.dma("sync", wslots[dc % 2], wts[:], i["w2f"][dc], writes=[bwts])
                p.emit("vector", (lambda wts_, am: lambda e: e.reduce_max(
                    out=am[:, 0:1], in_=wts_[:], axis=AX.X, apply_absolute_value=True))(wts, amax),
                    reads=[bwts], writes=[bamax])
                p.emit("vector", (lambda dc_, am: lambda e: e.tensor_single_scalar(
                    out=self.sw2[:, dc_:dc_ + 1], in_=am[:, 0:1], scalar=1.0 / 127.0, op=ALU.mult))(dc, amax),
                    reads=[bamax], writes=[self.bsw2])
                p.emit("vector", (lambda dc_, am: lambda e: e.reciprocal(
                    out=am[:, 1:2], in_=self.sw2[:, dc_:dc_ + 1]))(dc, amax),
                    reads=[self.bsw2], writes=[bamax])
                p.emit("vector", (lambda wts_, am, q1_: lambda e: e.tensor_scalar(
                    out=q1_[:], in0=wts_[:], scalar1=am[:, 1:2], scalar2=2.0 ** 23,
                    op0=ALU.mult, op1=ALU.add))(wts, amax, q1),
                    reads=[bwts, bamax], writes=[bq1])
                p.emit("vector", (lambda q1_, qb_: lambda e: e.tensor_single_scalar(
                    out=qb_[:], in_=q1_[:], scalar=2.0 ** 23, op=ALU.subtract))(q1, qbf),
                    reads=[bq1], writes=[bqbf])

            prep2(0)
            for dc in range(DC):
                qbf, bqbf = qbfb_[dc % 2]
                if dc + 1 < DC:
                    prep2(dc + 1)
                for grp in range(FFQ // TT):
                    for j in range(4):
                        p.emit("tensor", (lambda grp_, j_, qb_: lambda e: e.transpose(
                            psQ_bf[:, 128 * j_:128 * j_ + 128],
                            qb_[:, TT * grp_ + 128 * j_:TT * grp_ + 128 * j_ + 128],
                            self.id_bf[:]))(grp, j, qbf),
                            reads=[bqbf, self.bid_bf], writes=[bpsQ])
                    p.emit("scalar", (lambda grp_: lambda e: e.copy(
                        out=tq[:, TT * grp_:TT * grp_ + TT], in_=psQ_bf[:, 0:TT]))(grp),
                        reads=[bpsQ], writes=[btq])
                p.dma("scalar", sq, bass.AP(s["m2T"].tensor, 128 * dc,
                                            [[(FFQ // 128) * D, 128], [D, FFQ // 128], [1, 128]]),
                      tq[:], reads=[btq], writes=[bm2d])

    def pass_sync_scales(self, site, mcol, ctx=None):
        import contextlib
        p, nc, s = self.p, self.nc, self.s
        S = p.slot
        ctx = self._sync_ctx = contextlib.ExitStack()
        mx1, bmx1 = self.sb(ctx, f"mx1_{site}", (1, 2))
        mrow, bmrow = self.sb(ctx, f"mrow_{site}", (1, 128))
        psX, bpsX = self.ps(ctx, f"psx_{site}", (128, 64))
        bcin, bcout = p.buf(), p.buf()

        mbf, bmbf = self.sb(ctx, f"mbf_{site}", (128, 1), BF16)
        p.emit("vector", lambda e: e.tensor_copy(out=mbf[:], in_=self.mloc[:, mcol:mcol + 1]),
               reads=[self.bmloc], writes=[bmbf])
        p.emit("tensor", lambda e: e.transpose(psX.ap().bitcast(BF16)[0:1, 0:128], mbf[:], self.id_bf[:]),
               reads=[bmbf, self.bid_bf], writes=[bpsX])
        p.emit("scalar", lambda e: e.copy(out=mrow[:], in_=psX.ap().bitcast(BF16)[0:1, 0:128]),
               reads=[bpsX], writes=[bmrow])
        p.emit("vector", lambda e: e.reduce_max(out=mx1[:, 0:1], in_=mrow[:], axis=AX.X),
               reads=[bmrow], writes=[bmx1])
        p.dma("sync", S(f"ci{site}"), s[f"cin{site}"], mx1[:, 0:1], reads=[bmx1], writes=[bcin])
        p.collective("AllReduce", ALU.max, [list(range(NCORES))],
                     [s[f"cin{site}"].opt()], [s[f"cout{site}"].opt()],
                     reads=[bcin], writes=[bcout])
        p.dma("sync", S(f"co{site}"), mx1[:, 0:1], s[f"cout{site}"], reads=[bcout], writes=[bmx1])
        p.emit("vector", lambda e: e.tensor_scalar(out=mx1[:, 0:1], in0=mx1[:, 0:1],
                                                   scalar1=1e-8, scalar2=1.0 / 255.0,
                                                   op0=ALU.max, op1=ALU.mult),
               reads=[bmx1], writes=[bmx1])
        p.emit("vector", lambda e: e.reciprocal(out=mx1[:, 1:2], in_=mx1[:, 0:1]),
               reads=[bmx1], writes=[bmx1])
        p.emit("tensor", lambda e: e.matmul(psX[:, 0:2], self.ones_f[0:1, :], mx1[:, 0:2],
                                            start=True, stop=True),
               reads=[bmx1, self.bones_f], writes=[bpsX])
        p.emit("scalar", (lambda st: lambda e: e.copy(out=self.qcols[:, 2 * st:2 * st + 2],
                                                      in_=psX[:, 0:2]))(site),
               reads=[bpsX], writes=[self.bqcols])
        scol = self.qcols[:, 2 * site:2 * site + 1]
        if site % 2 == 0:
            zst, bzst, bt, bbt, nb = self.zs1, self.bzs1, self.v["b1f"], self.bv["b1f"], FFQ // 128
            bqt, bbqt = self.b1q, self.bb1q
        else:
            zst, bzst, bt, bbt, nb = self.zs2, self.bzs2, self.v["b2f"], self.bv["b2f"], DC
            bqt, bbqt = self.b2q, self.bb2q
        p.emit("vector", (lambda: lambda e: e.tensor_scalar_mul(
            out=zst[:], in0=self.bsw_t(site)[0][:], scalar1=scol))(),
            reads=[self.bsw_t(site)[1], self.bqcols], writes=[bzst])
        zr, bzr = self.sb(ctx, f"zr_{site}", (128, zst.ap().shape[1]))
        p.emit("vector", lambda e: e.reciprocal(out=zr[:], in_=zst[:]),
               reads=[bzst], writes=[bzr])
        p.emit("vector", lambda e: e.tensor_mul(out=bqt[:], in0=bt[:], in1=zr[:]),
               reads=[bbt, bzr], writes=[bbqt])
        p.emit("vector", lambda e: e.tensor_scalar(out=bqt[:], in0=bqt[:], scalar1=2.0 ** 23,
                                                   scalar2=2.0 ** 23, op0=ALU.add, op1=ALU.subtract),
               reads=[bbqt], writes=[bbqt])
        p.emit("vector", lambda e: e.tensor_mul(out=bqt[:], in0=bqt[:], in1=zst[:]),
               reads=[bbqt, bzst], writes=[bbqt])
        ctx.close()

    def bsw_t(self, site):
        return (self.sw1, self.bsw1) if site % 2 == 0 else (self.sw2, self.bsw2)

    def pass_f_l1(self, y_dram, site, mcol):
        p, nc, s = self.p, self.nc, self.s
        import contextlib
        T, NT = self.T, self.NT
        self.barrier()
        with contextlib.ExitStack() as ctx:
            S = p.slot
            m1t, bm1t = self.sb(ctx, "l1_m1t", (128, DC * FFQ), BF16)
            p.dma("sync", S("m1t"), m1t[:], s["m1T"], writes=[bm1t])
            ybuf = [self.sb(ctx, f"l1_y{j}", (128, DC * TT), BF16) for j in range(2)]
            yslots = [S(f"l1_y{j}") for j in range(2)]
            q1b = [self.sb(ctx, f"l1_q1{j}", (128, DC * TT)) for j in range(2)]
            n1b = [self.sb(ctx, f"l1_n1{j}", (128, DC * TT), BF16) for j in range(2)]
            hst, bhst = self.sb(ctx, "l1_hst", (128, 8 * TT), BF16)
            hstslots = [S(f"l1_hs{j}") for j in range(2)]
            mxc, bmxc = self.sb(ctx, "l1_mx", (128, FFQ // 128 + 1))
            ps4 = [self.ps(ctx, f"l1_ps{j}", (128, TT)) for j in range(4)]
            bhd = p.buf("l1_hd")
            rscol = self.qcols[:, 2 * site + 1:2 * site + 2]

            pend = []

            def prep(tj):
                y_sb, by = ybuf[tj % 2]
                q1, bq1 = q1b[tj % 2]
                n1, bn1 = n1b[tj % 2]
                p.dma("sync", yslots[tj % 2], y_sb[:], y_dram[tj], writes=[by])
                p.emit("vector", (lambda y2_, q2_: lambda e: e.tensor_scalar(
                    out=q2_[:], in0=y2_[:], scalar1=rscol, scalar2=2.0 ** 23,
                    op0=ALU.mult, op1=ALU.add))(y_sb, q1),
                    reads=[by, self.bqcols], writes=[bq1])
                p.emit("vector", (lambda q2_, n2_: lambda e: e.tensor_scalar(
                    out=n2_[:], in0=q2_[:], scalar1=2.0 ** 23, scalar2=0.0,
                    op0=ALU.subtract, op1=ALU.max))(q1, n1), reads=[bq1], writes=[bn1])

            prep(0)
            for ti in range(NT):
                t0 = ti * TT
                par = ti % 2
                n1, bn1 = n1b[par]
                if ti + 1 < NT:
                    prep(ti + 1)
                for fn in pend:
                    fn()
                pend = []
                for fo in range(FFQ // 128):
                    pst, bpst = ps4[fo % 4]
                    for c in range(DC):
                        p.emit("tensor", (lambda fo_, c_, pst_, n2_: lambda e: e.matmul(
                            pst_[:, 0:TT], m1t[:, c_ * FFQ + 128 * fo_: c_ * FFQ + 128 * fo_ + 128],
                            n2_[:, c_ * TT:(c_ + 1) * TT],
                            start=(c_ == 0), stop=(c_ == DC - 1)))(fo, c, pst, n1),
                            reads=[bm1t, bn1], writes=[bpst])
                    hoff = (fo % 4) * TT + ((fo // 4) % 2) * 4 * TT
                    p.emit("scalar", (lambda fo_, pst_, ho: lambda e: e.activation(
                        out=hst[:, ho:ho + TT], in_=pst_[:, 0:TT],
                        func=AF.Relu, scale=self.zs1[:, fo_:fo_ + 1],
                        bias=self.b1q[:, fo_:fo_ + 1]))(fo, pst, hoff),
                        reads=[bpst, self.bzs1, self.bb1q], writes=[bhst])
                    if fo % 4 == 3:
                        grp = fo // 4
                        p.emit("vector", (lambda g_: lambda e: e.reduce_max(
                            out=mxc[:, g_:g_ + 1], in_=hst[:, (g_ % 2) * 4 * TT:(g_ % 2) * 4 * TT + 4 * TT],
                            axis=AX.X))(grp),
                            reads=[bhst], writes=[bmxc])
                        p.dma("gpsimd", hstslots[grp % 2],
                              s["h"][ti][:, (fo - 3) * TT:(fo + 1) * TT],
                              hst[:, (grp % 2) * 4 * TT:(grp % 2) * 4 * TT + 4 * TT],
                              reads=[bhst], writes=[bhd])
                pend.append(lambda: p.emit("vector", lambda e: e.reduce_max(
                    out=mxc[:, FFQ // 128:FFQ // 128 + 1], in_=mxc[:, 0:FFQ // 512], axis=AX.X),
                    reads=[bmxc], writes=[bmxc]))
                pend.append(lambda: p.emit("vector", lambda e: e.tensor_max(
                    out=self.mloc[:, mcol:mcol + 1], in0=self.mloc[:, mcol:mcol + 1],
                    in1=mxc[:, FFQ // 128:FFQ // 128 + 1]),
                    reads=[bmxc, self.bmloc], writes=[self.bmloc]))
            for fn in pend:
                fn()

    def pass_f_l2(self, x_dram, dst_dram, site, y_out=None, mcol=None):
        p, nc, s = self.p, self.nc, self.s
        import contextlib
        T, NT = self.T, self.NT
        self.barrier()
        with contextlib.ExitStack() as ctx:
            S = p.slot
            m2t, bm2t = self.sb(ctx, "l2_m2t", (128, (FFQ // 128) * D), BF16)
            p.dma("sync", S("m2t"), m2t[:], s["m2T"], writes=[bm2t])
            hf, bhfl = self.sb(ctx, "l2_hf", (128, (FFQ // 256) * TT), BF16)
            hfslot = S("l2_hf")
            q1, bq1 = self.sb(ctx, "l2_q1", (128, 4 * TT))
            n2full, _bn2 = self.sb(ctx, "l2_n2", (128, (FFQ // 128) * TT), BF16)
            bn2 = [p.buf(f"l2_n2c{k}") for k in range(FFQ // 128)]
            xb2 = [self.sb(ctx, f"l2_x{j}", (128, DC * TT)) for j in range(2)]
            xslots2 = [S(f"l2_x{j}") for j in range(2)]
            xn, bxn = self.sb(ctx, "l2_xn", (128, DC * TT))
            xbf, bxbf = self.sb(ctx, "l2_xbf", (128, DC * TT), BF16)
            xsq, bxsq = self.sb(ctx, "l2_xsq", (128, DC * TT), BF16)
            mu, bmu = self.sb(ctx, "l2_mu", (128, TT))
            var, bvar = self.sb(ctx, "l2_var", (128, TT))
            rstd, brstd = self.sb(ctx, "l2_rstd", (128, TT))
            yb, byb = xbf, bxbf
            mxc, bmxc = self.sb(ctx, "l2_mx", (128, 1))
            ps4 = [self.ps(ctx, f"l2_ps{j}", (128, TT)) for j in range(4)]
            psS, bpsS = self.ps(ctx, "l2_psS", (128, 2 * TT))
            scr = (xbf, bxbf, xsq, bxsq, mu, bmu, var, bvar, rstd, brstd, psS, bpsS)
            sxd, syd = S("l2_xd"), S("l2_yd")
            bxd, byd = p.buf("l2_xd"), p.buf("l2_yd")
            rscol = self.qcols[:, 2 * site + 1:2 * site + 2]

            def epilogue(ti, par):
                x_sb, bxs = xb2[par]
                p.emit("vector", (lambda x2: lambda e: e.tensor_add(
                    out=xn[:], in0=xn[:], in1=x2[:]))(x_sb),
                    reads=[bxs, bxn], writes=[bxn])
                p.dma("gpsimd", sxd, dst_dram[ti], xn[:],
                      reads=[bxn], writes=[bxd])
                if y_out is not None:
                    self.emit_ln(ctx, lambda c: xn[:, c * TT:(c + 1) * TT], bxn, TT,
                                 out_f32=None, out_bf=(yb, byb), scratch=scr)
                    p.dma("scalar", syd, y_out[ti], yb[:],
                          reads=[byb], writes=[byd])
                    if mcol is not None:
                        p.emit("vector", lambda e: e.reduce_max(out=mxc[:], in_=yb[:], axis=AX.X),
                               reads=[byb], writes=[bmxc])
                        p.emit("vector", lambda e: e.tensor_max(
                            out=self.mloc[:, mcol:mcol + 1], in0=self.mloc[:, mcol:mcol + 1], in1=mxc[:]),
                            reads=[bmxc, self.bmloc], writes=[self.bmloc])

            for ti in range(NT):
                t0 = ti * TT
                par = ti % 2
                x_sb, bxs = xb2[par]
                p.dma("sync", xslots2[par], x_sb[:], x_dram[ti], writes=[bxs])
                NH = FFQ // 256
                for half in range(2):
                    p.dma("sync", hfslot, hf[:],
                          s["h"][ti][:, half * NH * TT:(half + 1) * NH * TT],
                          writes=[bhfl])
                    for j4 in range(NH // 4):
                        kc0 = half * NH + j4 * 4
                        p.emit("vector", (lambda j4_: lambda e: e.tensor_scalar(
                            out=q1[:], in0=hf[:, j4_ * 4 * TT:(j4_ + 1) * 4 * TT], scalar1=rscol,
                            scalar2=2.0 ** 23, op0=ALU.mult, op1=ALU.add))(j4),
                            reads=[bhfl, self.bqcols], writes=[bq1])
                        p.emit("vector", (lambda kc0_: lambda e: e.tensor_single_scalar(
                            out=n2full[:, kc0_ * TT:(kc0_ + 4) * TT], in_=q1[:],
                            scalar=2.0 ** 23, op=ALU.subtract))(kc0),
                            reads=[bq1], writes=[bn2[kc0], bn2[kc0 + 1], bn2[kc0 + 2], bn2[kc0 + 3]])
                if ti > 0:
                    epilogue(ti - 1, 1 - par)
                for do in range(DC):
                    pst, bpst = ps4[do % 4]
                    for kc in range(FFQ // 128):
                        p.emit("tensor", (lambda do_, kc_, pst_: lambda e: e.matmul(
                            pst_[:, 0:TT], m2t[:, kc_ * D + 128 * do_: kc_ * D + 128 * do_ + 128],
                            n2full[:, kc_ * TT:(kc_ + 1) * TT],
                            start=(kc_ == 0), stop=(kc_ == FFQ // 128 - 1)))(do, kc, pst),
                            reads=[bm2t, bn2[kc]], writes=[bpst])
                    p.emit("scalar", (lambda do_, pst_: lambda e: e.activation(
                        out=xn[:, do_ * TT:(do_ + 1) * TT], in_=pst_[:, 0:TT], func=AF.Identity,
                        scale=self.zs2[:, do_:do_ + 1], bias=self.b2q[:, do_:do_ + 1]))(do, pst),
                        reads=[bpst, self.bzs2, self.bb2q], writes=[bxn])
            epilogue(NT - 1, (NT - 1) % 2)

    def pass_attn_b_front(self):
        p, nc, i, s = self.p, self.nc, self.i, self.s
        import contextlib
        T, NT = self.T, self.NT
        self.barrier()
        with contextlib.ExitStack() as ctx:
            S = p.slot
            wq, bwq = self.sb(ctx, "bf_wq", (128, DC * D), BF16)
            wk, bwk = self.sb(ctx, "bf_wk", (128, DC * D), BF16)
            wv, bwv = self.sb(ctx, "bf_wv", (128, DC * D), BF16)
            p.dma("gpsimd", S("bwq"), wq[:], i["wq_b"].rearrange("c p f -> p c f"), writes=[bwq])
            p.dma("gpsimd", S("bwk"), wk[:], i["wk_b"].rearrange("c p f -> p c f"), writes=[bwk])
            p.dma("gpsimd", S("bwv"), wv[:], i["wv_b"].rearrange("c p f -> p c f"), writes=[bwv])

            ybP = [self.sb(ctx, f"bf_y{j}", (128, DC * TT), BF16) for j in range(2)]
            yslotsP = [S(f"bf_y{j}") for j in range(2)]
            xbf, bxbf = self.sb(ctx, "bf_xbf", (128, DC * TT), BF16)
            xsq, bxsq = self.sb(ctx, "bf_xsq", (128, DC * TT), BF16)
            mu, bmu = self.sb(ctx, "bf_mu", (128, TT))
            var, bvar = self.sb(ctx, "bf_var", (128, TT))
            rstd, brstd = self.sb(ctx, "bf_rstd", (128, TT))
            hbP = [self.sb(ctx, f"bf_hb{j}", (128, DC * TT), BF16) for j in range(2)]
            qt, bqt = self.sb(ctx, "bf_qt", (128, DC * TT), BF16)
            kt, bkt = self.sb(ctx, "bf_kt", (128, DC * TT), BF16)
            qf, bqf = self.sb(ctx, "bf_qf", (128, DC * TT), BF16)
            kf, bkf = self.sb(ctx, "bf_kf", (128, DC * TT), BF16)
            rb, brb = self.sb(ctx, "bf_rb", (128, TT), BF16)
            eb, beb = self.sb(ctx, "bf_eb", (128, TT), BF16)
            vt, bvt = self.sb(ctx, "bf_vt", (128, (TT // 128) * D), BF16)
            ktm_sb, bktm = self.sb(ctx, "bf_ktm", (128, (TT // 128) * D), BF16)
            stA, bstA = self.sb(ctx, "bf_stA", (64, 16 * 65))

            ps_a, bps_a = self.ps(ctx, "bf_psA", (128, TT))
            ps_b, bps_b = self.ps(ctx, "bf_psB", (128, TT))
            psS, bpsS = self.ps(ctx, "bf_psS", (128, 2 * TT))
            psKV, bpsKV = self.ps(ctx, "bf_psKV", (64, 1024))
            psKS, bpsKS = self.ps(ctx, "bf_psKS", (64, 16))
            scr = (xbf, bxbf, xsq, bxsq, mu, bmu, var, bvar, rstd, brstd, psS, bpsS)
            sq, sk, sktm, svt = S("bf_q"), S("bf_k"), S("bf_ktm"), S("bf_vt")
            bqd, bkd, bktd, bvd = p.buf(), p.buf(), p.buf(), p.buf()
            sgin = S("bf_gin")
            bgin = p.buf()
            NBLK = TT // 128


            CUTB = getattr(self, "cutb", 99)

            def prep(tj):
                y_sb, by = ybP[tj % 2]
                hb_, bhb_ = hbP[tj % 2]
                p.dma("sync", yslotsP[tj % 2], y_sb[:], s["y2"][tj], writes=[by])
                self.emit_ln(ctx, (lambda y2_: lambda c: y2_[:, c * TT:(c + 1) * TT])(y_sb), by, TT,
                             gamma=self.v["g1"], beta=self.v["be1"],
                             out_f32=None, out_bf=(hb_, bhb_), scratch=scr)

            prep(0)
            for ti in range(NT):
                t0 = ti * TT
                hb, bhb = hbP[ti % 2]
                if ti + 1 < NT:
                    prep(ti + 1)
                if CUTB < 2:
                    continue
                for which, wsb, bwsb, dst, bdst in (("q", wq, bwq, qt, bqt), ("k", wk, bwk, kt, bkt)):
                    for oc in range(DC):
                        pst, bpst = (ps_a, bps_a) if oc % 2 == 0 else (ps_b, bps_b)
                        for c in range(DC):
                            p.emit("tensor", (lambda oc_, c_, pst_, wsb_, hb_: lambda e: e.matmul(
                                pst_[:, 0:TT], wsb_[:, c_ * D + 128 * oc_: c_ * D + 128 * oc_ + 128],
                                hb_[:, c_ * TT:(c_ + 1) * TT],
                                start=(c_ == 0), stop=(c_ == DC - 1)))(oc, c, pst, wsb, hb),
                                reads=[bwsb, bhb], writes=[bpst])
                        p.emit("scalar", (lambda oc_, pst_, dst_: lambda e: e.copy(
                            out=dst_[:, oc_ * TT:(oc_ + 1) * TT], in_=pst_[:, 0:TT]))(oc, pst, dst),
                            reads=[bpst], writes=[bdst])
                if CUTB < 3:
                    continue
                for which, src_t, bsrc, dst, bdst, sd, bdd in (
                        ("q", qt, bqt, qf, bqf, sq, bqd), ("k", kt, bkt, kf, bkf, sk, bkd)):
                    for c in range(DC):
                        pst, bpst = (ps_a, bps_a) if c % 2 == 0 else (ps_b, bps_b)
                        p.emit("tensor", (lambda c_, pst_, src2: lambda e: e.matmul(
                            pst_[:, 0:TT], self.projblk[:], src2[:, c_ * TT:(c_ + 1) * TT],
                            start=True, stop=True))(c, pst, src_t),
                            reads=[self.bprojblk, bsrc], writes=[bpst])
                        if not getattr(self, "skip_exp", False):
                            p.emit("scalar", (lambda pst_: lambda e: e.activation(
                                out=eb[:], in_=pst_[:, 0:TT], func=AF.Exp))(pst),
                                reads=[bpst], writes=[beb])
                        if not getattr(self, "skip_relu", False):
                            p.emit("vector", (lambda pst_: lambda e: e.tensor_single_scalar(
                                out=rb[:], in_=pst_[:, 0:TT], scalar=0.0, op=ALU.max))(pst),
                                reads=[bpst], writes=[brb])
                        if not getattr(self, "skip_stt", False):
                            p.emit("vector", (lambda c_, dst_: lambda e: e.scalar_tensor_tensor(
                                out=dst_[:, c_ * TT:(c_ + 1) * TT], in0=eb[:], scalar=1.0, in1=rb[:],
                                op0=ALU.min, op1=ALU.add))(c, dst),
                                reads=[beb, brb], writes=[bdst])
                    if not getattr(self, "skip_spill", False):
                        p.dma("scalar", sd, (s["qfm"] if which == "q" else s["kfm"])[ti],
                              dst[:], reads=[bdst], writes=[bdd])
                if CUTB < 4:
                    continue
                for b_ in range(NBLK):
                    for half in range(2):
                        pst, bpst = (ps_a, bps_a) if half == 0 else (ps_b, bps_b)
                        for c in range(DC):
                            p.emit("tensor", (lambda b2, h2, c_, pst_, hb_: lambda e: e.matmul(
                                pst_[:, 0:512],
                                hb_[:, c_ * TT + 128 * b2: c_ * TT + 128 * b2 + 128],
                                wv[:, c_ * D + 512 * h2: c_ * D + 512 * h2 + 512],
                                start=(c_ == 0), stop=(c_ == DC - 1)))(b_, half, c, pst, hb),
                                reads=[bwv, bhb], writes=[bpst])
                        p.emit("scalar", (lambda b2, h2, pst_: lambda e: e.copy(
                            out=vt[:, b2 * D + 512 * h2: b2 * D + 512 * h2 + 512],
                            in_=pst_[:, 0:512]))(b_, half, pst),
                            reads=[bpst], writes=[bvt])
                p.dma("scalar", svt, s["vtm"][ti], vt[:], reads=[bvt], writes=[bvd])
                if CUTB < 5:
                    continue
                for b_ in range(NBLK):
                    for c in range(DC):
                        p.emit("tensor", (lambda b2, c_: lambda e: e.transpose(
                            psS.ap().bitcast(BF16)[:, 128 * c_:128 * c_ + 128],
                            kf[:, c_ * TT + 128 * b2: c_ * TT + 128 * b2 + 128],
                            self.id_bf[:]))(b_, c),
                            reads=[bkf, self.bid_bf], writes=[bpsS])
                    p.emit("scalar", (lambda b2: lambda e: e.copy(
                        out=ktm_sb[:, b2 * D:(b2 + 1) * D], in_=psS.ap().bitcast(BF16)[:, 0:D]))(b_),
                        reads=[bpsS], writes=[bktm])
                    first_blk = (ti == 0 and b_ == 0)
                    last_blk = (ti == NT - 1 and b_ == NBLK - 1)
                    for h in range(H):
                        kv_start = first_blk and h in (0, 8)
                        kv_stop = last_blk and h in (7, 15)
                        p.emit("tensor", (lambda b2, h_, st_, sp_: lambda e: e.matmul(
                            psKV[:, 64 * h_:64 * h_ + 64],
                            ktm_sb[:, b2 * D + 64 * h_: b2 * D + 64 * h_ + 64],
                            vt[:, b2 * D + 64 * h_: b2 * D + 64 * h_ + 64],
                            start=st_, stop=sp_))(b_, h, kv_start, kv_stop),
                            reads=[bktm, bvt], writes=[bpsKV])
                        p.emit("tensor", (lambda b2, h_, st_, sp_: lambda e: e.matmul(
                            psKS[:, h_:h_ + 1],
                            ktm_sb[:, b2 * D + 64 * h_: b2 * D + 64 * h_ + 64],
                            self.ones_bf[:, 0:1],
                            start=st_, stop=sp_))(b_, h, first_blk and h == 0, last_blk and h == 15),
                            reads=[bktm, self.bones], writes=[bpsKS])
                p.dma("scalar", sktm, s["ktm"][ti], ktm_sb[:], reads=[bktm], writes=[bktd])

            if CUTB < 6:
                return
            p.emit("scalar", lambda e: e.copy(
                out=fap(stA, 0, [[65, 8], [1, 64]]),
                in_=fap(psKV, 0, [[128, 8], [1, 64]])),
                reads=[bpsKV], writes=[bstA])
            p.emit("scalar", lambda e: e.copy(
                out=fap(stA, 520, [[65, 8], [1, 64]]),
                in_=fap(psKV, 64, [[128, 8], [1, 64]])),
                reads=[bpsKV], writes=[bstA])
            p.emit("vector", lambda e: e.tensor_scalar_add(
                out=fap(stA, 64, [[65, 8], [1, 1]]),
                in0=fap(psKS, 0, [[2, 8], [1, 1]]), scalar1=0.0),
                reads=[bpsKS], writes=[bstA])
            p.emit("vector", lambda e: e.tensor_scalar_add(
                out=fap(stA, 520 + 64, [[65, 8], [1, 1]]),
                in0=fap(psKS, 1, [[2, 8], [1, 1]]), scalar1=0.0),
                reads=[bpsKS], writes=[bstA])
            p.dma("sync", sgin, s["gin"], stA[:], reads=[bstA], writes=[bgin])
            bgout = p.buf("gout")
            p.collective("AllGather", ALU.bypass, [list(range(NCORES))],
                         [s["gin"].opt()], [s["gout"].opt()],
                         reads=[bgin], writes=[bgout])
            self.bgout = bgout

    def pass_attn_b_scan(self):
        p, nc, i, s = self.p, self.nc, self.i, self.s
        import contextlib
        T, NT = self.T, self.NT
        NBLK = TT // 128
        self.barrier()
        with contextlib.ExitStack() as ctx:
            S = p.slot
            wo, bwo = self.sb(ctx, "sc_wo", (128, DC * D), BF16)
            p.dma("gpsimd", S("wo"), wo[:], i["wo_b"].rearrange("c p f -> p c f"), writes=[bwo])
            qfm, bqfm = self.sb(ctx, "sc_q", (128, DC * TT), BF16)
            kfm, bkfm = self.sb(ctx, "sc_k", (128, DC * TT), BF16)
            ktm, bktm = self.sb(ctx, "sc_ktm", (128, NBLK * D), BF16)
            vplain, bvplain = self.sb(ctx, "sc_vp", (128, NBLK * D), BF16)
            vt65, bvt65 = self.sb(ctx, "sc_v65", (128, NBLK * H * 65), BF16)
            y2t, by2 = self.sb(ctx, "sc_y2", (128, DC * TT), BF16)
            KVr, bKVr = self.sb(ctx, "sc_kvr", (64, 16 * 65))
            stbfs = [self.sb(ctx, f"sc_stbf{j}", (128, 16 * 65), BF16) for j in range(TT // 128)]
            gacc = [self.sb(ctx, f"sc_ga{j}", (64, 16 * 65)) for j in range(2)]
            grk, bgrk = self.sb(ctx, "sc_grk", (64, 16 * 65))
            mskd, bmskd = self.sb(ctx, "sc_mskd", (128, 2 * 512), BF16)
            attn_tmb = [self.sb(ctx, f"sc_atm{j}", (128, 1024), BF16) for j in range(2)]
            attn_fm, bfm = self.sb(ctx, "sc_afm", (128, DC * TT), BF16)
            den16b = [self.sb(ctx, f"sc_den{j}", (128, H)) for j in range(2)]
            den16ib = [self.sb(ctx, f"sc_deni{j}", (128, H)) for j in range(2)]
            xo, bxo = self.sb(ctx, "sc_xo", (128, DC * TT))
            ps_a, bps_a = self.ps(ctx, "sc_psA", (128, TT))
            ps_b, bps_b = self.ps(ctx, "sc_psB", (128, TT))
            psN = [self.ps(ctx, f"sc_psN{g}", (128, 4 * 65)) for g in range(2)]
            psSt, bpsSt = self.ps(ctx, "sc_psSt", (64, 1024))
            psKS, bpsKS = self.ps(ctx, "sc_psKS", (64, 16))
            psT, bpsT = self.ps(ctx, "sc_psT", (128, 512))
            sq, sk, sktm, sv, sy2 = S("sc_q"), S("sc_k"), S("sc_ktm"), S("sc_v"), S("sc_y2")
            sxo = S("sc_xo")
            sod = S("sc_odd")
            bxod = p.buf()
            sgo = S("sc_go")

            p.emit("vector", lambda e: e.memset(gacc[0][0][:], 0.0), writes=[gacc[0][1]])
            for r in range(NCORES):
                src_t, bsrc = gacc[r % 2]
                dst_t, bdst = gacc[(r + 1) % 2]
                p.dma("sync", sgo, grk[:], s["gout"][r * 64:(r + 1) * 64, :],
                      reads=[self.bgout], writes=[bgrk])
                p.emit("vector", (lambda r_, s_, d_: lambda e: e.scalar_tensor_tensor(
                    out=d_[:], in0=grk[:], scalar=self.v["wpref"][0:64, r_:r_ + 1],
                    in1=s_[:], op0=ALU.mult, op1=ALU.add))(r, src_t, dst_t),
                    reads=[bgrk, bsrc, self.bv["wpref"]], writes=[bdst])
            p.emit("vector", lambda e: e.tensor_copy(out=KVr[:], in_=gacc[NCORES % 2][0][:]),
                   reads=[gacc[NCORES % 2][1]], writes=[bKVr])
            for stbf_j, bstbf_j in stbfs:
                p.emit("vector", (lambda s2: lambda e: e.memset(s2[:], 0.0))(stbf_j),
                       writes=[bstbf_j])
            p.emit("vector", lambda e: e.memset(
                fap(vt65, 64, [[1040, NBLK], [65, H], [1, 1]]), 1.0),
                writes=[bvt65])

            def upd_statebf(ci):
                stbf_j, bstbf_j = stbfs[ci]
                p.emit("vector", (lambda s2: lambda e: e.tensor_copy(
                    out=s2[0:64, 0:520], in_=KVr[0:64, 0:520]))(stbf_j),
                    reads=[bKVr], writes=[bstbf_j])
                p.dma("gpsimd", sod, stbf_j[64:128, 520:1040], KVr[0:64, 520:1040],
                      reads=[bKVr], writes=[bstbf_j])

            for ti in range(NT):
                t0 = ti * TT
                p.dma("sync", sq, qfm[:], s["qfm"][ti], writes=[bqfm])
                p.dma("sync", sk, kfm[:], s["kfm"][ti], writes=[bkfm])
                p.dma("sync", sktm, ktm[:], s["ktm"][ti], writes=[bktm])
                p.dma("sync", sv, vplain[:], s["vtm"][ti], writes=[bvplain])
                p.dma("scalar", sy2, y2t[:], s["y2"][ti], writes=[by2])
                p.emit("vector", lambda e: e.tensor_copy(
                    out=fap(vt65, 0, [[1040, NBLK], [65, H], [1, 64]]),
                    in_=vplain[:].rearrange("p (b f) -> p b f", b=NBLK).rearrange("p b (h e) -> p b h e", h=H)),
                    reads=[bvplain], writes=[bvt65])
                for ch in range(NBLK):
                    upd_statebf(ch)
                    for h in range(H):
                        p.emit("tensor", (lambda h_, ch_: lambda e: e.matmul(
                            psSt[:, 64 * h_:64 * h_ + 64],
                            ktm[:, ch_ * D + 64 * h_: ch_ * D + 64 * h_ + 64],
                            vplain[:, ch_ * D + 64 * h_: ch_ * D + 64 * h_ + 64],
                            start=True, stop=True))(h, ch),
                            reads=[bktm, bvplain], writes=[bpsSt])
                        p.emit("tensor", (lambda h_, ch_: lambda e: e.matmul(
                            psKS[:, h_:h_ + 1],
                            ktm[:, ch_ * D + 64 * h_: ch_ * D + 64 * h_ + 64],
                            self.ones_bf[:, 0:1],
                            start=True, stop=True))(h, ch),
                            reads=[bktm, self.bones], writes=[bpsKS])
                    p.emit("vector", lambda e: e.tensor_add(
                        out=fap(KVr, 0, [[65, 8], [1, 64]]),
                        in0=fap(KVr, 0, [[65, 8], [1, 64]]),
                        in1=fap(psSt, 0, [[128, 8], [1, 64]])),
                        reads=[bpsSt, bKVr], writes=[bKVr])
                    p.emit("vector", lambda e: e.tensor_add(
                        out=fap(KVr, 520, [[65, 8], [1, 64]]),
                        in0=fap(KVr, 520, [[65, 8], [1, 64]]),
                        in1=fap(psSt, 64, [[128, 8], [1, 64]])),
                        reads=[bpsSt, bKVr], writes=[bKVr])
                    p.emit("vector", lambda e: e.tensor_add(
                        out=fap(KVr, 64, [[65, 8], [1, 1]]),
                        in0=fap(KVr, 64, [[65, 8], [1, 1]]),
                        in1=fap(psKS, 0, [[2, 8], [1, 1]])),
                        reads=[bpsKS, bKVr], writes=[bKVr])
                    p.emit("vector", lambda e: e.tensor_add(
                        out=fap(KVr, 520 + 64, [[65, 8], [1, 1]]),
                        in0=fap(KVr, 520 + 64, [[65, 8], [1, 1]]),
                        in1=fap(psKS, 1, [[2, 8], [1, 1]])),
                        reads=[bpsKS, bKVr], writes=[bKVr])
                GROUPS = [(0, 2, 4, 6), (8, 10, 12, 14), (1, 3, 5, 7), (9, 11, 13, 15)]

                def sc_group(ch, gi):
                    grp = GROUPS[gi]
                    pst, bpst = (ps_a, bps_a) if gi % 2 == 0 else (ps_b, bps_b)
                    pr0 = 64 * (grp[0] % 2)
                    for idx, h in enumerate(grp):
                        p.emit("tensor", (lambda h_, pr0_, pst_, idx_, ch_: lambda e: e.matmul(
                            pst_[:, 128 * idx_:128 * idx_ + 128],
                            kfm[pr0_:pr0_ + 64, (h_ // 2) * TT + 128 * ch_:(h_ // 2) * TT + 128 * ch_ + 128],
                            qfm[pr0_:pr0_ + 64, (h_ // 2) * TT + 128 * ch_:(h_ // 2) * TT + 128 * ch_ + 128],
                            start=True, stop=True))(h, pr0, pst, idx, ch),
                            reads=[bkfm, bqfm], writes=[bpst])
                    p.emit("vector", (lambda gi_, pst_: lambda e: e.tensor_mul(
                        out=mskd[:, 512 * (gi_ % 2):512 * (gi_ % 2) + 512],
                        in0=pst_[:, 0:512], in1=self.mTb4[:]))(gi, pst),
                        reads=[bpst, self.bmTb4], writes=[bmskd])

                def num_group(ch, gi):
                    grp = GROUPS[gi]
                    pn, bpn = psN[gi % 2]
                    for idx, h in enumerate(grp):
                        base = 65 * (h // 2) if h % 2 == 0 else 520 + 65 * (h // 2)
                        p.emit("tensor", (lambda h_, gi_, pn_, idx_, ch_: lambda e: e.matmul(
                            pn_[:, 65 * idx_:65 * idx_ + 65],
                            mskd[:, 512 * (gi_ % 2) + 128 * idx_:512 * (gi_ % 2) + 128 * idx_ + 128],
                            fap(vt65, ch_ * 1040 + 65 * h_, [[1, 65]]),
                            start=True, stop=False))(h, gi, pn, idx, ch),
                            reads=[bmskd, bvt65], writes=[bpn])
                        p.emit("tensor", (lambda h_, pn_, idx_, base_, ch_, st2: lambda e: e.matmul(
                            pn_[:, 65 * idx_:65 * idx_ + 65],
                            qfm[:, (h_ // 2) * TT + 128 * ch_:(h_ // 2) * TT + 128 * ch_ + 128],
                            st2[:, base_:base_ + 65],
                            start=False, stop=True))(h, pn, idx, base, ch, stbfs[ch][0]),
                            reads=[bqfm, stbfs[ch][1]], writes=[bpn])

                def div_group(ch, gi):
                    grp = GROUPS[gi]
                    pn, bpn = psN[gi % 2]
                    den16, bden = den16b[ch % 2]
                    den16i, bdeni = den16ib[ch % 2]
                    attn_tm, btm = attn_tmb[ch % 2]
                    p.emit("vector", (lambda pn_, d_, gi_: lambda e: e.tensor_scalar_add(
                        out=d_[:, 4 * gi_:4 * gi_ + 4], in0=fap(pn_, 64, [[65, 4], [1, 1]]),
                        scalar1=1e-6))(pn, den16, gi),
                        reads=[bpn], writes=[bden])
                    p.emit("vector", (lambda d_, di_, gi_: lambda e: e.reciprocal(
                        out=di_[:, 4 * gi_:4 * gi_ + 4], in_=d_[:, 4 * gi_:4 * gi_ + 4]))(den16, den16i, gi),
                        reads=[bden], writes=[bdeni])
                    for idx, h in enumerate(grp):
                        eng = "vector" if idx % 2 == 0 else "scalar"
                        if eng == "vector":
                            p.emit("vector", (lambda h_, idx_, pn_, di_, tm_, gi_: lambda e: e.tensor_scalar_mul(
                                out=tm_[:, 64 * h_:64 * h_ + 64],
                                in0=pn_[:, 65 * idx_:65 * idx_ + 64],
                                scalar1=di_[:, 4 * gi_ + idx_:4 * gi_ + idx_ + 1]))(h, idx, pn, den16i, attn_tm, gi),
                                reads=[bpn, bdeni], writes=[btm])
                        else:
                            p.emit("scalar", (lambda h_, idx_, pn_, di_, tm_, gi_: lambda e: e.mul(
                                out=tm_[:, 64 * h_:64 * h_ + 64],
                                in_=pn_[:, 65 * idx_:65 * idx_ + 64],
                                mul=di_[:, 4 * gi_ + idx_:4 * gi_ + idx_ + 1]))(h, idx, pn, den16i, attn_tm, gi),
                                reads=[bpn, bdeni], writes=[btm])

                def transposes(ch):
                    attn_tm, btm = attn_tmb[ch % 2]
                    ptbf = psT.ap().bitcast(BF16)
                    for c in range(DC):
                        p.emit("tensor", (lambda c_, tm_: lambda e: e.transpose(
                            ptbf[:, 128 * c_:128 * c_ + 128],
                            tm_[:, 128 * c_:128 * c_ + 128], self.id_bf[:]))(c, attn_tm),
                            reads=[btm, self.bid_bf], writes=[bpsT])
                    p.emit("scalar", (lambda ch_: lambda e: e.copy(
                        out=fap(attn_fm, 128 * ch_, [[TT, DC], [1, 128]]),
                        in_=ptbf[:, 0:1024]))(ch),
                        reads=[bpsT], writes=[bfm])

                items = [(ch, gi) for ch in range(NBLK) for gi in range(4)]
                for idx2, (ch, gi) in enumerate(items):
                    sc_group(ch, gi)
                    if idx2 >= 1:
                        num_group(*items[idx2 - 1])
                    if idx2 >= 2:
                        div_group(*items[idx2 - 2])
                    if gi == 2 and ch > 0:
                        transposes(ch - 1)
                num_group(*items[-1])
                div_group(*items[-2])
                div_group(*items[-1])
                transposes(NBLK - 1)
                for oc in range(DC):
                    pst, bpst = (ps_a, bps_a) if oc % 2 == 0 else (ps_b, bps_b)
                    for c in range(DC):
                        p.emit("tensor", (lambda oc_, c_, pst_: lambda e: e.matmul(
                            pst_[:, 0:TT], wo[:, c_ * D + 128 * oc_: c_ * D + 128 * oc_ + 128],
                            attn_fm[:, c_ * TT:(c_ + 1) * TT],
                            start=(c_ == 0), stop=(c_ == DC - 1)))(oc, c, pst),
                            reads=[bwo, bfm], writes=[bpst])
                    p.emit("vector", (lambda oc_, pst_: lambda e: e.scalar_tensor_tensor(
                        out=xo[:, oc_ * TT:(oc_ + 1) * TT], in0=pst_[:, 0:TT], scalar=1.0,
                        in1=y2t[:, oc_ * TT:(oc_ + 1) * TT], op0=ALU.mult, op1=ALU.add))(oc, pst),
                        reads=[bpst, by2], writes=[bxo])
                p.dma("gpsimd", sxo, s["xo"][ti], xo[:],
                      reads=[bxo], writes=[bxod])

    def pass_attn_b_ffn1(self):
        p, nc, i, s = self.p, self.nc, self.i, self.s
        import contextlib
        T, NT = self.T, self.NT
        self.barrier()
        with contextlib.ExitStack() as ctx:
            S = p.slot
            w1, bw1 = self.sb(ctx, "g1_w1", (128, DC * FFB), BF16)
            p.dma("gpsimd", S("w1b"), w1[:], i["w1b"].rearrange("c p f -> p c f"), writes=[bw1])
            xob = [self.sb(ctx, f"g1_xo{j}", (128, DC * TT)) for j in range(2)]
            xoslots = [S(f"g1_xo{j}") for j in range(2)]
            xbf, bxbf = self.sb(ctx, "g1_xbf", (128, DC * TT), BF16)
            xsq, bxsq = self.sb(ctx, "g1_xsq", (128, DC * TT), BF16)
            mu, bmu = self.sb(ctx, "g1_mu", (128, TT))
            var, bvar = self.sb(ctx, "g1_var", (128, TT))
            rstd, brstd = self.sb(ctx, "g1_rstd", (128, TT))
            h2b = [self.sb(ctx, f"g1_h2{j}", (128, DC * TT), BF16) for j in range(2)]
            hst, bhst = self.sb(ctx, "g1_hst", (128, 8 * TT), BF16)
            hstslots = [S(f"g1_hs{j}") for j in range(2)]
            ps4 = [self.ps(ctx, f"g1_ps{j}", (128, TT)) for j in range(4)]
            psS, bpsS = self.ps(ctx, "g1_psS", (128, 2 * TT))
            scr = (xbf, bxbf, xsq, bxsq, mu, bmu, var, bvar, rstd, brstd, psS, bpsS)
            bhd = p.buf()

            def prep(tj):
                xo_sb, bxo = xob[tj % 2]
                h2, bh2 = h2b[tj % 2]
                p.dma("sync", xoslots[tj % 2], xo_sb[:], s["xo"][tj], writes=[bxo])
                self.emit_ln(ctx, (lambda x2: lambda c: x2[:, c * TT:(c + 1) * TT])(xo_sb), bxo, TT,
                             gamma=self.v["g2"], beta=self.v["be2"],
                             out_f32=None, out_bf=(h2, bh2), scratch=scr)

            prep(0)
            for ti in range(NT):
                t0 = ti * TT
                h2, bh2 = h2b[ti % 2]
                if ti + 1 < NT:
                    prep(ti + 1)
                for fo in range(FFB // 128):
                    pst, bpst = ps4[fo % 4]
                    for c in range(DC):
                        p.emit("tensor", (lambda fo_, c_, pst_, h2_: lambda e: e.matmul(
                            pst_[:, 0:TT], w1[:, c_ * FFB + 128 * fo_: c_ * FFB + 128 * fo_ + 128],
                            h2_[:, c_ * TT:(c_ + 1) * TT],
                            start=(c_ == 0), stop=(c_ == DC - 1)))(fo, c, pst, h2),
                            reads=[bw1, bh2], writes=[bpst])
                    hoff = (fo % 4) * TT + ((fo // 4) % 2) * 4 * TT
                    p.emit("scalar", (lambda fo_, pst_, ho: lambda e: e.activation(
                        out=hst[:, ho:ho + TT], in_=pst_[:, 0:TT],
                        func=AF.Gelu, bias=self.v["b1b"][:, fo_:fo_ + 1]))(fo, pst, hoff),
                        reads=[bpst, self.bv["b1b"]], writes=[bhst])
                    if fo % 4 == 3:
                        grp = fo // 4
                        p.dma("gpsimd", hstslots[grp % 2],
                              s["h"][ti][:, (fo - 3) * TT:(fo + 1) * TT],
                              hst[:, (grp % 2) * 4 * TT:(grp % 2) * 4 * TT + 4 * TT],
                              reads=[bhst], writes=[bhd])

    def pass_attn_b_ffn2(self, dst):
        p, nc, i, s = self.p, self.nc, self.i, self.s
        import contextlib
        T, NT = self.T, self.NT
        self.barrier()
        with contextlib.ExitStack() as ctx:
            S = p.slot
            w2, bw2 = self.sb(ctx, "g2_w2", (128, (FFB // 128) * D), BF16)
            p.dma("gpsimd", S("w2b"), w2[:], i["w2b"].rearrange("c p f -> p c f"), writes=[bw2])
            hf, _bhf = self.sb(ctx, "g2_hf", (128, (FFB // 128) * TT), BF16)
            bhf = [p.buf(f"g2_hc{k}") for k in range(FFB // 128)]
            xo_sb, bxo = self.sb(ctx, "g2_xo", (128, DC * TT))
            x2_sb, bx2 = self.sb(ctx, "g2_x2", (128, DC * TT))
            xs, bxs = self.sb(ctx, "g2_xs", (128, DC * TT))
            xbf, bxbf = self.sb(ctx, "g2_xbf", (128, DC * TT), BF16)
            xsq, bxsq = self.sb(ctx, "g2_xsq", (128, DC * TT), BF16)
            mu, bmu = self.sb(ctx, "g2_mu", (128, TT))
            var, bvar = self.sb(ctx, "g2_var", (128, TT))
            rstd, brstd = self.sb(ctx, "g2_rstd", (128, TT))
            y3b, by3 = xbf, bxbf
            mxc, bmxc = self.sb(ctx, "g2_mx", (128, 1))
            ps4 = [self.ps(ctx, f"g2_ps{j}", (128, TT)) for j in range(4)]
            psS, bpsS = self.ps(ctx, "g2_psS", (128, 2 * TT))
            scr = (xbf, bxbf, xsq, bxsq, mu, bmu, var, bvar, rstd, brstd, psS, bpsS)
            sh, sxo, sx2 = S("g2_h"), S("g2_xo"), S("g2_x2")
            sx3, sy3 = S("g2_x3"), S("g2_y3")
            bx3d, by3d = p.buf(), p.buf()

            def epilogue(ti):
                p.dma("gpsimd", sx3, dst[ti], xs[:], reads=[bxs], writes=[bx3d])
                if self.n_sub >= 4:
                    self.emit_ln(ctx, lambda c: xs[:, c * TT:(c + 1) * TT], bxs, TT,
                                 out_f32=None, out_bf=(y3b, by3), scratch=scr)
                    p.dma("scalar", sy3, s["y3"][ti], y3b[:],
                          reads=[by3], writes=[by3d])
                    p.emit("vector", lambda e: e.reduce_max(out=mxc[:], in_=y3b[:], axis=AX.X),
                           reads=[by3], writes=[bmxc])
                    p.emit("vector", lambda e: e.tensor_max(
                        out=self.mloc[:, 2:3], in0=self.mloc[:, 2:3], in1=mxc[:]),
                        reads=[bmxc, self.bmloc], writes=[self.bmloc])

            for ti in range(NT):
                t0 = ti * TT
                p.dma("sync", sh, hf[:], s["h"][ti], writes=bhf)
                ftok = (sh, p.cnt[sh])
                for kc in range(FFB // 128):
                    bhf[kc].last_w = ftok
                p.dma("sync", sxo, xo_sb[:], s["xo"][ti], writes=[bxo])
                p.dma("sync", sx2, x2_sb[:], s["x2"][ti], writes=[bx2])
                if ti > 0:
                    epilogue(ti - 1)
                p.emit("gpsimd", lambda e: e.tensor_add(out=xs[:], in0=xo_sb[:], in1=x2_sb[:]),
                       reads=[bxo, bx2], writes=[bxs])
                for do in range(DC):
                    pst, bpst = ps4[do % 4]
                    for kc in range(FFB // 128):
                        p.emit("tensor", (lambda do_, kc_, pst_: lambda e: e.matmul(
                            pst_[:, 0:TT], w2[:, kc_ * D + 128 * do_: kc_ * D + 128 * do_ + 128],
                            hf[:, kc_ * TT:(kc_ + 1) * TT],
                            start=(kc_ == 0), stop=(kc_ == FFB // 128 - 1)))(do, kc, pst),
                            reads=[bw2, bhf[kc]], writes=[bpst])
                    p.emit("vector", (lambda do_, pst_: lambda e: e.scalar_tensor_tensor(
                        out=xs[:, do_ * TT:(do_ + 1) * TT], in0=pst_[:, 0:TT],
                        scalar=self.v["b2b"][:, do_:do_ + 1], in1=xs[:, do_ * TT:(do_ + 1) * TT],
                        op0=ALU.add, op1=ALU.add))(do, pst),
                        reads=[bpst, self.bv["b2b"], bxs], writes=[bxs])
            epilogue(NT - 1)



def build(T=T_CORE, n_sub=4, dbg=False):
    import contextlib
    kb = KB(T=T, n_sub=n_sub)
    kb.dbg = dbg
    p = kb.p
    with contextlib.ExitStack() as topctx:
        kb.setup(topctx)
        if n_sub >= 2:
            kb.pass_quantize_f()
        kb.pass_attn_a_front()
        kb.pass_attn_a_ffn(kb.o["xout"] if n_sub == 1 else kb.s["x1"])
        if n_sub >= 2:
            kb.barrier()
            kb.pass_sync_scales(0, 0)
            kb.pass_f_l1(kb.s["y1"], 0, 1)
            kb.barrier()
            kb.pass_sync_scales(1, 1)
            kb.pass_f_l2(kb.s["x1"], kb.o["xout"] if n_sub == 2 else kb.s["x2"], 1,
                         y_out=kb.s["y2"] if n_sub >= 3 else None)
            if n_sub >= 3:
                kb.pass_attn_b_front()
                kb.pass_attn_b_scan()
                kb.pass_attn_b_ffn1()
                kb.pass_attn_b_ffn2(kb.o["xout"] if n_sub == 3 else kb.s["x3"])
            if n_sub >= 4:
                kb.barrier()
                kb.pass_sync_scales(2, 2)
                kb.pass_f_l1(kb.s["y3"], 2, 3)
                kb.barrier()
                kb.pass_sync_scales(3, 3)
                kb.pass_f_l2(kb.s["x3"], kb.o["xout"], 3)
        p.realize()
    return kb


def kernel(**inputs):
    kb = build()
    in_maps = host_inputs(inputs)
    res = run_bass_kernel_spmd(kb.nc, in_maps, core_ids=list(range(NCORES)))
    return host_output(res.results)
